# version 02
# baseline (speedup 1.0000x reference)
import contextlib
import os
import numpy as np
import ml_dtypes

os.environ.setdefault("GAUGE_NTFF_IGNORE_DMA", "1")
import concourse.bass as bass
import concourse.mybir as mybir
from concourse.bass_utils import run_bass_kernel_spmd

F32 = mybir.dt.float32
BF16 = mybir.dt.bfloat16
AF = mybir.ActivationFunctionType
ALU = mybir.AluOpType
BF = ml_dtypes.bfloat16

B_, T_, D_, HEADS, CF, HID = 4, 8192, 1024, 16, 4, 4096
ROWS = T_ // 2
NL = ROWS // 128
NQG = ROWS // 512
N1 = T_ // 128
NEG = -1e30
GROUPS = [[0, 1], [2, 3], [4, 5], [6, 7]]

PHASE_LIMIT = 99
DEBUG_DUMP = None



def _perm():
    p = np.empty(N1, np.int64)
    p[:N1 // 2] = 2 * np.arange(N1 // 2)
    p[N1 // 2:] = 2 * np.arange(N1 // 2) + 1
    return p


def _stepA_const():
    perm = _perm()
    k1 = np.arange(N1)
    C = np.cos(2 * np.pi * perm[:, None] * k1[None, :] / N1)
    S = np.sin(2 * np.pi * perm[:, None] * k1[None, :] / N1)
    Ar = np.concatenate([C, -S], 0)
    Ai = np.concatenate([-S, -C], 0)
    return np.concatenate([Ar, Ai], 1).astype(BF)


def _stepC_const(r):
    t2 = np.arange(128)
    out = np.zeros((128, 64 * 128), np.float32)
    ll = np.arange(32)
    bb = np.arange(2)
    k2 = (4 * ll[:, None] + 2 * r + bb[None, :]).reshape(-1)
    for k1 in range(N1):
        th = 2 * np.pi * t2[:, None] * (k1 / T_ + k2[None, :] / 128.0)
        out[:, 128 * k1:128 * k1 + 64] = np.cos(th)
        out[:, 128 * k1 + 64:128 * k1 + 128] = np.sin(th)
    return out.astype(BF)


def _maskA(r):
    null = np.full((128, 512), NEG, np.float32)
    null[0, :] = 0.0
    bands = []
    for half in range(2):
        m = np.zeros((128, 512), np.float32)
        for a in range(4):
            ga = 2 * a + half
            for s in range(4):
                gq = 2 * s + r
                blk = m[32 * a:32 * a + 32, 128 * s:128 * s + 128]
                if ga < gq:
                    blk[:] = 0.0
                elif ga > gq:
                    blk[:] = NEG
                else:
                    c = np.arange(32)
                    i = np.arange(128)
                    blk[:] = np.where((4 * c[:, None] + 3) <= i[None, :], 0.0, NEG)
        bands.append(m)
    return np.concatenate([null] + bands, 1).astype(BF)


def _maskB(r):
    m = np.zeros((128, 8 * 512), np.float32)
    for half in range(2):
        for a in range(4):
            ga = 2 * a + half
            blk = m[:, (half * 4 + a) * 512:(half * 4 + a + 1) * 512]
            for s in range(4):
                gq = 2 * s + r
                sb = blk[:, 128 * s:128 * s + 128]
                if ga < gq:
                    sb[:] = 0.0
                elif ga > gq:
                    sb[:] = NEG
                else:
                    kk = np.arange(128)
                    i = np.arange(128)
                    sb[:] = np.where(kk[:, None] <= i[None, :], 0.0, NEG)
    return m.astype(BF)



class MK:
    ENGS = ("sync", "gpsimd", "scalar", "vector", "tensor")
    QENG = {"sq": "sync", "aq": "scalar", "gq": "gpsimd"}

    def __init__(self, phase_limit=99, debug_dump=None):
        self.phase_limit = phase_limit
        self.debug_dump = debug_dump
        self.nc = bass.Bass("TRN2", target_bir_lowering=False, debug=False)
        self.prog = {e: [] for e in self.ENGS}
        self.c = {}
        self.lastwait = {}
        self.bankfree = {}
        self._declare_dram()

    def emit(self, eng, fn):
        self.prog[eng].append(fn)

    def bump(self, sem, n):
        self.c[sem] = self.c.get(sem, 0) + n
        return self.c[sem]

    def dma(self, q, out, in_, **kw):
        eng = self.QENG[q]
        tgt = self.bump(q, 16)
        def f(E, out=out, in_=in_, kw=kw, eng=eng, q=q):
            E[eng].dma_start(out=out, in_=in_, **kw).then_inc(self.sem[q], 16)
        self.emit(eng, f)
        return tgt

    def wait(self, eng, semname, val=None):
        if val is None:
            val = self.c.get(semname, 0)
        if val <= 0:
            return
        key = (eng, semname)
        if self.lastwait.get(key, -1) >= val:
            return
        self.lastwait[key] = val
        def f(E, semname=semname, val=val, eng=eng):
            E[eng].wait_ge(self.sem[semname], val)
        self.emit(eng, f)

    def op(self, eng, semname, fn, n=1):
        tgt = self.bump(semname, n)
        def f(E, fn=fn, semname=semname, n=n):
            fn(E).then_inc(self.sem[semname], n)
        self.emit(eng, f)
        return tgt

    def opq(self, eng, fn):
        self.emit(eng, lambda E, fn=fn: fn(E))

    def barrier(self):
        for e in self.ENGS:
            for s in ("sq", "aq", "gq", "pe", "dv", "ac"):
                self.wait(e, s)

    def use_bank(self, bank):
        ent = self.bankfree.get(bank)
        if ent:
            self.wait("tensor", ent[0], ent[1])

    def free_bank(self, bank, sem):
        self.bankfree[bank] = (sem, self.c.get(sem, 0))

    def collective(self, sem, src, dst):
        self.bump(sem, 1)
        def f(E, sem=sem, src=src, dst=dst):
            E["gpsimd"].collective_compute(
                "AllReduce", ALU.add, replica_groups=GROUPS,
                ins=[src.opt()], outs=[dst.opt()]).then_inc(self.sem[sem])
        self.emit("gpsimd", f)

    def _declare_dram(self):
        nc = self.nc
        def di(name, shape, dt=F32):
            setattr(self, name, nc.dram_tensor(name, list(shape), dt,
                                               kind="ExternalInput").ap())
        def ds(name, shape, dt):
            setattr(self, name, nc.dram_tensor(name, list(shape), dt,
                                               kind="Internal").ap())
        di("x0f", (1024, 4096))
        di("wqkv", (1024, 3072))
        di("convw_in", (128, 2048))
        di("convb_in", (128, 8))
        di("nullk_in", (128, 8))
        di("nullv_in", (128, 8))
        di("wout", (1024, 1024))
        di("bout_in", (128, 8))
        di("wqT", (1024, 1024)); di("wkT", (1024, 1024))
        di("wvT", (1024, 1024)); di("woT", (1024, 1024))
        di("w1t", (1024, 4096)); di("w2t", (1024, 4096))
        di("h1t", (4096, 4096)); di("h2t", (4096, 4096))
        di("w3t", (4096, 1024))
        di("s1_in", (128, 32)); di("s2_in", (128, 32))
        di("sh1_in", (128, 32)); di("sh2_in", (128, 32))
        di("s3_in", (128, 8))
        di("mvec_in", (128, 8))
        di("cdsd", (1024, 2048), BF16)
        di("stA_in", (128, 128), BF16)
        di("stC_in", (128, 8192), BF16)
        di("am_in", (128, 1536), BF16)
        di("bm_in", (128, 4096), BF16)
        di("eyeb_in", (128, 128), BF16)
        di("eyef_in", (128, 128))
        di("ones_in", (128, 128))
        self.outT = nc.dram_tensor("out", [1024, 4096], F32,
                                   kind="ExternalOutput").ap()
        if self.debug_dump:
            self.dbg = nc.dram_tensor("dbg", [16384, 4096], F32,
                                      kind="ExternalOutput").ap()
        ds("xres", (1024, 4096), F32)
        ds("qscr", (1024, 4096), BF16)
        ds("wbf_qkv", (1024, 3072), BF16)
        ds("wbf_out", (1024, 1024), BF16)
        ds("wbf_b", (1024, 4096), BF16)
        ds("w1bf", (1024, 4096), BF16)
        ds("w2bf", (1024, 4096), BF16)
        ds("h1bf", (4096, 4096), BF16)
        ds("h2bf", (4096, 4096), BF16)
        ds("w3bf", (4096, 1024), BF16)
        ds("ag1k_i", (2048, 1024), BF16); ds("ag1k_o", (2048, 1024), BF16)
        ds("ag1v_i", (16384, 130), BF16); ds("ag1v_o", (16384, 130), BF16)
        ds("ag2_i", (16384, 1024), BF16); ds("ag2_o", (16384, 1024), BF16)
        ds("ag3k_i", (2048, 4096), BF16); ds("ag3k_o", (2048, 4096), BF16)
        ds("ag3v_i", (65536, 130), BF16); ds("ag3v_o", (65536, 130), BF16)
        ds("y1scr", (16384, 1024), BF16)

    def _alloc(self, stack):
        nc = self.nc
        def sb(name, shape, dt):
            setattr(self, name, stack.enter_context(
                nc.sbuf_tensor(name, list(shape), dt)))
        sb("cA", (128, 16384), BF16)
        sb("cB", (128, 16384), BF16)
        sb("cC", (128, 8192), BF16)
        sb("cD", (128, 8192), BF16)
        sb("xk", (128, 4096), F32)
        sb("yb", (128, 4096), BF16)
        sb("stC_sb", (128, 8192), BF16)
        sb("am_sb", (128, 1536), BF16)
        sb("bm_sb", (128, 4096), BF16)
        sb("nKT", (128, 1024), BF16)
        sb("stA_sb", (128, 128), BF16)
        sb("eyeb_sb", (128, 128), BF16)
        sb("eyef_sb", (128, 128), F32)
        sb("ones_f", (128, 128), F32)
        sb("scal", (128, 160), F32)
        sb("lnf", (128, 4096), F32)
        sb("pb", (128, 4096), BF16)
        self.psb = {}
        for nm in ("p0", "p1", "p2", "p3", "p4", "p5"):
            self.psb[nm] = stack.enter_context(
                nc.psum_tensor(nm, [128, 512], F32))
        self.psb["ta"] = stack.enter_context(
            nc.psum_tensor("ta", [128, 1024], BF16))
        self.psb["tb"] = stack.enter_context(
            nc.psum_tensor("tb", [128, 1024], BF16))
        self.sem = {}
        for n in ("sq", "aq", "gq", "pe", "dv", "ac",
                  "cc1", "cc2", "cc3", "cc4", "cc5"):
            self.sem[n] = stack.enter_context(nc.semaphore(n))

    S1, S2, SH1, SH2 = 0, 32, 64, 96
    S3, CONVB, BOUT, MV = 128, 136, 144, 152

    def m0(self):
        return self.scal[:, self.MV:self.MV + 1]

    def m1(self):
        return self.scal[:, self.MV + 1:self.MV + 2]

    def kconst(self, idx, p0=0, n=128):
        return self.scal[p0:p0 + n, self.MV + idx:self.MV + idx + 1]

    def ph_init(self):
        for dst, src in [(self.stC_sb, self.stC_in), (self.am_sb, self.am_in),
                         (self.bm_sb, self.bm_in), (self.stA_sb, self.stA_in),
                         (self.eyeb_sb, self.eyeb_in), (self.eyef_sb, self.eyef_in),
                         (self.ones_f, self.ones_in)]:
            self.dma("sq", dst[:, :], src[:, :])
        self.dma("sq", self.scal[:, 0:32], self.s1_in[:, :])
        self.dma("sq", self.scal[:, 32:64], self.s2_in[:, :])
        self.dma("sq", self.scal[:, 64:96], self.sh1_in[:, :])
        self.dma("sq", self.scal[:, 96:128], self.sh2_in[:, :])
        self.dma("sq", self.scal[:, 128:136], self.s3_in[:, :])
        self.dma("sq", self.scal[:, 136:144], self.convb_in[:, :])
        self.dma("sq", self.scal[:, 144:152], self.bout_in[:, :])
        self.dma("sq", self.scal[:, 152:160], self.mvec_in[:, :])
        self.consts_ld = self.c["sq"]

        self.dma("gq", self.wbf_qkv[:, :], self.wqkv[:, :])
        self.dma("gq", self.pb[:, 0:2048], self.convw_in[:, :])
        self.dma("gq", self.lnf[:, 0:8], self.nullk_in[:, :])
        self.dma("gq", self.lnf[:, 8:16], self.nullv_in[:, :])
        self.cast_a1 = self.c["gq"]
        self.dma("gq", self.wbf_out[:, :], self.wout[:, :])
        self.cast_wout = self.c["gq"]
        self.dma("gq", self.wbf_b[:, 0:1024], self.wqT[:, :])
        self.dma("gq", self.wbf_b[:, 1024:2048], self.wkT[:, :])
        self.dma("gq", self.wbf_b[:, 2048:3072], self.wvT[:, :])
        self.dma("gq", self.wbf_b[:, 3072:4096], self.woT[:, :])
        self.cast_b = self.c["gq"]
        self.dma("gq", self.w1bf[:, :], self.w1t[:, :])
        self.dma("gq", self.w2bf[:, :], self.w2t[:, :])
        for i in range(4):
            self.dma("gq", self.h1bf[1024 * i:1024 * (i + 1), :],
                     self.h1t[1024 * i:1024 * (i + 1), :])
        for i in range(4):
            self.dma("gq", self.h2bf[1024 * i:1024 * (i + 1), :],
                     self.h2t[1024 * i:1024 * (i + 1), :])
        self.dma("gq", self.w3bf[:, :], self.w3t[:, :])
        self.cast_ffn = self.c["gq"]

        self.wait("vector", "gq", self.cast_a1)
        self.wait("vector", "sq", self.consts_ld)
        self.op("vector", "dv", lambda E: E["vector"].memset(self.nKT[:, :], 0.0))
        for hp in range(8):
            self.op("vector", "dv", lambda E, hp=hp: E["vector"].tensor_copy(
                self.nKT[:, 128 * hp:128 * hp + 1], self.lnf[:, hp:hp + 1]))
        self.wait("tensor", "sq", self.consts_ld)
        self.wait("tensor", "gq", self.cast_a1)
        self.use_bank("p5")
        tnv = self.op("tensor", "pe", lambda E: E["tensor"].transpose(
            self.psb["p5"][0:8, 0:128], self.lnf[:, 8:16], self.eyef_sb[:, :]))
        self.wait("vector", "pe", tnv)
        self.op("vector", "dv", lambda E: E["vector"].tensor_copy(
            self.lnf[0:8, 16:144], self.psb["p5"][0:8, 0:128]))
        self.free_bank("p5", "dv")

    def build_nvt(self, dst, c0):
        self.op("vector", "dv", lambda E, dst=dst, c0=c0: E["vector"].memset(
            dst[:, c0:c0 + 1040], 0.0))
        for hp in range(8):
            b = c0 + 130 * hp
            self.op("vector", "dv", lambda E, dst=dst, b=b: E["vector"].memset(
                dst[:, b + 64:b + 65], 1.0))
            done = self.op("vector", "dv", lambda E, dst=dst, b=b: E["vector"].memset(
                dst[:, b + 129:b + 130], 1.0))
        self.wait("sync", "dv", done)

    def _nvt_rows(self, dst, c0):
        for hp in range(8):
            b = c0 + 130 * hp
            self.dma("gq", dst[0:1, b + 0:b + 64], self.lnf[hp:hp + 1, 16:80])
            self.dma("gq", dst[0:1, b + 65:b + 129], self.lnf[hp:hp + 1, 80:144])

    def ln_tg(self, src, tg):
        t0 = 512 * tg
        self.wait("sync", "dv")
        ld = 0
        for cc in range(8):
            ld = self.dma("sq", self.xk[:, 512 * cc:512 * (cc + 1)],
                          src[128 * cc:128 * (cc + 1), t0:t0 + 512])
        self.wait("tensor", "sq", ld)
        self.wait("scalar", "sq", ld)
        self.use_bank("p0"); self.use_bank("p1")
        for cc in range(8):
            xc = self.xk[:, 512 * cc:512 * (cc + 1)]
            self.opq("tensor", lambda E, xc=xc, cc=cc: E["tensor"].matmul(
                self.psb["p0"][0:1, :], self.ones_f[:, 0:1], xc,
                start=(cc == 0), stop=(cc == 7)))
            sq = self.lnf[:, 2048 + 512 * (cc % 2):2560 + 512 * (cc % 2)]
            if cc >= 2:
                self.wait("scalar", "pe")
            t = self.op("scalar", "ac", lambda E, xc=xc, sq=sq: E["scalar"].activation(
                out=sq, in_=xc, func=AF.Square))
            self.wait("tensor", "ac", t)
            self.op("tensor", "pe", lambda E, sq=sq, cc=cc: E["tensor"].matmul(
                self.psb["p1"][0:1, :], self.ones_f[:, 0:1], sq,
                start=(cc == 0), stop=(cc == 7)))
        stats_done = self.c["pe"]
        self.wait("vector", "pe", stats_done)
        mrow = self.lnf[0:1, 3072:3584]
        arow = self.lnf[0:1, 3584:4096]
        self.op("vector", "dv", lambda E, mrow=mrow: E["vector"].tensor_scalar(
            mrow, self.psb["p0"][0:1, :], 1.0 / 1024, None, ALU.mult))
        self.op("vector", "dv", lambda E, mrow=mrow, arow=arow: E["vector"].tensor_tensor(
            arow, mrow, mrow, ALU.mult))
        self.op("vector", "dv", lambda E, arow=arow: E["vector"].scalar_tensor_tensor(
            arow, self.psb["p1"][0:1, :], 1.0 / 1024, arow, ALU.mult, ALU.subtract))
        self.free_bank("p0", "dv"); self.free_bank("p1", "dv")
        self.wait("scalar", "dv")
        self.op("scalar", "ac", lambda E, arow=arow: E["scalar"].activation(
            out=arow, in_=arow, func=AF.Ln, bias=self.kconst(2, 0, 1)))
        self.op("scalar", "ac", lambda E, arow=arow: E["scalar"].activation(
            out=arow, in_=arow, func=AF.Exp, scale=self.kconst(5, 0, 1)))
        self.wait("vector", "ac")
        self.op("vector", "dv", lambda E, mrow=mrow, arow=arow: E["vector"].tensor_tensor(
            mrow, mrow, arow, ALU.mult))
        rows_done = self.c["dv"]
        self.wait("tensor", "dv", rows_done)
        self.use_bank("p2"); self.use_bank("p3")
        self.opq("tensor", lambda E, arow=arow: E["tensor"].matmul(
            self.psb["p2"][:, :], self.ones_f[0:1, :], arow, start=True, stop=True))
        self.op("tensor", "pe", lambda E, mrow=mrow: E["tensor"].matmul(
            self.psb["p3"][:, :], self.ones_f[0:1, :], mrow, start=True, stop=True))
        bc_done = self.c["pe"]
        self.wait("vector", "pe", bc_done)
        for cc in range(8):
            xc = self.xk[:, 512 * cc:512 * (cc + 1)]
            t1 = self.lnf[:, 2048 + 512 * (cc % 2):2560 + 512 * (cc % 2)]
            yc = self.yb[:, 512 * cc:512 * (cc + 1)]
            self.op("vector", "dv", lambda E, xc=xc, t1=t1: E["vector"].tensor_tensor(
                t1, xc, self.psb["p2"][:, :], ALU.mult))
            self.op("vector", "dv", lambda E, t1=t1, yc=yc: E["vector"].tensor_tensor(
                yc, t1, self.psb["p3"][:, :], ALU.subtract))
        self.free_bank("p2", "dv"); self.free_bank("p3", "dv")
        return self.c["dv"]

    def ph_a1(self):
        self.wait("sync", "gq", self.cast_a1)
        wslot = [self.cC[:, 0:1024], self.cC[:, 1024:2048]]
        wsrc = self.wbf_qkv.rearrange("(c p) o -> p c o", p=128)
        self.wait("vector", "gq", self.cast_a1)
        self.wait("vector", "sq", self.consts_ld)
        for sl in range(2):
            for ver in range(2):
                vs = 2048 + 520 * (2 * sl + ver)
                mv = self.m0() if ver == 0 else self.m1()
                self.op("vector", "dv", lambda E, vs=vs, mv=mv: E["vector"].tensor_scalar(
                    self.pb[:, vs + 64:vs + 65], self.ones_f[:, 0:1], mv, None, ALU.mult))
                self.op("vector", "dv", lambda E, vs=vs, mv=mv: E["vector"].tensor_scalar(
                    self.pb[:, vs + 129:vs + 130], self.ones_f[:, 0:1], mv, None, ALU.mult))
        for tg in range(NQG):
            y_done = self.ln_tg(self.x0f, tg)
            self.wait("tensor", "dv", y_done)
            pe_iter = {}
            self.wait("sync", "pe")
            ld = {0: self.dma("sq", wslot[0], wsrc[:, :, 0:128])}
            for i in range(24):
                if i + 1 < 24:
                    if i >= 1:
                        self.wait("sync", "pe", pe_iter.get(i - 1, 0))
                    ld[i + 1] = self.dma("sq", wslot[(i + 1) % 2],
                                         wsrc[:, :, 128 * (i + 1):128 * (i + 2)])
                self.wait("tensor", "sq", ld[i])
                bank = "p0" if i % 2 == 0 else "p1"
                self.use_bank(bank)
                ps = self.psb[bank]
                w = wslot[i % 2]
                for cc in range(8):
                    ff = (cc == 0); lst = (cc == 7)
                    if lst:
                        pe_iter[i] = self.op("tensor", "pe", lambda E, w=w, cc=cc, ps=ps: E["tensor"].matmul(
                            ps[:, :], w[:, 128 * cc:128 * cc + 128],
                            self.yb[:, 512 * cc:512 * cc + 512], start=False, stop=True))
                    else:
                        self.opq("tensor", lambda E, w=w, cc=cc, ps=ps, ff=ff: E["tensor"].matmul(
                            ps[:, :], w[:, 128 * cc:128 * cc + 128],
                            self.yb[:, 512 * cc:512 * cc + 512], start=ff, stop=False))
                self.wait("vector", "pe", pe_iter[i])
                if i < 8:
                    stg = self.cD[:, 512 * (i % 2):512 * (i % 2) + 512]
                    self.wait("vector", "gq")
                    ev = self.op("vector", "dv", lambda E, ps=ps, stg=stg: E["vector"].tensor_copy(stg, ps[:, :]))
                    self.free_bank(bank, "dv")
                    self.wait("gpsimd", "dv", ev)
                    self.dma("gq", self.qscr[128 * i:128 * i + 128, 512 * tg:512 * tg + 512], stg)
                elif i < 16:
                    kc = i - 8
                    self.op("vector", "dv", lambda E, ps=ps, kc=kc: E["vector"].tensor_copy(
                        self.cB[:, 512 * kc:512 * kc + 512], ps[:, :]))
                    self.free_bank(bank, "dv")
                else:
                    vc = i - 16
                    self.op("vector", "dv", lambda E, ps=ps, vc=vc: E["vector"].tensor_copy(
                        self.cA[:, 512 * vc:512 * vc + 512], ps[:, :]))
                    self.free_bank(bank, "dv")
            kv_done = self.c["dv"]
            self.wait("tensor", "dv", kv_done)
            for src_arena, is_v in ((self.cB, False), (self.cA, True)):
                for hp in range(8):
                    for par in range(2):
                        bank = "p2" if par == 0 else "p3"
                        self.use_bank(bank)
                        ps = self.psb[bank]
                        p0 = 64 * par
                        for tau in range(4):
                            wcol = 64 * (4 * hp + tau)
                            lhs = self.pb[p0:p0 + 64, wcol:wcol + 64]
                            rhs = src_arena[p0:p0 + 64, 512 * hp + tau:512 * hp + 512:4]
                            ff = (tau == 0); lst = (tau == 3)
                            if lst:
                                self.op("tensor", "pe", lambda E, lhs=lhs, rhs=rhs, ps=ps, p0=p0: E["tensor"].matmul(
                                    ps[0:64, 0:128], lhs, rhs, start=False, stop=True,
                                    tile_position=(p0, 0)))
                            else:
                                self.opq("tensor", lambda E, lhs=lhs, rhs=rhs, ps=ps, ff=ff, p0=p0: E["tensor"].matmul(
                                    ps[0:64, 0:128], lhs, rhs, start=ff, stop=False,
                                    tile_position=(p0, 0)))
                        mm = self.c["pe"]
                        self.wait("vector", "pe", mm)
                        bias = self.scal[p0:p0 + 64, self.CONVB + hp:self.CONVB + hp + 1]
                        if not is_v:
                            self.wait("vector", "gq")
                            for ver, mv, colb in ((0, self.m0(), 1024), (1, self.m1(), 3072)):
                                dst = self.cD[p0:p0 + 64, colb + 128 * hp:colb + 128 * hp + 128]
                                mvs = mv[p0:p0 + 64, :] if p0 else mv[0:64, :]
                                self.op("vector", "dv", lambda E, ps=ps, dst=dst, bias=bias, mvs=mvs: E["vector"].tensor_scalar(
                                    dst, ps[0:64, 0:128], bias, mvs, ALU.add, ALU.mult))
                        else:
                            dst = self.cD[p0:p0 + 64, 2048 + 128 * hp:2048 + 128 * hp + 128]
                            self.op("vector", "dv", lambda E, ps=ps, dst=dst, bias=bias: E["vector"].tensor_scalar(
                                dst, ps[0:64, 0:128], bias, None, ALU.add))
                        self.free_bank(bank, "dv")
            conv_done = self.c["dv"]
            self.wait("gpsimd", "dv", conv_done)
            for hp in range(8):
                self.dma("gq", self.ag1k_i[128 * hp:128 * hp + 128, 128 * tg:128 * tg + 128],
                         self.cD[:, 1024 + 128 * hp:1024 + 128 * hp + 128])
                self.dma("gq", self.ag1k_i[1024 + 128 * hp:1024 + 128 * hp + 128, 128 * tg:128 * tg + 128],
                         self.cD[:, 3072 + 128 * hp:3072 + 128 * hp + 128])
            for hp in range(8):
                tb = "ta" if hp % 2 == 0 else "tb"
                self.use_bank(tb)
                pst = self.psb[tb]
                self.wait("tensor", "dv", conv_done)
                td = self.op("tensor", "pe", lambda E, hp=hp, pst=pst: E["tensor"].transpose(
                    pst[:, 0:128], self.cD[:, 2048 + 128 * hp:2048 + 128 * hp + 128],
                    self.eyeb_sb[:, :]))
                self.wait("vector", "pe", td)
                sl = hp % 2
                self.wait("vector", "gq")
                dvs = 0
                for ver, mv in ((0, self.m0()), (1, self.m1())):
                    vs = 2048 + 520 * (2 * sl + ver)
                    self.op("vector", "dv", lambda E, pst=pst, vs=vs, mv=mv: E["vector"].tensor_scalar(
                        self.pb[:, vs + 0:vs + 64], pst[:, 0:64], mv, None, ALU.mult))
                    dvs = self.op("vector", "dv", lambda E, pst=pst, vs=vs, mv=mv: E["vector"].tensor_scalar(
                        self.pb[:, vs + 65:vs + 129], pst[:, 64:128], mv, None, ALU.mult))
                self.free_bank(tb, "dv")
                self.wait("gpsimd", "dv", dvs)
                for ver in range(2):
                    vs = 2048 + 520 * (2 * sl + ver)
                    rbase = 8192 * ver + 1024 * hp + 128 * tg
                    self.dma("gq", self.ag1v_i[rbase:rbase + 128, :],
                             self.pb[:, vs:vs + 130])
        self.wait("gpsimd", "gq")
        self.collective("cc1", self.ag1k_i, self.ag1k_o)
        self.collective("cc2", self.ag1v_i, self.ag1v_o)

    def attn_qg(self, qg, kind):
        j = qg
        scale = self.kconst(3) if kind == "a" else self.kconst(4)
        nch_half = (j + 1) if kind == "a" else 4 * (j + 1)
        ext = 128 * nch_half
        for hp in range(8):
            kE = self.cC[:, 0:ext]
            kO = self.cC[:, 4096:4096 + ext]
            self.wait("sync", "pe")
            ksrc = self.ag1k_o if kind == "a" else self.ag3k_o
            self.dma("sq", kE, ksrc[128 * hp:128 * hp + 128, 0:ext])
            l2 = self.dma("sq", kO, ksrc[1024 + 128 * hp:1024 + 128 * hp + 128, 0:ext])
            vsrc = self.ag1v_o if kind == "a" else self.ag3v_o
            vre = vsrc.rearrange("(x p) c -> p x c", p=128)
            blk = 8 if kind == "a" else 32
            base_e = blk * hp
            base_o = (64 if kind == "a" else 256) + blk * hp
            vE_c0, vO_c0 = 0, nch_half * 130
            self.wait("scalar", "pe")
            self.dma("aq", self.cA[:, vE_c0:vE_c0 + nch_half * 130],
                     vre[:, base_e:base_e + nch_half, :])
            l4 = self.dma("aq", self.cA[:, vO_c0:vO_c0 + nch_half * 130],
                          vre[:, base_o:base_o + nch_half, :])
            self.wait("tensor", "sq", l2)
            self.wait("tensor", "aq", l4)
            chunks = []
            if kind == "a":
                chunks.append(("null", 0, None, 0))
            for half in range(2):
                kb = kE if half == 0 else kO
                vc0 = vE_c0 if half == 0 else vO_c0
                for ci in range(nch_half):
                    if kind == "a":
                        mask = (1 + half) if ci == nch_half - 1 else None
                    else:
                        mask = (half * 4 + (ci - (nch_half - 4))) if ci >= nch_half - 4 else None
                    chunks.append((kb, 128 * ci, vc0 + 130 * ci, mask))
            qcolb = 12288 + 512 * hp
            self.use_bank("p4"); self.use_bank("p5")
            oh, ohp = self.psb["p4"], self.psb["p5"]
            nch = len(chunks)
            for ci, (kb, kcol, vcol, mask) in enumerate(chunks):
                sb_name = ("p0", "p1") if ci % 2 == 0 else ("p2", "p3")
                self.use_bank(sb_name[0]); self.use_bank(sb_name[1])
                s_h, s_hp = self.psb[sb_name[0]], self.psb[sb_name[1]]
                if kb == "null":
                    klhs0 = self.nKT[0:64, 128 * hp:128 * hp + 128]
                    klhs1 = self.nKT[64:128, 128 * hp:128 * hp + 128]
                else:
                    klhs0 = kb[0:64, kcol:kcol + 128]
                    klhs1 = kb[64:128, kcol:kcol + 128]
                self.opq("tensor", lambda E, klhs0=klhs0, s_h=s_h, qcolb=qcolb: E["tensor"].matmul(
                    s_h[:, :], klhs0, self.cB[0:64, qcolb:qcolb + 512],
                    start=True, stop=True, tile_position=(0, 0)))
                mm = self.op("tensor", "pe", lambda E, klhs1=klhs1, s_hp=s_hp, qcolb=qcolb: E["tensor"].matmul(
                    s_hp[:, :], klhs1, self.cB[64:128, qcolb:qcolb + 512],
                    start=True, stop=True, tile_position=(64, 0)))
                pcol = 1024 * (ci % 2)
                self.wait("scalar", "pe", mm)
                msk = self.am_sb if kind == "a" else self.bm_sb
                for which, sps in ((0, s_h), (1, s_hp)):
                    pt = self.pb[:, pcol + 512 * which:pcol + 512 * which + 512]
                    if mask is not None:
                        sf = self.lnf[:, 512 * which:512 * which + 512]
                        self.wait("vector", "pe", mm)
                        self.wait("vector", "ac")
                        dv = self.op("vector", "dv", lambda E, sps=sps, sf=sf, mask=mask, msk=msk: E["vector"].tensor_tensor(
                            sf, sps[:, :], msk[:, 512 * mask:512 * mask + 512], ALU.add))
                        self.wait("scalar", "dv", dv)
                        self.op("scalar", "ac", lambda E, sf=sf, pt=pt, scale=scale: E["scalar"].activation(
                            out=pt, in_=sf, func=AF.Exp, scale=scale))
                    else:
                        self.op("scalar", "ac", lambda E, sps=sps, pt=pt, scale=scale: E["scalar"].activation(
                            out=pt, in_=sps[:, :], func=AF.Exp, scale=scale))
                expd = self.c["ac"]
                self.free_bank(sb_name[0], "ac"); self.free_bank(sb_name[1], "ac")
                if kb == "null":
                    nb = 16384 - 1040 + 130 * hp
                    vl0 = self.cA[:, nb:nb + 65]
                    vl1 = self.cA[:, nb + 65:nb + 130]
                else:
                    vl0 = self.cA[:, vcol:vcol + 65]
                    vl1 = self.cA[:, vcol + 65:vcol + 130]
                self.wait("tensor", "ac", expd)
                first = ci == 0
                last = ci == nch - 1
                self.opq("tensor", lambda E, vl0=vl0, oh=oh, pcol=pcol, first=first, last=last: E["tensor"].matmul(
                    oh[0:65, :], vl0, self.pb[:, pcol:pcol + 512], start=first, stop=last))
                self.op("tensor", "pe", lambda E, vl1=vl1, ohp=ohp, pcol=pcol, first=first, last=last: E["tensor"].matmul(
                    ohp[0:65, :], vl1, self.pb[:, pcol + 512:pcol + 1024], start=first, stop=last))
            av_done = self.c["pe"]
            self.wait("vector", "pe", av_done)
            rlh = self.lnf[64:65, 2048:2560]
            rlhp = self.lnf[64:65, 2560:3072]
            self.op("vector", "dv", lambda E, oh=oh, rlh=rlh: E["vector"].reciprocal(rlh, oh[64:65, :]))
            rl_done = self.op("vector", "dv", lambda E, ohp=ohp, rlhp=rlhp: E["vector"].reciprocal(rlhp, ohp[64:65, :]))
            self.wait("tensor", "dv", rl_done)
            self.use_bank("p0")
            rb = self.psb["p0"]
            self.opq("tensor", lambda E, rb=rb, rlh=rlh: E["tensor"].matmul(
                rb[0:64, :], self.ones_f[64:65, 0:64], rlh,
                start=True, stop=True, tile_position=(64, 0)))
            bc = self.op("tensor", "pe", lambda E, rb=rb, rlhp=rlhp: E["tensor"].matmul(
                rb[64:128, :], self.ones_f[64:65, 0:64], rlhp,
                start=True, stop=True, tile_position=(64, 64)))
            self.wait("scalar", "pe", av_done)
            ob_h = self.lnf[0:64, 1024:1536]
            ob_t = self.lnf[0:64, 1536:2048]
            self.wait("scalar", "dv")
            self.wait("scalar", "sq")
            self.op("scalar", "ac", lambda E, oh=oh, ob_h=ob_h: E["scalar"].copy(ob_h, oh[0:64, :]))
            a2 = self.op("scalar", "ac", lambda E, ohp=ohp, ob_t=ob_t: E["scalar"].copy(ob_t, ohp[0:64, :]))
            self.free_bank("p4", "ac"); self.free_bank("p5", "ac")
            self.wait("sync", "ac", a2)
            mv = self.dma("sq", self.lnf[64:128, 1024:1536], ob_t)
            self.wait("vector", "sq", mv)
            self.wait("vector", "pe", bc)
            self.op("vector", "dv", lambda E, rb=rb, hp=hp: E["vector"].tensor_tensor(
                self.cB[:, 8192 + 512 * hp:8192 + 512 * hp + 512],
                self.lnf[:, 1024:1536], rb[:, :], ALU.mult))
            self.free_bank("p0", "dv")
        if self.debug_dump == "AOB_" + kind.upper():
            self.wait("gpsimd", "dv")
            self.dma("gq", self.dbg[128 * qg:128 * qg + 128, 0:4096],
                     self.cB[:, 8192:12288])
        return self.c["dv"]

    def ph_a2(self):
        self.wait("sync", "cc1", self.c["cc1"])
        self.wait("scalar", "cc2", self.c["cc2"])
        self.wait("sync", "gq", self.cast_wout)
        self.build_nvt(self.cA, 16384 - 1040)
        self.wait("gpsimd", "dv")
        self._nvt_rows(self.cA, 16384 - 1040)
        self.wait("tensor", "gq")
        self.wait("vector", "gq")
        wslot = [self.cC[:, 6144:7168], self.cC[:, 7168:8192]]
        wo_src = self.wbf_out.rearrange("(c p) o -> p c o", p=128)
        cd_src = self.cdsd.rearrange("(c p) o -> p c o", p=128)
        for qg in range(NQG):
            for hp in range(8):
                self.wait("sync", "pe")
                lq = self.dma("sq", self.cB[:, 12288 + 512 * hp:12288 + 512 * hp + 512],
                              self.qscr[128 * hp:128 * hp + 128, 512 * qg:512 * qg + 512])
            self.wait("tensor", "sq", lq)
            aob_done = self.attn_qg(qg, "a")
            self.wait("tensor", "dv", aob_done)
            self.wait("sync", "pe")
            ld = {0: self.dma("sq", wslot[0], wo_src[:, :, 0:128])}
            pe_ob = {}
            for i in range(8):
                if i + 1 < 8:
                    if i >= 1:
                        self.wait("sync", "pe", pe_ob.get(i - 1, 0))
                    ld[i + 1] = self.dma("sq", wslot[(i + 1) % 2],
                                         wo_src[:, :, 128 * (i + 1):128 * (i + 2)])
                self.wait("tensor", "sq", ld[i])
                bank = "p0" if i % 2 == 0 else "p1"
                self.use_bank(bank)
                ps = self.psb[bank]
                w = wslot[i % 2]
                for cc in range(8):
                    ff = (cc == 0); lst = (cc == 7)
                    if lst:
                        pe_ob[i] = self.op("tensor", "pe", lambda E, w=w, cc=cc, ps=ps: E["tensor"].matmul(
                            ps[:, :], w[:, 128 * cc:128 * cc + 128],
                            self.cB[:, 8192 + 512 * cc:8192 + 512 * cc + 512],
                            start=False, stop=True))
                    else:
                        self.opq("tensor", lambda E, w=w, cc=cc, ps=ps, ff=ff: E["tensor"].matmul(
                            ps[:, :], w[:, 128 * cc:128 * cc + 128],
                            self.cB[:, 8192 + 512 * cc:8192 + 512 * cc + 512],
                            start=ff, stop=False))
                self.wait("vector", "pe", pe_ob[i])
                bias = self.scal[:, self.BOUT + i:self.BOUT + i + 1]
                self.op("vector", "dv", lambda E, ps=ps, i=i, bias=bias: E["vector"].tensor_scalar(
                    self.yb[:, 512 * i:512 * i + 512], ps[:, :], bias, None, ALU.add))
                self.free_bank(bank, "dv")
            o_done = self.c["dv"]
            self.wait("tensor", "dv", o_done)
            for plane in range(2):
                self.wait("sync", "pe")
                ld2 = {0: self.dma("sq", wslot[0], cd_src[:, :, 1024 * plane:1024 * plane + 128])}
                pe_kb = {}
                for i in range(8):
                    if i + 1 < 8:
                        if i >= 1:
                            self.wait("sync", "pe", pe_kb.get(i - 1, 0))
                        ld2[i + 1] = self.dma("sq", wslot[(i + 1) % 2],
                                              cd_src[:, :, 1024 * plane + 128 * (i + 1):1024 * plane + 128 * (i + 2)])
                    self.wait("tensor", "sq", ld2[i])
                    bank = "p0" if i % 2 == 0 else "p1"
                    self.use_bank(bank)
                    ps = self.psb[bank]
                    w = wslot[i % 2]
                    for cc in range(8):
                        ff = (cc == 0); lst = (cc == 7)
                        if lst:
                            pe_kb[i] = self.op("tensor", "pe", lambda E, w=w, cc=cc, ps=ps: E["tensor"].matmul(
                                ps[:, :], w[:, 128 * cc:128 * cc + 128],
                                self.yb[:, 512 * cc:512 * cc + 512], start=False, stop=True))
                        else:
                            self.opq("tensor", lambda E, w=w, cc=cc, ps=ps, ff=ff: E["tensor"].matmul(
                                ps[:, :], w[:, 128 * cc:128 * cc + 128],
                                self.yb[:, 512 * cc:512 * cc + 512], start=ff, stop=False))
                    self.wait("vector", "pe", pe_kb[i])
                    zf = self.pb[:, 512 * (i % 2):512 * (i % 2) + 512]
                    self.wait("vector", "pe")
                    ev = self.op("vector", "dv", lambda E, ps=ps, zf=zf: E["vector"].tensor_copy(zf, ps[:, :]))
                    self.free_bank(bank, "dv")
                    for ts in range(4):
                        tbk = "ta" if (i * 4 + ts) % 2 == 0 else "tb"
                        self.use_bank(tbk)
                        pst = self.psb[tbk]
                        self.wait("tensor", "dv", ev)
                        td = self.op("tensor", "pe", lambda E, zf=zf, ts=ts, pst=pst: E["tensor"].transpose(
                            pst[:, 0:128], zf[:, 128 * ts:128 * ts + 128], self.eyeb_sb[:, :]))
                        self.wait("vector", "pe", td)
                        self.wait("vector", "gq")
                        for ver, mv in ((0, self.m0()), (1, self.m1())):
                            dst = self.cB[:, 4096 * ver + 1024 * ts + 128 * i:
                                          4096 * ver + 1024 * ts + 128 * i + 128]
                            self.op("vector", "dv", lambda E, pst=pst, dst=dst, mv=mv: E["vector"].tensor_scalar(
                                dst, pst[:, 0:128], mv, None, ALU.mult))
                        self.free_bank(tbk, "dv")
                z_done = self.c["dv"]
                self.wait("gpsimd", "dv", z_done)
                for ver in range(2):
                    for ts in range(4):
                        rbase = 8192 * ver + 4096 * plane + 512 * qg + 128 * ts
                        self.dma("gq", self.ag2_i[rbase:rbase + 128, :],
                                 self.cB[:, 4096 * ver + 1024 * ts:4096 * ver + 1024 * ts + 1024])
        self.wait("gpsimd", "gq")
        self.collective("cc3", self.ag2_i, self.ag2_o)

    def ph_fft(self):
        self.wait("sync", "cc3", self.c["cc3"])
        ag2 = self.ag2_o.rearrange("(r z l t) k -> r z l t k", r=2, z=2, l=32, t=128)
        y1 = self.y1scr.rearrange("(z k t) d -> z k t d", z=2, k=64, t=128)
        abuf = [self.cA[:, 0:4096], self.cA[:, 4096:8192]]
        stg = [self.cA[:, 8192:12288], self.cA[:, 12288:16384]]
        ld_it, pe_it = {}, {}
        def loadA(i):
            t2g, dkc = divmod(i, 2)
            slot = abuf[i % 2]
            for piece in range(4):
                r, z = piece % 2, piece // 2
                src = ag2[r, z, :, 8 * t2g:8 * t2g + 8, 512 * dkc:512 * dkc + 512]
                self.dma("sq", slot[32 * piece:32 * piece + 32, :], src)
            return self.c["sq"]
        ld_it[0] = loadA(0)
        NIT = 32
        for i in range(NIT):
            t2g, dkc = divmod(i, 2)
            if i + 1 < NIT:
                if i >= 1:
                    self.wait("sync", "pe", pe_it.get(i - 1, 0))
                ld_it[i + 1] = loadA(i + 1)
            self.wait("tensor", "sq", ld_it[i])
            slot = abuf[i % 2]
            sbank = ("p0", "p1") if i % 2 == 0 else ("p2", "p3")
            stage = stg[i % 2]
            self.wait("vector", "gq")
            for t2i in range(8):
                self.use_bank(sbank[0]); self.use_bank(sbank[1])
                pr, pi = self.psb[sbank[0]], self.psb[sbank[1]]
                rhs = slot[:, 512 * t2i:512 * t2i + 512]
                self.opq("tensor", lambda E, rhs=rhs, pr=pr: E["tensor"].matmul(
                    pr[0:64, :], self.stA_sb[:, 0:64], rhs, start=True, stop=True))
                mm = self.op("tensor", "pe", lambda E, rhs=rhs, pi=pi: E["tensor"].matmul(
                    pi[0:64, :], self.stA_sb[:, 64:128], rhs, start=True, stop=True))
                self.wait("vector", "pe", mm)
                self.op("vector", "dv", lambda E, pr=pr, stage=stage, t2i=t2i: E["vector"].tensor_copy(
                    stage[0:64, 512 * t2i:512 * t2i + 512], pr[0:64, :]))
                self.op("vector", "dv", lambda E, pi=pi, stage=stage, t2i=t2i: E["vector"].tensor_copy(
                    stage[64:128, 512 * t2i:512 * t2i + 512], pi[0:64, :]))
                self.free_bank(sbank[0], "dv"); self.free_bank(sbank[1], "dv")
            pe_it[i] = self.c["pe"]
            self.wait("gpsimd", "dv")
            for plane in range(2):
                dst = y1[plane, :, 8 * t2g:8 * t2g + 8, 512 * dkc:512 * dkc + 512]
                self.dma("gq", dst, stage[64 * plane:64 * plane + 64, :])
        self.wait("sync", "gq")
        self.wait("vector", "gq")
        yt = [self.cC[:, 0:2048], self.cC[:, 2048:4096]]
        ldc = {}
        def loadC(k1):
            slot = yt[k1 % 2]
            self.dma("sq", slot[:, 0:1024], y1[0, k1, :, :])
            self.dma("sq", slot[:, 1024:2048], y1[1, k1, :, :])
            return self.c["sq"]
        ldc[0] = loadC(0)
        pe_k1 = {}
        for k1 in range(64):
            if k1 + 1 < 64:
                if k1 >= 1:
                    self.wait("sync", "pe", pe_k1.get(k1 - 1, 0))
                ldc[k1 + 1] = loadC(k1 + 1)
            self.wait("tensor", "sq", ldc[k1])
            slot = yt[k1 % 2]
            for dc in range(8):
                bank = "p0" if dc % 2 == 0 else "p1"
                self.use_bank(bank)
                ps = self.psb[bank]
                lc = self.stC_sb[:, 128 * k1:128 * k1 + 64]
                ls = self.stC_sb[:, 128 * k1 + 64:128 * k1 + 128]
                self.opq("tensor", lambda E, lc=lc, ps=ps, slot=slot, dc=dc: E["tensor"].matmul(
                    ps[0:64, 0:128], lc, slot[:, 128 * dc:128 * dc + 128],
                    start=True, stop=False))
                mm = self.op("tensor", "pe", lambda E, ls=ls, ps=ps, slot=slot, dc=dc: E["tensor"].matmul(
                    ps[0:64, 0:128], ls, slot[:, 1024 + 128 * dc:1024 + 128 * dc + 128],
                    start=False, stop=True))
                ev_slot = self.lnf[0:64, 512 * (dc % 2):512 * (dc % 2) + 128]
                self.wait("vector", "pe", mm)
                ev = self.op("vector", "dv", lambda E, ps=ps, ev_slot=ev_slot: E["vector"].tensor_copy(
                    ev_slot, ps[0:64, 0:128]))
                self.free_bank(bank, "dv")
                tbk = "p2" if dc % 2 == 0 else "p3"
                self.use_bank(tbk)
                pst = self.psb[tbk]
                self.wait("tensor", "dv", ev)
                td = self.op("tensor", "pe", lambda E, ev_slot=ev_slot, pst=pst: E["tensor"].transpose(
                    pst[:, 0:64], ev_slot, self.eyef_sb[0:64, 0:64]))
                self.wait("vector", "pe", td)
                arena = self.cA if dc < 4 else self.cB
                rt3 = arena[:, 4096 * (dc % 4):4096 * (dc % 4) + 4096].rearrange(
                    "p (l b k) -> p l b k", l=32, b=2, k=64)
                dst = rt3[:, :, :, k1]
                self.op("vector", "dv", lambda E, pst=pst, dst=dst: E["vector"].tensor_copy(dst, pst[:, 0:64]))
                self.free_bank(tbk, "dv")
            pe_k1[k1] = self.c["pe"]
        rt_done = self.c["dv"]
        self.wait("vector", "dv", rt_done)
        for dc in range(8):
            lx = self.dma("sq", self.xk[:, :], self.x0f[128 * dc:128 * dc + 128, :])
            self.wait("vector", "sq", lx)
            arena = self.cA if dc < 4 else self.cB
            rt = arena[:, 4096 * (dc % 4):4096 * (dc % 4) + 4096]
            self.wait("vector", "gq")
            wb = self.op("vector", "dv", lambda E, rt=rt: E["vector"].tensor_tensor(
                self.lnf[:, :], self.xk[:, :], rt, ALU.add))
            self.wait("gpsimd", "dv", wb)
            self.dma("gq", self.xres[128 * dc:128 * dc + 128, :], self.lnf[:, :])
            self.wait("sync", "dv", wb)

    def ph_ffn(self, dst_dram):
        self.wait("sync", "gq", self.cast_ffn)
        w1s = self.w1bf.rearrange("(c p) o -> p c o", p=128)
        w2s = self.w2bf.rearrange("(c p) o -> p c o", p=128)
        h1s = self.h1bf.rearrange("(c p) o -> p c o", p=128)
        h2s = self.h2bf.rearrange("(c p) o -> p c o", p=128)
        w3s = self.w3bf.rearrange("(c p) o -> p c o", p=128)
        wA = [self.cC[:, 0:4096], self.cC[:, 4096:8192]]
        wB = [self.cD[:, 0:4096], self.cD[:, 4096:8192]]
        for tg in range(NQG):
            y_done = self.ln_tg(self.xres, tg)
            self.wait("tensor", "dv", y_done)
            pe_u = {}
            self.wait("sync", "pe")
            self.wait("scalar", "pe")
            ldu = {0: (self.dma("sq", wA[0][:, 0:1024], w1s[:, :, 0:128]),
                       self.dma("aq", wB[0][:, 0:1024], w2s[:, :, 0:128]))}
            for i in range(32):
                if i + 1 < 32:
                    if i >= 1:
                        self.wait("sync", "pe", pe_u.get(i - 1, 0))
                        self.wait("scalar", "pe", pe_u.get(i - 1, 0))
                    a = self.dma("sq", wA[(i + 1) % 2][:, 0:1024],
                                 w1s[:, :, 128 * (i + 1):128 * (i + 2)])
                    b = self.dma("aq", wB[(i + 1) % 2][:, 0:1024],
                                 w2s[:, :, 128 * (i + 1):128 * (i + 2)])
                    ldu[i + 1] = (a, b)
                self.wait("tensor", "sq", ldu[i][0])
                self.wait("tensor", "aq", ldu[i][1])
                b1 = "p0" if i % 2 == 0 else "p2"
                b2 = "p1" if i % 2 == 0 else "p3"
                self.use_bank(b1); self.use_bank(b2)
                u1, u2 = self.psb[b1], self.psb[b2]
                for cc in range(8):
                    self.opq("tensor", lambda E, i=i, cc=cc, u1=u1: E["tensor"].matmul(
                        u1[:, :], wA[i % 2][:, 128 * cc:128 * cc + 128],
                        self.yb[:, 512 * cc:512 * cc + 512],
                        start=(cc == 0), stop=(cc == 7)))
                for cc in range(8):
                    ff = (cc == 0); lst = (cc == 7)
                    if lst:
                        pe_u[i] = self.op("tensor", "pe", lambda E, i=i, cc=cc, u2=u2: E["tensor"].matmul(
                            u2[:, :], wB[i % 2][:, 128 * cc:128 * cc + 128],
                            self.yb[:, 512 * cc:512 * cc + 512], start=False, stop=True))
                    else:
                        self.opq("tensor", lambda E, i=i, cc=cc, u2=u2, ff=ff: E["tensor"].matmul(
                            u2[:, :], wB[i % 2][:, 128 * cc:128 * cc + 128],
                            self.yb[:, 512 * cc:512 * cc + 512], start=ff, stop=False))
                self.wait("scalar", "pe", pe_u[i])
                self.wait("vector", "pe", pe_u[i])
                sa = self.pb[:, 512 * (i % 2):512 * (i % 2) + 512]
                sbf = self.pb[:, 1024 + 512 * (i % 2):1536 + 512 * (i % 2)]
                s1c = self.scal[:, self.S1 + i:self.S1 + i + 1]
                s2c = self.scal[:, self.S2 + i:self.S2 + i + 1]
                self.wait("scalar", "dv")
                ac1 = self.op("scalar", "ac", lambda E, u1=u1, sa=sa, s1c=s1c: E["scalar"].activation(
                    out=sa, in_=u1[:, :], func=AF.Silu, scale=s1c))
                self.free_bank(b1, "ac")
                self.op("vector", "dv", lambda E, u2=u2, sbf=sbf, s2c=s2c: E["vector"].tensor_scalar(
                    sbf, u2[:, :], s2c, None, ALU.mult))
                self.free_bank(b2, "dv")
                self.wait("vector", "ac", ac1)
                self.op("vector", "dv", lambda E, sa=sa, sbf=sbf, i=i: E["vector"].tensor_tensor(
                    self.cA[:, 512 * i:512 * i + 512], sa, sbf, ALU.mult))
            g_done = self.c["dv"]
            self.wait("tensor", "dv", g_done)
            pe_h = {}
            self.wait("sync", "pe")
            ldh = {0: self.dma("sq", wA[0], h1s[:, :, 0:128])}
            for i in range(32):
                if i + 1 < 32:
                    if i >= 1:
                        self.wait("sync", "pe", pe_h.get(i - 1, 0))
                    ldh[i + 1] = self.dma("sq", wA[(i + 1) % 2],
                                          h1s[:, :, 128 * (i + 1):128 * (i + 2)])
                self.wait("tensor", "sq", ldh[i])
                bank = "p0" if i % 2 == 0 else "p1"
                self.use_bank(bank)
                ps = self.psb[bank]
                for cc in range(32):
                    ff = (cc == 0); lst = (cc == 31)
                    if lst:
                        pe_h[i] = self.op("tensor", "pe", lambda E, i=i, cc=cc, ps=ps: E["tensor"].matmul(
                            ps[:, :], wA[i % 2][:, 128 * cc:128 * cc + 128],
                            self.cA[:, 512 * cc:512 * cc + 512], start=False, stop=True))
                    else:
                        self.opq("tensor", lambda E, i=i, cc=cc, ps=ps, ff=ff: E["tensor"].matmul(
                            ps[:, :], wA[i % 2][:, 128 * cc:128 * cc + 128],
                            self.cA[:, 512 * cc:512 * cc + 512], start=ff, stop=False))
                self.wait("scalar", "pe", pe_h[i])
                shc = self.scal[:, self.SH1 + i:self.SH1 + i + 1]
                self.op("scalar", "ac", lambda E, ps=ps, i=i, shc=shc: E["scalar"].activation(
                    out=self.cB[:, 512 * i:512 * i + 512], in_=ps[:, :],
                    func=AF.Gelu, scale=shc))
                self.free_bank(bank, "ac")
            g2_done = self.c["ac"]
            self.wait("tensor", "ac", g2_done)
            pe_h2 = {}
            self.wait("sync", "pe")
            ldh2 = {0: self.dma("sq", wA[0], h2s[:, :, 0:128])}
            for i in range(32):
                if i + 1 < 32:
                    if i >= 1:
                        self.wait("sync", "pe", pe_h2.get(i - 1, 0))
                    ldh2[i + 1] = self.dma("sq", wA[(i + 1) % 2],
                                           h2s[:, :, 128 * (i + 1):128 * (i + 2)])
                self.wait("tensor", "sq", ldh2[i])
                bank = "p0" if i % 2 == 0 else "p1"
                self.use_bank(bank)
                ps = self.psb[bank]
                for cc in range(32):
                    ff = (cc == 0); lst = (cc == 31)
                    if lst:
                        pe_h2[i] = self.op("tensor", "pe", lambda E, i=i, cc=cc, ps=ps: E["tensor"].matmul(
                            ps[:, :], wA[i % 2][:, 128 * cc:128 * cc + 128],
                            self.cB[:, 512 * cc:512 * cc + 512], start=False, stop=True))
                    else:
                        self.opq("tensor", lambda E, i=i, cc=cc, ps=ps, ff=ff: E["tensor"].matmul(
                            ps[:, :], wA[i % 2][:, 128 * cc:128 * cc + 128],
                            self.cB[:, 512 * cc:512 * cc + 512], start=ff, stop=False))
                self.wait("scalar", "pe", pe_h2[i])
                shc = self.scal[:, self.SH2 + i:self.SH2 + i + 1]
                self.op("scalar", "ac", lambda E, ps=ps, i=i, shc=shc: E["scalar"].activation(
                    out=self.cA[:, 512 * i:512 * i + 512], in_=ps[:, :],
                    func=AF.Gelu, scale=shc))
                self.free_bank(bank, "ac")
            g3_done = self.c["ac"]
            self.wait("tensor", "ac", g3_done)
            pe_w3 = {}
            self.wait("sync", "pe")
            ldw3 = {0: self.dma("sq", wA[0], w3s[:, :, 0:128])}
            for i in range(8):
                if i + 1 < 8:
                    if i >= 1:
                        self.wait("sync", "pe", pe_w3.get(i - 1, 0))
                    ldw3[i + 1] = self.dma("sq", wA[(i + 1) % 2],
                                           w3s[:, :, 128 * (i + 1):128 * (i + 2)])
                self.wait("tensor", "sq", ldw3[i])
                bank = "p2" if i % 2 == 0 else "p3"
                self.use_bank(bank)
                ps = self.psb[bank]
                for cc in range(32):
                    ff = (cc == 0); lst = (cc == 31)
                    if lst:
                        pe_w3[i] = self.op("tensor", "pe", lambda E, i=i, cc=cc, ps=ps: E["tensor"].matmul(
                            ps[:, :], wA[i % 2][:, 128 * cc:128 * cc + 128],
                            self.cA[:, 512 * cc:512 * cc + 512], start=False, stop=True))
                    else:
                        self.opq("tensor", lambda E, i=i, cc=cc, ps=ps, ff=ff: E["tensor"].matmul(
                            ps[:, :], wA[i % 2][:, 128 * cc:128 * cc + 128],
                            self.cA[:, 512 * cc:512 * cc + 512], start=ff, stop=False))
                self.wait("vector", "pe", pe_w3[i])
                s3c = self.scal[:, self.S3 + i:self.S3 + i + 1]
                stgf = self.lnf[:, 512 * (i % 2):512 * (i % 2) + 512]
                self.wait("vector", "gq")
                wb = self.op("vector", "dv", lambda E, ps=ps, s3c=s3c, stgf=stgf, i=i: E["vector"].scalar_tensor_tensor(
                    stgf, ps[:, :], s3c, self.xk[:, 512 * i:512 * i + 512],
                    ALU.mult, ALU.add))
                self.free_bank(bank, "dv")
                self.wait("gpsimd", "dv", wb)
                self.dma("gq", dst_dram[128 * i:128 * i + 128, 512 * tg:512 * tg + 512], stgf)

    def ph_b_proj(self):
        self.wait("sync", "gq", self.cast_b)
        wsrc = self.wbf_b.rearrange("(c p) o -> p c o", p=128)
        wslot = [self.cC[:, 0:1024], self.cC[:, 1024:2048]]
        self.wait("vector", "sq", self.consts_ld)
        for sl in range(2):
            for ver in range(2):
                vs = 2048 + 2080 * sl + 520 * ver
                mv = self.m0() if ver == 0 else self.m1()
                for ts in range(4):
                    self.op("vector", "dv", lambda E, vs=vs, ts=ts, mv=mv: E["vector"].tensor_scalar(
                        self.cD[:, vs + 130 * ts + 64:vs + 130 * ts + 65],
                        self.ones_f[:, 0:1], mv, None, ALU.mult))
                    self.op("vector", "dv", lambda E, vs=vs, ts=ts, mv=mv: E["vector"].tensor_scalar(
                        self.cD[:, vs + 130 * ts + 129:vs + 130 * ts + 130],
                        self.ones_f[:, 0:1], mv, None, ALU.mult))
        for tg in range(NQG):
            y_done = self.ln_tg(self.xres, tg)
            self.wait("tensor", "dv", y_done)
            pe_i = {}
            self.wait("sync", "pe")
            ld = {0: self.dma("sq", wslot[0], wsrc[:, :, 0:128])}
            for i in range(24):
                if i + 1 < 24:
                    if i >= 1:
                        self.wait("sync", "pe", pe_i.get(i - 1, 0))
                    ld[i + 1] = self.dma("sq", wslot[(i + 1) % 2],
                                         wsrc[:, :, 128 * (i + 1):128 * (i + 2)])
                self.wait("tensor", "sq", ld[i])
                bank = "p0" if i % 2 == 0 else "p1"
                self.use_bank(bank)
                ps = self.psb[bank]
                w = wslot[i % 2]
                for cc in range(8):
                    ff = (cc == 0); lst = (cc == 7)
                    if lst:
                        pe_i[i] = self.op("tensor", "pe", lambda E, w=w, cc=cc, ps=ps: E["tensor"].matmul(
                            ps[:, :], w[:, 128 * cc:128 * cc + 128],
                            self.yb[:, 512 * cc:512 * cc + 512], start=False, stop=True))
                    else:
                        self.opq("tensor", lambda E, w=w, cc=cc, ps=ps, ff=ff: E["tensor"].matmul(
                            ps[:, :], w[:, 128 * cc:128 * cc + 128],
                            self.yb[:, 512 * cc:512 * cc + 512], start=ff, stop=False))
                self.wait("vector", "pe", pe_i[i])
                ob = i % 8
                if i < 8:
                    stg = self.pb[:, 512 * (i % 2):512 * (i % 2) + 512]
                    self.wait("vector", "gq")
                    ev = self.op("vector", "dv", lambda E, ps=ps, stg=stg: E["vector"].tensor_copy(stg, ps[:, :]))
                    self.free_bank(bank, "dv")
                    self.wait("gpsimd", "dv", ev)
                    self.dma("gq", self.qscr[128 * ob:128 * ob + 128, 512 * tg:512 * tg + 512], stg)
                elif i < 16:
                    self.wait("vector", "gq")
                    for ver, mv in ((0, self.m0()), (1, self.m1())):
                        stg = self.pb[:, 1024 + 1024 * ver + 512 * (i % 2):
                                      1024 + 1024 * ver + 512 * (i % 2) + 512]
                        dvs = self.op("vector", "dv", lambda E, ps=ps, stg=stg, mv=mv: E["vector"].tensor_scalar(
                            stg, ps[:, :], mv, None, ALU.mult))
                        self.wait("gpsimd", "dv", dvs)
                        self.dma("gq", self.ag3k_i[1024 * ver + 128 * ob:1024 * ver + 128 * ob + 128,
                                                   512 * tg:512 * tg + 512], stg)
                    self.free_bank(bank, "dv")
                else:
                    hp = ob
                    stg = self.pb[:, 3072 + 512 * (i % 2):3072 + 512 * (i % 2) + 512]
                    ev = self.op("vector", "dv", lambda E, ps=ps, stg=stg: E["vector"].tensor_copy(stg, ps[:, :]))
                    self.free_bank(bank, "dv")
                    sl = i % 2
                    self.wait("vector", "gq")
                    dvs = 0
                    for ts in range(4):
                        tbk = "ta" if ts % 2 == 0 else "tb"
                        self.use_bank(tbk)
                        pst = self.psb[tbk]
                        self.wait("tensor", "dv", ev)
                        td = self.op("tensor", "pe", lambda E, stg=stg, ts=ts, pst=pst: E["tensor"].transpose(
                            pst[:, 0:128], stg[:, 128 * ts:128 * ts + 128], self.eyeb_sb[:, :]))
                        self.wait("vector", "pe", td)
                        for ver, mv in ((0, self.m0()), (1, self.m1())):
                            vs = 2048 + 2080 * sl + 520 * ver
                            self.op("vector", "dv", lambda E, pst=pst, vs=vs, ts=ts, mv=mv: E["vector"].tensor_scalar(
                                self.cD[:, vs + 130 * ts + 0:vs + 130 * ts + 64],
                                pst[:, 0:64], mv, None, ALU.mult))
                            dvs = self.op("vector", "dv", lambda E, pst=pst, vs=vs, ts=ts, mv=mv: E["vector"].tensor_scalar(
                                self.cD[:, vs + 130 * ts + 65:vs + 130 * ts + 129],
                                pst[:, 64:128], mv, None, ALU.mult))
                        self.free_bank(tbk, "dv")
                    self.wait("gpsimd", "dv", dvs)
                    for ver in range(2):
                        vs = 2048 + 2080 * sl + 520 * ver
                        rbase = 32768 * ver + 4096 * hp + 512 * tg
                        dst = self.ag3v_i[rbase:rbase + 512, :].rearrange(
                            "(ts p) c -> p ts c", ts=4)
                        self.dma("gq", dst, self.cD[:, vs:vs + 520])
        self.wait("gpsimd", "gq")
        self.collective("cc4", self.ag3k_i, self.ag3k_o)
        self.collective("cc5", self.ag3v_i, self.ag3v_o)

    def ph_b_attn(self):
        self.wait("sync", "cc4", self.c["cc4"])
        self.wait("scalar", "cc5", self.c["cc5"])
        self.wait("sync", "gq", self.cast_b)
        wo_src = self.wbf_b.rearrange("(c p) o -> p c o", p=128)
        wslot = [self.cD[:, 0:1024], self.cD[:, 1024:2048]]
        for qg in range(NQG):
            for hp in range(8):
                self.wait("sync", "pe")
                lq = self.dma("sq", self.cB[:, 12288 + 512 * hp:12288 + 512 * hp + 512],
                              self.qscr[128 * hp:128 * hp + 128, 512 * qg:512 * qg + 512])
            self.wait("tensor", "sq", lq)
            aob_done = self.attn_qg(qg, "b")
            self.wait("tensor", "dv", aob_done)
            self.wait("sync", "pe")
            ld = {0: self.dma("sq", wslot[0], wo_src[:, :, 3072:3200])}
            pe_ob = {}
            for i in range(8):
                if i + 1 < 8:
                    if i >= 1:
                        self.wait("sync", "pe", pe_ob.get(i - 1, 0))
                    ld[i + 1] = self.dma("sq", wslot[(i + 1) % 2],
                                         wo_src[:, :, 3072 + 128 * (i + 1):3072 + 128 * (i + 2)])
                self.wait("tensor", "sq", ld[i])
                bank = "p1" if i % 2 == 0 else "p2"
                self.use_bank(bank)
                ps = self.psb[bank]
                w = wslot[i % 2]
                for cc in range(8):
                    ff = (cc == 0); lst = (cc == 7)
                    if lst:
                        pe_ob[i] = self.op("tensor", "pe", lambda E, w=w, cc=cc, ps=ps: E["tensor"].matmul(
                            ps[:, :], w[:, 128 * cc:128 * cc + 128],
                            self.cB[:, 8192 + 512 * cc:8192 + 512 * cc + 512],
                            start=False, stop=True))
                    else:
                        self.opq("tensor", lambda E, w=w, cc=cc, ps=ps, ff=ff: E["tensor"].matmul(
                            ps[:, :], w[:, 128 * cc:128 * cc + 128],
                            self.cB[:, 8192 + 512 * cc:8192 + 512 * cc + 512],
                            start=ff, stop=False))
                self.wait("scalar", "dv")
                lx = self.dma("aq", self.xk[:, 512 * i:512 * i + 512],
                              self.xres[128 * i:128 * i + 128, 512 * qg:512 * qg + 512])
                self.wait("vector", "pe", pe_ob[i])
                self.wait("vector", "aq", lx)
                stgf = self.lnf[:, 512 * (i % 2):512 * (i % 2) + 512]
                self.wait("vector", "gq")
                wb = self.op("vector", "dv", lambda E, ps=ps, stgf=stgf, i=i: E["vector"].tensor_tensor(
                    stgf, ps[:, :], self.xk[:, 512 * i:512 * i + 512], ALU.add))
                self.free_bank(bank, "dv")
                self.wait("gpsimd", "dv", wb)
                self.dma("gq", self.xres[128 * i:128 * i + 128, 512 * qg:512 * qg + 512], stgf)

    def ph_debug(self):
        name = self.debug_dump
        if not name or not hasattr(self, name):
            return
        src = getattr(self, name)
        rows, cols = src.tensor.shape
        rows = min(rows, 16384)
        cols = min(cols, 4096)
        self.barrier()
        for cc in ("cc1", "cc2", "cc3", "cc4", "cc5"):
            if self.c.get(cc, 0):
                self.wait("gpsimd", cc, self.c[cc])
        for base in range(0, rows, 128):
            self.dma("gq", self.dbg[base:base + 128, 0:cols], src[base:base + 128, 0:cols])
        self.barrier()

    def build(self):
        with contextlib.ExitStack() as stack:
            self._alloc(stack)
            pl = self.phase_limit
            self.src_ffn = self.xres
            self.ph_init()
            if pl in (104, 105):
                for dc in range(8):
                    self.dma("gq", self.xres[128 * dc:128 * dc + 128, :],
                             self.x0f[128 * dc:128 * dc + 128, :])
                self.barrier()
                if pl == 104:
                    self.ph_ffn(self.xres)
                else:
                    self.ph_b_proj()
                    self.ph_b_attn()
                self.ph_debug()
                self.barrier()
                self._emit_block()
                return self.nc
            if pl >= 1:
                self.ph_a1()
            if pl >= 2:
                self.ph_a2()
            if pl >= 3:
                self.ph_fft()
            if pl >= 4:
                self.barrier()
                self.ph_ffn(self.xres)
            if pl >= 5:
                self.barrier()
                self.ph_b_proj()
                self.ph_b_attn()
            if pl >= 6:
                self.barrier()
                self.ph_ffn(self.outT)
            self.ph_debug()
            self.barrier()
            self._emit_block()
        return self.nc

    def _emit_block(self):
        nc = self.nc
        E = {e: getattr(nc, e) for e in self.ENGS}
        with nc.Block() as block:
            @block.sync
            def _(sync):
                for f in self.prog["sync"]:
                    f(E)
            @block.gpsimd
            def _(gpsimd):
                for f in self.prog["gpsimd"]:
                    f(E)
            @block.scalar
            def _(scalar):
                for f in self.prog["scalar"]:
                    f(E)
            @block.vector
            def _(vector):
                for f in self.prog["vector"]:
                    f(E)
            @block.tensor
            def _(tensor):
                for f in self.prog["tensor"]:
                    f(E)



def _gidx(r):
    l = np.arange(ROWS) // 128
    off = np.arange(ROWS) % 128
    return 128 * (2 * l + r) + off


def _prep_shared(w):
    f = np.float32
    ct = np.ascontiguousarray
    d = {}
    d["wqkv"] = ct(np.asarray(w["L0_attn_a_Wqkv"], f))
    conv = np.asarray(w["L0_attn_a_conv_w"], f)
    cw = conv.reshape(16, 64, 64, 4)
    sbuf = np.zeros((128, 2048), f)
    for tau in range(4):
        for hp in range(8):
            blk = 64 * (4 * hp + tau)
            sbuf[0:64, blk:blk + 64] = cw[2 * hp, :, :, tau].T
            sbuf[64:128, blk:blk + 64] = cw[2 * hp + 1, :, :, tau].T
    d["convw_in"] = sbuf
    d["convb_in"] = ct(np.asarray(w["L0_attn_a_conv_b"], f).reshape(8, 128).T)
    d["nullk_in"] = ct(np.asarray(w["L0_attn_a_null_k"], f).reshape(8, 128).T)
    d["nullv_in"] = ct(np.asarray(w["L0_attn_a_null_v"], f).reshape(8, 128).T)
    d["wout"] = ct(np.asarray(w["L0_attn_a_W_out"], f))
    d["bout_in"] = ct(np.asarray(w["L0_attn_a_b_out"], f).reshape(8, 128).T)
    d["wqT"] = ct(np.asarray(w["L0_attn_b_Wq"], f).T)
    d["wkT"] = ct(np.asarray(w["L0_attn_b_Wk"], f).T)
    d["wvT"] = ct(np.asarray(w["L0_attn_b_Wv"], f).T)
    d["woT"] = ct(np.asarray(w["L0_attn_b_Wo"], f).T)
    d["w1t"] = ct(np.asarray(w["L0_ffn_w1_w"], f).T)
    d["w2t"] = ct(np.asarray(w["L0_ffn_w2_w"], f).T)
    d["h1t"] = ct(np.asarray(w["L0_ffn_h1_w"], f).T)
    d["h2t"] = ct(np.asarray(w["L0_ffn_h2_w"], f).T)
    d["w3t"] = ct(np.asarray(w["L0_ffn_w3_w"], f).T)
    d["s1_in"] = ct(np.asarray(w["L0_ffn_w1_s"], f).reshape(32, 128).T)
    d["s2_in"] = ct(np.asarray(w["L0_ffn_w2_s"], f).reshape(32, 128).T)
    d["sh1_in"] = ct(np.asarray(w["L0_ffn_h1_s"], f).reshape(32, 128).T)
    d["sh2_in"] = ct(np.asarray(w["L0_ffn_h2_s"], f).reshape(32, 128).T)
    d["s3_in"] = ct(np.asarray(w["L0_ffn_w3_s"], f).reshape(8, 128).T)
    n = np.arange(D_)
    CD = np.cos(2 * np.pi * np.outer(n, n) / D_)
    SD = np.sin(2 * np.pi * np.outer(n, n) / D_)
    d["cdsd"] = np.concatenate([CD, SD], 1).astype(BF)
    d["stA_in"] = _stepA_const()
    d["eyeb_in"] = np.eye(128, dtype=BF)
    d["eyef_in"] = np.eye(128, dtype=np.float32)
    d["ones_in"] = np.ones((128, 128), np.float32)
    return d


_CACHED = {}


def _get_nc():
    key = (PHASE_LIMIT, DEBUG_DUMP)
    if key not in _CACHED:
        _CACHED[key] = MK(phase_limit=PHASE_LIMIT, debug_dump=DEBUG_DUMP).build()
    return _CACHED[key]


def kernel(**inputs):
    shared = _prep_shared(inputs)
    x0 = np.asarray(inputs["L0_attn_a_x"], np.float32)
    percore = []
    for r in (0, 1):
        mvec = np.zeros((128, 8), np.float32)
        mvec[:, 0] = 1.0 if r == 0 else 0.0
        mvec[:, 1] = 0.0 if r == 0 else 1.0
        mvec[:, 2] = 1e-5
        mvec[:, 3] = 1.0 / 32.0
        mvec[:, 4] = 1.0 / 8.0
        mvec[:, 5] = -0.5
        percore.append({"stC_in": _stepC_const(r), "am_in": _maskA(r),
                        "bm_in": _maskB(r), "mvec_in": mvec})
    gidx = [_gidx(0), _gidx(1)]
    in_maps = []
    for c in range(8):
        b, r = c // 2, c % 2
        m = dict(shared)
        m.update(percore[r])
        m["x0f"] = np.ascontiguousarray(x0[b][gidx[r]].T)
        in_maps.append(m)
    nc = _get_nc()
    res = run_bass_kernel_spmd(nc, in_maps, core_ids=list(range(8)))
    out = np.zeros((B_, T_, D_), np.float32)
    for c in range(8):
        b, r = c // 2, c % 2
        out[b][gidx[r]] = res.results[c]["out"].T
    return out


# version 03
# speedup vs baseline: 1.0099x; 1.0099x over previous
import contextlib
import os
import numpy as np
import ml_dtypes

os.environ.setdefault("GAUGE_NTFF_IGNORE_DMA", "1")
import concourse.bass as bass
import concourse.mybir as mybir
from concourse.bass_utils import run_bass_kernel_spmd

F32 = mybir.dt.float32
BF16 = mybir.dt.bfloat16
AF = mybir.ActivationFunctionType
ALU = mybir.AluOpType
BF = ml_dtypes.bfloat16

B_, T_, D_, HEADS, CF, HID = 4, 8192, 1024, 16, 4, 4096
ROWS = T_ // 2
NL = ROWS // 128
NQG = ROWS // 512
N1 = T_ // 128
NEG = -1e30
GROUPS = [[0, 1], [2, 3], [4, 5], [6, 7]]

PHASE_LIMIT = 99
DEBUG_DUMP = None



def _perm():
    p = np.empty(N1, np.int64)
    p[:N1 // 2] = 2 * np.arange(N1 // 2)
    p[N1 // 2:] = 2 * np.arange(N1 // 2) + 1
    return p


def _stepA_const():
    perm = _perm()
    k1 = np.arange(N1)
    C = np.cos(2 * np.pi * perm[:, None] * k1[None, :] / N1)
    S = np.sin(2 * np.pi * perm[:, None] * k1[None, :] / N1)
    Ar = np.concatenate([C, -S], 0)
    Ai = np.concatenate([-S, -C], 0)
    return np.concatenate([Ar, Ai], 1).astype(BF)


def _stepC_const(r):
    t2 = np.arange(128)
    out = np.zeros((128, 64 * 128), np.float32)
    ll = np.arange(32)
    bb = np.arange(2)
    k2 = (4 * ll[:, None] + 2 * r + bb[None, :]).reshape(-1)
    for k1 in range(N1):
        th = 2 * np.pi * t2[:, None] * (k1 / T_ + k2[None, :] / 128.0)
        out[:, 128 * k1:128 * k1 + 64] = np.cos(th)
        out[:, 128 * k1 + 64:128 * k1 + 128] = np.sin(th)
    return out.astype(BF)


def _maskA(r):
    null = np.full((128, 512), NEG, np.float32)
    null[0, :] = 0.0
    bands = []
    for half in range(2):
        m = np.zeros((128, 512), np.float32)
        for a in range(4):
            ga = 2 * a + half
            for s in range(4):
                gq = 2 * s + r
                blk = m[32 * a:32 * a + 32, 128 * s:128 * s + 128]
                if ga < gq:
                    blk[:] = 0.0
                elif ga > gq:
                    blk[:] = NEG
                else:
                    c = np.arange(32)
                    i = np.arange(128)
                    blk[:] = np.where((4 * c[:, None] + 3) <= i[None, :], 0.0, NEG)
        bands.append(m)
    return np.concatenate([null] + bands, 1).astype(BF)


def _maskB(r):
    m = np.zeros((128, 8 * 512), np.float32)
    for half in range(2):
        for a in range(4):
            ga = 2 * a + half
            blk = m[:, (half * 4 + a) * 512:(half * 4 + a + 1) * 512]
            for s in range(4):
                gq = 2 * s + r
                sb = blk[:, 128 * s:128 * s + 128]
                if ga < gq:
                    sb[:] = 0.0
                elif ga > gq:
                    sb[:] = NEG
                else:
                    kk = np.arange(128)
                    i = np.arange(128)
                    sb[:] = np.where(kk[:, None] <= i[None, :], 0.0, NEG)
    return m.astype(BF)



class MK:
    ENGS = ("sync", "gpsimd", "scalar", "vector", "tensor")
    QENG = {"sq": "sync", "aq": "scalar", "gq": "gpsimd"}

    def __init__(self, phase_limit=99, debug_dump=None):
        self.phase_limit = phase_limit
        self.debug_dump = debug_dump
        self.nc = bass.Bass("TRN2", target_bir_lowering=False, debug=False)
        self.prog = {e: [] for e in self.ENGS}
        self.c = {}
        self.lastwait = {}
        self.bankfree = {}
        self._declare_dram()

    def emit(self, eng, fn):
        self.prog[eng].append(fn)

    def bump(self, sem, n):
        self.c[sem] = self.c.get(sem, 0) + n
        return self.c[sem]

    def dma(self, q, out, in_, **kw):
        eng = self.QENG[q]
        tgt = self.bump(q, 16)
        def f(E, out=out, in_=in_, kw=kw, eng=eng, q=q):
            E[eng].dma_start(out=out, in_=in_, **kw).then_inc(self.sem[q], 16)
        self.emit(eng, f)
        return tgt

    def wait(self, eng, semname, val=None):
        if val is None:
            val = self.c.get(semname, 0)
        if val <= 0:
            return
        key = (eng, semname)
        if self.lastwait.get(key, -1) >= val:
            return
        self.lastwait[key] = val
        def f(E, semname=semname, val=val, eng=eng):
            E[eng].wait_ge(self.sem[semname], val)
        self.emit(eng, f)

    def op(self, eng, semname, fn, n=1):
        tgt = self.bump(semname, n)
        def f(E, fn=fn, semname=semname, n=n):
            fn(E).then_inc(self.sem[semname], n)
        self.emit(eng, f)
        return tgt

    def opq(self, eng, fn):
        self.emit(eng, lambda E, fn=fn: fn(E))

    def barrier(self):
        for e in self.ENGS:
            for s in ("sq", "aq", "gq", "pe", "dv", "ac"):
                self.wait(e, s)

    def use_bank(self, bank):
        ent = self.bankfree.get(bank)
        if ent:
            self.wait("tensor", ent[0], ent[1])

    def free_bank(self, bank, sem):
        self.bankfree[bank] = (sem, self.c.get(sem, 0))

    def collective(self, sem, src, dst):
        self.bump(sem, 1)
        def f(E, sem=sem, src=src, dst=dst):
            E["gpsimd"].collective_compute(
                "AllReduce", ALU.add, replica_groups=GROUPS,
                ins=[src.opt()], outs=[dst.opt()]).then_inc(self.sem[sem])
        self.emit("gpsimd", f)

    def _declare_dram(self):
        nc = self.nc
        def di(name, shape, dt=F32):
            setattr(self, name, nc.dram_tensor(name, list(shape), dt,
                                               kind="ExternalInput").ap())
        def ds(name, shape, dt):
            setattr(self, name, nc.dram_tensor(name, list(shape), dt,
                                               kind="Internal").ap())
        di("x0f", (1024, 4096))
        di("wqkv", (1024, 3072))
        di("convw_in", (128, 2048))
        di("convb_in", (128, 8))
        di("nullk_in", (128, 8))
        di("nullv_in", (128, 8))
        di("wout", (1024, 1024))
        di("bout_in", (128, 8))
        di("wqT", (1024, 1024)); di("wkT", (1024, 1024))
        di("wvT", (1024, 1024)); di("woT", (1024, 1024))
        di("w1t", (1024, 4096)); di("w2t", (1024, 4096))
        di("h1t", (4096, 4096)); di("h2t", (4096, 4096))
        di("w3t", (4096, 1024))
        di("s1_in", (128, 32)); di("s2_in", (128, 32))
        di("sh1_in", (128, 32)); di("sh2_in", (128, 32))
        di("s3_in", (128, 8))
        di("mvec_in", (128, 8))
        di("cdsd", (1024, 2048), BF16)
        di("stA_in", (128, 128), BF16)
        di("stC_in", (128, 8192), BF16)
        di("am_in", (128, 1536), BF16)
        di("bm_in", (128, 4096), BF16)
        di("eyeb_in", (128, 128), BF16)
        di("eyef_in", (128, 128))
        di("ones_in", (128, 128))
        self.outT = nc.dram_tensor("out", [1024, 4096], F32,
                                   kind="ExternalOutput").ap()
        if self.debug_dump:
            self.dbg = nc.dram_tensor("dbg", [16384, 4096], F32,
                                      kind="ExternalOutput").ap()
        ds("xres", (1024, 4096), F32)
        ds("qscr", (1024, 4096), BF16)
        ds("wbf_qkv", (1024, 3072), BF16)
        ds("wbf_out", (1024, 1024), BF16)
        ds("wbf_b", (1024, 4096), BF16)
        ds("w1bf", (1024, 4096), BF16)
        ds("w2bf", (1024, 4096), BF16)
        ds("h1bf", (4096, 4096), BF16)
        ds("h2bf", (4096, 4096), BF16)
        ds("w3bf", (4096, 1024), BF16)
        ds("ag1k_i", (2048, 1024), BF16); ds("ag1k_o", (2048, 1024), BF16)
        ds("ag1v_i", (16384, 130), BF16); ds("ag1v_o", (16384, 130), BF16)
        ds("ag2_i", (16384, 1024), BF16); ds("ag2_o", (16384, 1024), BF16)
        ds("ag3k_i", (2048, 4096), BF16); ds("ag3k_o", (2048, 4096), BF16)
        ds("ag3v_i", (65536, 130), BF16); ds("ag3v_o", (65536, 130), BF16)
        ds("y1scr", (16384, 1024), BF16)

    def _alloc(self, stack):
        nc = self.nc
        def sb(name, shape, dt):
            setattr(self, name, stack.enter_context(
                nc.sbuf_tensor(name, list(shape), dt)))
        sb("cA", (128, 16384), BF16)
        sb("cB", (128, 16384), BF16)
        sb("cC", (128, 8192), BF16)
        sb("cD", (128, 8192), BF16)
        sb("xk", (128, 4096), F32)
        sb("yb", (128, 4096), BF16)
        sb("stC_sb", (128, 8192), BF16)
        sb("am_sb", (128, 1536), BF16)
        sb("bm_sb", (128, 4096), BF16)
        sb("nKT", (128, 1024), BF16)
        sb("stA_sb", (128, 128), BF16)
        sb("eyeb_sb", (128, 128), BF16)
        sb("eyef_sb", (128, 128), F32)
        sb("ones_f", (128, 128), F32)
        sb("scal", (128, 160), F32)
        sb("lnf", (128, 4096), F32)
        sb("pb", (128, 4096), BF16)
        self.psb = {}
        for nm in ("p0", "p1", "p2", "p3", "p4", "p5"):
            self.psb[nm] = stack.enter_context(
                nc.psum_tensor(nm, [128, 512], F32))
        self.psb["ta"] = stack.enter_context(
            nc.psum_tensor("ta", [128, 1024], BF16))
        self.psb["tb"] = stack.enter_context(
            nc.psum_tensor("tb", [128, 1024], BF16))
        self.sem = {}
        for n in ("sq", "aq", "gq", "pe", "dv", "ac",
                  "cc1", "cc2", "cc3", "cc4", "cc5"):
            self.sem[n] = stack.enter_context(nc.semaphore(n))

    S1, S2, SH1, SH2 = 0, 32, 64, 96
    S3, CONVB, BOUT, MV = 128, 136, 144, 152

    def m0(self):
        return self.scal[:, self.MV:self.MV + 1]

    def m1(self):
        return self.scal[:, self.MV + 1:self.MV + 2]

    def kconst(self, idx, p0=0, n=128):
        return self.scal[p0:p0 + n, self.MV + idx:self.MV + idx + 1]

    def ph_init(self):
        for dst, src in [(self.stC_sb, self.stC_in), (self.am_sb, self.am_in),
                         (self.bm_sb, self.bm_in), (self.stA_sb, self.stA_in),
                         (self.eyeb_sb, self.eyeb_in), (self.eyef_sb, self.eyef_in),
                         (self.ones_f, self.ones_in)]:
            self.dma("sq", dst[:, :], src[:, :])
        self.dma("sq", self.scal[:, 0:32], self.s1_in[:, :])
        self.dma("sq", self.scal[:, 32:64], self.s2_in[:, :])
        self.dma("sq", self.scal[:, 64:96], self.sh1_in[:, :])
        self.dma("sq", self.scal[:, 96:128], self.sh2_in[:, :])
        self.dma("sq", self.scal[:, 128:136], self.s3_in[:, :])
        self.dma("sq", self.scal[:, 136:144], self.convb_in[:, :])
        self.dma("sq", self.scal[:, 144:152], self.bout_in[:, :])
        self.dma("sq", self.scal[:, 152:160], self.mvec_in[:, :])
        self.consts_ld = self.c["sq"]

        self.dma("gq", self.wbf_qkv[:, :], self.wqkv[:, :])
        self.dma("gq", self.pb[:, 0:2048], self.convw_in[:, :])
        self.dma("gq", self.lnf[:, 0:8], self.nullk_in[:, :])
        self.dma("gq", self.lnf[:, 8:16], self.nullv_in[:, :])
        self.cast_a1 = self.c["gq"]
        self.dma("gq", self.wbf_out[:, :], self.wout[:, :])
        self.cast_wout = self.c["gq"]
        self.dma("gq", self.wbf_b[:, 0:1024], self.wqT[:, :])
        self.dma("gq", self.wbf_b[:, 1024:2048], self.wkT[:, :])
        self.dma("gq", self.wbf_b[:, 2048:3072], self.wvT[:, :])
        self.dma("gq", self.wbf_b[:, 3072:4096], self.woT[:, :])
        self.cast_b = self.c["gq"]
        self.dma("gq", self.w1bf[:, :], self.w1t[:, :])
        self.dma("gq", self.w2bf[:, :], self.w2t[:, :])
        for i in range(4):
            self.dma("gq", self.h1bf[1024 * i:1024 * (i + 1), :],
                     self.h1t[1024 * i:1024 * (i + 1), :])
        for i in range(4):
            self.dma("gq", self.h2bf[1024 * i:1024 * (i + 1), :],
                     self.h2t[1024 * i:1024 * (i + 1), :])
        self.dma("gq", self.w3bf[:, :], self.w3t[:, :])
        self.cast_ffn = self.c["gq"]

        self.wait("vector", "gq", self.cast_a1)
        self.wait("vector", "sq", self.consts_ld)
        self.op("vector", "dv", lambda E: E["vector"].memset(self.nKT[:, :], 0.0))
        for hp in range(8):
            self.op("vector", "dv", lambda E, hp=hp: E["vector"].tensor_copy(
                self.nKT[:, 128 * hp:128 * hp + 1], self.lnf[:, hp:hp + 1]))
        self.wait("tensor", "sq", self.consts_ld)
        self.wait("tensor", "gq", self.cast_a1)
        self.use_bank("p5")
        tnv = self.op("tensor", "pe", lambda E: E["tensor"].transpose(
            self.psb["p5"][0:8, 0:128], self.lnf[:, 8:16], self.eyef_sb[:, :]))
        self.wait("vector", "pe", tnv)
        self.op("vector", "dv", lambda E: E["vector"].tensor_copy(
            self.lnf[0:8, 16:144], self.psb["p5"][0:8, 0:128]))
        self.free_bank("p5", "dv")

    def build_nvt(self, dst, c0):
        self.op("vector", "dv", lambda E, dst=dst, c0=c0: E["vector"].memset(
            dst[:, c0:c0 + 1040], 0.0))
        for hp in range(8):
            b = c0 + 130 * hp
            self.op("vector", "dv", lambda E, dst=dst, b=b: E["vector"].memset(
                dst[:, b + 64:b + 65], 1.0))
            done = self.op("vector", "dv", lambda E, dst=dst, b=b: E["vector"].memset(
                dst[:, b + 129:b + 130], 1.0))
        self.wait("sync", "dv", done)

    def _nvt_rows(self, dst, c0):
        for hp in range(8):
            b = c0 + 130 * hp
            self.dma("gq", dst[0:1, b + 0:b + 64], self.lnf[hp:hp + 1, 16:80])
            self.dma("gq", dst[0:1, b + 65:b + 129], self.lnf[hp:hp + 1, 80:144])

    def ln_tg(self, src, tg):
        t0 = 512 * tg
        self.wait("sync", "dv")
        ld = 0
        for cc in range(8):
            ld = self.dma("sq", self.xk[:, 512 * cc:512 * (cc + 1)],
                          src[128 * cc:128 * (cc + 1), t0:t0 + 512])
        self.wait("tensor", "sq", ld)
        self.wait("scalar", "sq", ld)
        self.use_bank("p0"); self.use_bank("p1")
        for cc in range(8):
            xc = self.xk[:, 512 * cc:512 * (cc + 1)]
            self.opq("tensor", lambda E, xc=xc, cc=cc: E["tensor"].matmul(
                self.psb["p0"][0:1, :], self.ones_f[:, 0:1], xc,
                start=(cc == 0), stop=(cc == 7)))
            sq = self.lnf[:, 2048 + 512 * (cc % 2):2560 + 512 * (cc % 2)]
            if cc >= 2:
                self.wait("scalar", "pe")
            t = self.op("scalar", "ac", lambda E, xc=xc, sq=sq: E["scalar"].activation(
                out=sq, in_=xc, func=AF.Square))
            self.wait("tensor", "ac", t)
            self.op("tensor", "pe", lambda E, sq=sq, cc=cc: E["tensor"].matmul(
                self.psb["p1"][0:1, :], self.ones_f[:, 0:1], sq,
                start=(cc == 0), stop=(cc == 7)))
        stats_done = self.c["pe"]
        self.wait("vector", "pe", stats_done)
        mrow = self.lnf[0:1, 3072:3584]
        arow = self.lnf[0:1, 3584:4096]
        self.op("vector", "dv", lambda E, mrow=mrow: E["vector"].tensor_scalar(
            mrow, self.psb["p0"][0:1, :], 1.0 / 1024, None, ALU.mult))
        self.op("vector", "dv", lambda E, mrow=mrow, arow=arow: E["vector"].tensor_tensor(
            arow, mrow, mrow, ALU.mult))
        self.op("vector", "dv", lambda E, arow=arow: E["vector"].scalar_tensor_tensor(
            arow, self.psb["p1"][0:1, :], 1.0 / 1024, arow, ALU.mult, ALU.subtract))
        self.free_bank("p0", "dv"); self.free_bank("p1", "dv")
        self.wait("scalar", "dv")
        self.op("scalar", "ac", lambda E, arow=arow: E["scalar"].activation(
            out=arow, in_=arow, func=AF.Ln, bias=self.kconst(2, 0, 1)))
        self.op("scalar", "ac", lambda E, arow=arow: E["scalar"].activation(
            out=arow, in_=arow, func=AF.Exp, scale=self.kconst(5, 0, 1)))
        self.wait("vector", "ac")
        self.op("vector", "dv", lambda E, mrow=mrow, arow=arow: E["vector"].tensor_tensor(
            mrow, mrow, arow, ALU.mult))
        rows_done = self.c["dv"]
        self.wait("tensor", "dv", rows_done)
        self.use_bank("p2"); self.use_bank("p3")
        self.opq("tensor", lambda E, arow=arow: E["tensor"].matmul(
            self.psb["p2"][:, :], self.ones_f[0:1, :], arow, start=True, stop=True))
        self.op("tensor", "pe", lambda E, mrow=mrow: E["tensor"].matmul(
            self.psb["p3"][:, :], self.ones_f[0:1, :], mrow, start=True, stop=True))
        bc_done = self.c["pe"]
        self.wait("vector", "pe", bc_done)
        for cc in range(8):
            xc = self.xk[:, 512 * cc:512 * (cc + 1)]
            t1 = self.lnf[:, 2048 + 512 * (cc % 2):2560 + 512 * (cc % 2)]
            yc = self.yb[:, 512 * cc:512 * (cc + 1)]
            self.op("vector", "dv", lambda E, xc=xc, t1=t1: E["vector"].tensor_tensor(
                t1, xc, self.psb["p2"][:, :], ALU.mult))
            self.op("vector", "dv", lambda E, t1=t1, yc=yc: E["vector"].tensor_tensor(
                yc, t1, self.psb["p3"][:, :], ALU.subtract))
        self.free_bank("p2", "dv"); self.free_bank("p3", "dv")
        return self.c["dv"]

    def ph_a1(self):
        self.wait("sync", "gq", self.cast_a1)
        wslot = [self.cC[:, 0:1024], self.cC[:, 1024:2048]]
        wsrc = self.wbf_qkv.rearrange("(c p) o -> p c o", p=128)
        self.wait("vector", "gq", self.cast_a1)
        self.wait("vector", "sq", self.consts_ld)
        for sl in range(2):
            for ver in range(2):
                vs = 2048 + 520 * (2 * sl + ver)
                mv = self.m0() if ver == 0 else self.m1()
                self.op("vector", "dv", lambda E, vs=vs, mv=mv: E["vector"].tensor_scalar(
                    self.pb[:, vs + 64:vs + 65], self.ones_f[:, 0:1], mv, None, ALU.mult))
                self.op("vector", "dv", lambda E, vs=vs, mv=mv: E["vector"].tensor_scalar(
                    self.pb[:, vs + 129:vs + 130], self.ones_f[:, 0:1], mv, None, ALU.mult))
        for tg in range(NQG):
            y_done = self.ln_tg(self.x0f, tg)
            self.wait("tensor", "dv", y_done)
            pe_iter = {}
            self.wait("sync", "pe")
            ld = {0: self.dma("sq", wslot[0], wsrc[:, :, 0:128])}
            for i in range(24):
                if i + 1 < 24:
                    if i >= 1:
                        self.wait("sync", "pe", pe_iter.get(i - 1, 0))
                    ld[i + 1] = self.dma("sq", wslot[(i + 1) % 2],
                                         wsrc[:, :, 128 * (i + 1):128 * (i + 2)])
                self.wait("tensor", "sq", ld[i])
                bank = "p0" if i % 2 == 0 else "p1"
                self.use_bank(bank)
                ps = self.psb[bank]
                w = wslot[i % 2]
                for cc in range(8):
                    ff = (cc == 0); lst = (cc == 7)
                    if lst:
                        pe_iter[i] = self.op("tensor", "pe", lambda E, w=w, cc=cc, ps=ps: E["tensor"].matmul(
                            ps[:, :], w[:, 128 * cc:128 * cc + 128],
                            self.yb[:, 512 * cc:512 * cc + 512], start=False, stop=True))
                    else:
                        self.opq("tensor", lambda E, w=w, cc=cc, ps=ps, ff=ff: E["tensor"].matmul(
                            ps[:, :], w[:, 128 * cc:128 * cc + 128],
                            self.yb[:, 512 * cc:512 * cc + 512], start=ff, stop=False))
                self.wait("vector", "pe", pe_iter[i])
                if i < 8:
                    stg = self.cD[:, 512 * (i % 2):512 * (i % 2) + 512]
                    self.wait("vector", "gq")
                    ev = self.op("vector", "dv", lambda E, ps=ps, stg=stg: E["vector"].tensor_copy(stg, ps[:, :]))
                    self.free_bank(bank, "dv")
                    self.wait("gpsimd", "dv", ev)
                    self.dma("gq", self.qscr[128 * i:128 * i + 128, 512 * tg:512 * tg + 512], stg)
                elif i < 16:
                    kc = i - 8
                    self.op("vector", "dv", lambda E, ps=ps, kc=kc: E["vector"].tensor_copy(
                        self.cB[:, 512 * kc:512 * kc + 512], ps[:, :]))
                    self.free_bank(bank, "dv")
                else:
                    vc = i - 16
                    self.op("vector", "dv", lambda E, ps=ps, vc=vc: E["vector"].tensor_copy(
                        self.cA[:, 512 * vc:512 * vc + 512], ps[:, :]))
                    self.free_bank(bank, "dv")
            kv_done = self.c["dv"]
            self.wait("tensor", "dv", kv_done)
            for src_arena, is_v in ((self.cB, False), (self.cA, True)):
                for hp in range(8):
                    for par in range(2):
                        bank = "p2" if par == 0 else "p3"
                        self.use_bank(bank)
                        ps = self.psb[bank]
                        p0 = 64 * par
                        for tau in range(4):
                            wcol = 64 * (4 * hp + tau)
                            lhs = self.pb[p0:p0 + 64, wcol:wcol + 64]
                            rhs = src_arena[p0:p0 + 64, 512 * hp + tau:512 * hp + 512:4]
                            ff = (tau == 0); lst = (tau == 3)
                            if lst:
                                self.op("tensor", "pe", lambda E, lhs=lhs, rhs=rhs, ps=ps, p0=p0: E["tensor"].matmul(
                                    ps[0:64, 0:128], lhs, rhs, start=False, stop=True,
                                    tile_position=(p0, 0)))
                            else:
                                self.opq("tensor", lambda E, lhs=lhs, rhs=rhs, ps=ps, ff=ff, p0=p0: E["tensor"].matmul(
                                    ps[0:64, 0:128], lhs, rhs, start=ff, stop=False,
                                    tile_position=(p0, 0)))
                        mm = self.c["pe"]
                        self.wait("vector", "pe", mm)
                        bias = self.scal[p0:p0 + 64, self.CONVB + hp:self.CONVB + hp + 1]
                        if not is_v:
                            self.wait("vector", "gq")
                            for ver, mv, colb in ((0, self.m0(), 1024), (1, self.m1(), 3072)):
                                dst = self.cD[p0:p0 + 64, colb + 128 * hp:colb + 128 * hp + 128]
                                mvs = mv[p0:p0 + 64, :] if p0 else mv[0:64, :]
                                self.op("vector", "dv", lambda E, ps=ps, dst=dst, bias=bias, mvs=mvs: E["vector"].tensor_scalar(
                                    dst, ps[0:64, 0:128], bias, mvs, ALU.add, ALU.mult))
                        else:
                            dst = self.cD[p0:p0 + 64, 2048 + 128 * hp:2048 + 128 * hp + 128]
                            self.op("vector", "dv", lambda E, ps=ps, dst=dst, bias=bias: E["vector"].tensor_scalar(
                                dst, ps[0:64, 0:128], bias, None, ALU.add))
                        self.free_bank(bank, "dv")
            conv_done = self.c["dv"]
            self.wait("gpsimd", "dv", conv_done)
            for hp in range(8):
                self.dma("gq", self.ag1k_i[128 * hp:128 * hp + 128, 128 * tg:128 * tg + 128],
                         self.cD[:, 1024 + 128 * hp:1024 + 128 * hp + 128])
                self.dma("gq", self.ag1k_i[1024 + 128 * hp:1024 + 128 * hp + 128, 128 * tg:128 * tg + 128],
                         self.cD[:, 3072 + 128 * hp:3072 + 128 * hp + 128])
            for hp in range(8):
                tb = "ta" if hp % 2 == 0 else "tb"
                self.use_bank(tb)
                pst = self.psb[tb]
                self.wait("tensor", "dv", conv_done)
                td = self.op("tensor", "pe", lambda E, hp=hp, pst=pst: E["tensor"].transpose(
                    pst[:, 0:128], self.cD[:, 2048 + 128 * hp:2048 + 128 * hp + 128],
                    self.eyeb_sb[:, :]))
                self.wait("vector", "pe", td)
                sl = hp % 2
                self.wait("vector", "gq")
                dvs = 0
                for ver, mv in ((0, self.m0()), (1, self.m1())):
                    vs = 2048 + 520 * (2 * sl + ver)
                    self.op("vector", "dv", lambda E, pst=pst, vs=vs, mv=mv: E["vector"].tensor_scalar(
                        self.pb[:, vs + 0:vs + 64], pst[:, 0:64], mv, None, ALU.mult))
                    dvs = self.op("vector", "dv", lambda E, pst=pst, vs=vs, mv=mv: E["vector"].tensor_scalar(
                        self.pb[:, vs + 65:vs + 129], pst[:, 64:128], mv, None, ALU.mult))
                self.free_bank(tb, "dv")
                self.wait("gpsimd", "dv", dvs)
                for ver in range(2):
                    vs = 2048 + 520 * (2 * sl + ver)
                    rbase = 8192 * ver + 1024 * hp + 128 * tg
                    self.dma("gq", self.ag1v_i[rbase:rbase + 128, :],
                             self.pb[:, vs:vs + 130])
        self.wait("gpsimd", "gq")
        self.collective("cc1", self.ag1k_i, self.ag1k_o)
        self.collective("cc2", self.ag1v_i, self.ag1v_o)

    def attn_qg(self, qg, kind):
        j = qg
        scale = self.kconst(3) if kind == "a" else self.kconst(4)
        nch_half = (j + 1) if kind == "a" else 4 * (j + 1)
        ext = 128 * nch_half
        for hp in range(8):
            kE = self.cC[:, 0:ext]
            kO = self.cC[:, 4096:4096 + ext]
            self.wait("sync", "pe")
            ksrc = self.ag1k_o if kind == "a" else self.ag3k_o
            self.dma("sq", kE, ksrc[128 * hp:128 * hp + 128, 0:ext])
            l2 = self.dma("sq", kO, ksrc[1024 + 128 * hp:1024 + 128 * hp + 128, 0:ext])
            vsrc = self.ag1v_o if kind == "a" else self.ag3v_o
            vre = vsrc.rearrange("(x p) c -> p x c", p=128)
            blk = 8 if kind == "a" else 32
            base_e = blk * hp
            base_o = (64 if kind == "a" else 256) + blk * hp
            vE_c0, vO_c0 = 0, nch_half * 130
            self.wait("scalar", "pe")
            self.dma("aq", self.cA[:, vE_c0:vE_c0 + nch_half * 130],
                     vre[:, base_e:base_e + nch_half, :])
            l4 = self.dma("aq", self.cA[:, vO_c0:vO_c0 + nch_half * 130],
                          vre[:, base_o:base_o + nch_half, :])
            self.wait("tensor", "sq", l2)
            self.wait("tensor", "aq", l4)
            chunks = []
            if kind == "a":
                chunks.append(("null", 0, None, 0))
            for half in range(2):
                kb = kE if half == 0 else kO
                vc0 = vE_c0 if half == 0 else vO_c0
                for ci in range(nch_half):
                    if kind == "a":
                        mask = (1 + half) if ci == nch_half - 1 else None
                    else:
                        mask = (half * 4 + (ci - (nch_half - 4))) if ci >= nch_half - 4 else None
                    chunks.append((kb, 128 * ci, vc0 + 130 * ci, mask))
            qcolb = 12288 + 512 * hp
            self.use_bank("p4"); self.use_bank("p5")
            oh, ohp = self.psb["p4"], self.psb["p5"]
            nch = len(chunks)
            for ci, (kb, kcol, vcol, mask) in enumerate(chunks):
                sb_name = ("p0", "p1") if ci % 2 == 0 else ("p2", "p3")
                self.use_bank(sb_name[0]); self.use_bank(sb_name[1])
                s_h, s_hp = self.psb[sb_name[0]], self.psb[sb_name[1]]
                if kb == "null":
                    klhs0 = self.nKT[0:64, 128 * hp:128 * hp + 128]
                    klhs1 = self.nKT[64:128, 128 * hp:128 * hp + 128]
                else:
                    klhs0 = kb[0:64, kcol:kcol + 128]
                    klhs1 = kb[64:128, kcol:kcol + 128]
                self.opq("tensor", lambda E, klhs0=klhs0, s_h=s_h, qcolb=qcolb: E["tensor"].matmul(
                    s_h[:, :], klhs0, self.cB[0:64, qcolb:qcolb + 512],
                    start=True, stop=True, tile_position=(0, 0)))
                mm = self.op("tensor", "pe", lambda E, klhs1=klhs1, s_hp=s_hp, qcolb=qcolb: E["tensor"].matmul(
                    s_hp[:, :], klhs1, self.cB[64:128, qcolb:qcolb + 512],
                    start=True, stop=True, tile_position=(64, 0)))
                pcol = 1024 * (ci % 4)
                self.wait("scalar", "pe", mm)
                msk = self.am_sb if kind == "a" else self.bm_sb
                for which, sps in ((0, s_h), (1, s_hp)):
                    pt = self.pb[:, pcol + 512 * which:pcol + 512 * which + 512]
                    if mask is not None:
                        sf = self.lnf[:, 512 * which:512 * which + 512]
                        self.wait("vector", "pe", mm)
                        self.wait("vector", "ac")
                        dv = self.op("vector", "dv", lambda E, sps=sps, sf=sf, mask=mask, msk=msk: E["vector"].tensor_tensor(
                            sf, sps[:, :], msk[:, 512 * mask:512 * mask + 512], ALU.add))
                        self.wait("scalar", "dv", dv)
                        self.op("scalar", "ac", lambda E, sf=sf, pt=pt, scale=scale: E["scalar"].activation(
                            out=pt, in_=sf, func=AF.Exp, scale=scale))
                    else:
                        self.op("scalar", "ac", lambda E, sps=sps, pt=pt, scale=scale: E["scalar"].activation(
                            out=pt, in_=sps[:, :], func=AF.Exp, scale=scale))
                expd = self.c["ac"]
                self.free_bank(sb_name[0], "ac"); self.free_bank(sb_name[1], "ac")
                if kb == "null":
                    nb = 16384 - 1040 + 130 * hp
                    vl0 = self.cA[:, nb:nb + 65]
                    vl1 = self.cA[:, nb + 65:nb + 130]
                else:
                    vl0 = self.cA[:, vcol:vcol + 65]
                    vl1 = self.cA[:, vcol + 65:vcol + 130]
                self.wait("tensor", "ac", expd)
                first = ci == 0
                last = ci == nch - 1
                self.opq("tensor", lambda E, vl0=vl0, oh=oh, pcol=pcol, first=first, last=last: E["tensor"].matmul(
                    oh[0:65, :], vl0, self.pb[:, pcol:pcol + 512], start=first, stop=last))
                self.op("tensor", "pe", lambda E, vl1=vl1, ohp=ohp, pcol=pcol, first=first, last=last: E["tensor"].matmul(
                    ohp[0:65, :], vl1, self.pb[:, pcol + 512:pcol + 1024], start=first, stop=last))
            av_done = self.c["pe"]
            self.wait("vector", "pe", av_done)
            rlh = self.lnf[64:65, 2048:2560]
            rlhp = self.lnf[64:65, 2560:3072]
            self.op("vector", "dv", lambda E, oh=oh, rlh=rlh: E["vector"].reciprocal(rlh, oh[64:65, :]))
            rl_done = self.op("vector", "dv", lambda E, ohp=ohp, rlhp=rlhp: E["vector"].reciprocal(rlhp, ohp[64:65, :]))
            self.wait("tensor", "dv", rl_done)
            self.use_bank("p0")
            rb = self.psb["p0"]
            self.opq("tensor", lambda E, rb=rb, rlh=rlh: E["tensor"].matmul(
                rb[0:64, :], self.ones_f[64:65, 0:64], rlh,
                start=True, stop=True, tile_position=(64, 0)))
            bc = self.op("tensor", "pe", lambda E, rb=rb, rlhp=rlhp: E["tensor"].matmul(
                rb[64:128, :], self.ones_f[64:65, 0:64], rlhp,
                start=True, stop=True, tile_position=(64, 64)))
            self.wait("scalar", "pe", av_done)
            ob_h = self.lnf[0:64, 1024:1536]
            ob_t = self.lnf[0:64, 1536:2048]
            self.wait("scalar", "dv")
            self.wait("scalar", "sq")
            self.op("scalar", "ac", lambda E, oh=oh, ob_h=ob_h: E["scalar"].copy(ob_h, oh[0:64, :]))
            a2 = self.op("scalar", "ac", lambda E, ohp=ohp, ob_t=ob_t: E["scalar"].copy(ob_t, ohp[0:64, :]))
            self.free_bank("p4", "ac"); self.free_bank("p5", "ac")
            self.wait("sync", "ac", a2)
            mv = self.dma("sq", self.lnf[64:128, 1024:1536], ob_t)
            self.wait("vector", "sq", mv)
            self.wait("vector", "pe", bc)
            self.op("vector", "dv", lambda E, rb=rb, hp=hp: E["vector"].tensor_tensor(
                self.cB[:, 8192 + 512 * hp:8192 + 512 * hp + 512],
                self.lnf[:, 1024:1536], rb[:, :], ALU.mult))
            self.free_bank("p0", "dv")
        if self.debug_dump == "AOB_" + kind.upper():
            self.wait("gpsimd", "dv")
            self.dma("gq", self.dbg[128 * qg:128 * qg + 128, 0:4096],
                     self.cB[:, 8192:12288])
        return self.c["dv"]

    def ph_a2(self):
        self.wait("sync", "cc1", self.c["cc1"])
        self.wait("scalar", "cc2", self.c["cc2"])
        self.wait("sync", "gq", self.cast_wout)
        self.build_nvt(self.cA, 16384 - 1040)
        self.wait("gpsimd", "dv")
        self._nvt_rows(self.cA, 16384 - 1040)
        self.wait("tensor", "gq")
        self.wait("vector", "gq")
        wslot = [self.cC[:, 6144:7168], self.cC[:, 7168:8192]]
        wo_src = self.wbf_out.rearrange("(c p) o -> p c o", p=128)
        cd_src = self.cdsd.rearrange("(c p) o -> p c o", p=128)
        for qg in range(NQG):
            for hp in range(8):
                self.wait("sync", "pe")
                lq = self.dma("sq", self.cB[:, 12288 + 512 * hp:12288 + 512 * hp + 512],
                              self.qscr[128 * hp:128 * hp + 128, 512 * qg:512 * qg + 512])
            self.wait("tensor", "sq", lq)
            aob_done = self.attn_qg(qg, "a")
            self.wait("tensor", "dv", aob_done)
            self.wait("sync", "pe")
            ld = {0: self.dma("sq", wslot[0], wo_src[:, :, 0:128])}
            pe_ob = {}
            for i in range(8):
                if i + 1 < 8:
                    if i >= 1:
                        self.wait("sync", "pe", pe_ob.get(i - 1, 0))
                    ld[i + 1] = self.dma("sq", wslot[(i + 1) % 2],
                                         wo_src[:, :, 128 * (i + 1):128 * (i + 2)])
                self.wait("tensor", "sq", ld[i])
                bank = "p0" if i % 2 == 0 else "p1"
                self.use_bank(bank)
                ps = self.psb[bank]
                w = wslot[i % 2]
                for cc in range(8):
                    ff = (cc == 0); lst = (cc == 7)
                    if lst:
                        pe_ob[i] = self.op("tensor", "pe", lambda E, w=w, cc=cc, ps=ps: E["tensor"].matmul(
                            ps[:, :], w[:, 128 * cc:128 * cc + 128],
                            self.cB[:, 8192 + 512 * cc:8192 + 512 * cc + 512],
                            start=False, stop=True))
                    else:
                        self.opq("tensor", lambda E, w=w, cc=cc, ps=ps, ff=ff: E["tensor"].matmul(
                            ps[:, :], w[:, 128 * cc:128 * cc + 128],
                            self.cB[:, 8192 + 512 * cc:8192 + 512 * cc + 512],
                            start=ff, stop=False))
                self.wait("vector", "pe", pe_ob[i])
                bias = self.scal[:, self.BOUT + i:self.BOUT + i + 1]
                self.op("vector", "dv", lambda E, ps=ps, i=i, bias=bias: E["vector"].tensor_scalar(
                    self.yb[:, 512 * i:512 * i + 512], ps[:, :], bias, None, ALU.add))
                self.free_bank(bank, "dv")
            o_done = self.c["dv"]
            self.wait("tensor", "dv", o_done)
            for plane in range(2):
                self.wait("sync", "pe")
                ld2 = {0: self.dma("sq", wslot[0], cd_src[:, :, 1024 * plane:1024 * plane + 128])}
                pe_kb = {}
                for i in range(8):
                    if i + 1 < 8:
                        if i >= 1:
                            self.wait("sync", "pe", pe_kb.get(i - 1, 0))
                        ld2[i + 1] = self.dma("sq", wslot[(i + 1) % 2],
                                              cd_src[:, :, 1024 * plane + 128 * (i + 1):1024 * plane + 128 * (i + 2)])
                    self.wait("tensor", "sq", ld2[i])
                    bank = "p0" if i % 2 == 0 else "p1"
                    self.use_bank(bank)
                    ps = self.psb[bank]
                    w = wslot[i % 2]
                    for cc in range(8):
                        ff = (cc == 0); lst = (cc == 7)
                        if lst:
                            pe_kb[i] = self.op("tensor", "pe", lambda E, w=w, cc=cc, ps=ps: E["tensor"].matmul(
                                ps[:, :], w[:, 128 * cc:128 * cc + 128],
                                self.yb[:, 512 * cc:512 * cc + 512], start=False, stop=True))
                        else:
                            self.opq("tensor", lambda E, w=w, cc=cc, ps=ps, ff=ff: E["tensor"].matmul(
                                ps[:, :], w[:, 128 * cc:128 * cc + 128],
                                self.yb[:, 512 * cc:512 * cc + 512], start=ff, stop=False))
                    self.wait("vector", "pe", pe_kb[i])
                    zf = self.pb[:, 512 * (i % 2):512 * (i % 2) + 512]
                    self.wait("vector", "pe")
                    ev = self.op("vector", "dv", lambda E, ps=ps, zf=zf: E["vector"].tensor_copy(zf, ps[:, :]))
                    self.free_bank(bank, "dv")
                    for ts in range(4):
                        tbk = "ta" if (i * 4 + ts) % 2 == 0 else "tb"
                        self.use_bank(tbk)
                        pst = self.psb[tbk]
                        self.wait("tensor", "dv", ev)
                        td = self.op("tensor", "pe", lambda E, zf=zf, ts=ts, pst=pst: E["tensor"].transpose(
                            pst[:, 0:128], zf[:, 128 * ts:128 * ts + 128], self.eyeb_sb[:, :]))
                        self.wait("vector", "pe", td)
                        self.wait("vector", "gq")
                        for ver, mv in ((0, self.m0()), (1, self.m1())):
                            dst = self.cB[:, 4096 * ver + 1024 * ts + 128 * i:
                                          4096 * ver + 1024 * ts + 128 * i + 128]
                            self.op("vector", "dv", lambda E, pst=pst, dst=dst, mv=mv: E["vector"].tensor_scalar(
                                dst, pst[:, 0:128], mv, None, ALU.mult))
                        self.free_bank(tbk, "dv")
                z_done = self.c["dv"]
                self.wait("gpsimd", "dv", z_done)
                for ver in range(2):
                    for ts in range(4):
                        rbase = 8192 * ver + 4096 * plane + 512 * qg + 128 * ts
                        self.dma("gq", self.ag2_i[rbase:rbase + 128, :],
                                 self.cB[:, 4096 * ver + 1024 * ts:4096 * ver + 1024 * ts + 1024])
        self.wait("gpsimd", "gq")
        self.collective("cc3", self.ag2_i, self.ag2_o)

    def ph_fft(self):
        self.wait("sync", "cc3", self.c["cc3"])
        ag2 = self.ag2_o.rearrange("(r z l t) k -> r z l t k", r=2, z=2, l=32, t=128)
        y1 = self.y1scr.rearrange("(z k t) d -> z k t d", z=2, k=64, t=128)
        abuf = [self.cA[:, 0:4096], self.cA[:, 4096:8192]]
        stg = [self.cA[:, 8192:12288], self.cA[:, 12288:16384]]
        ld_it, pe_it = {}, {}
        def loadA(i):
            t2g, dkc = divmod(i, 2)
            slot = abuf[i % 2]
            for piece in range(4):
                r, z = piece % 2, piece // 2
                src = ag2[r, z, :, 8 * t2g:8 * t2g + 8, 512 * dkc:512 * dkc + 512]
                self.dma("sq", slot[32 * piece:32 * piece + 32, :], src)
            return self.c["sq"]
        ld_it[0] = loadA(0)
        NIT = 32
        for i in range(NIT):
            t2g, dkc = divmod(i, 2)
            if i + 1 < NIT:
                if i >= 1:
                    self.wait("sync", "pe", pe_it.get(i - 1, 0))
                ld_it[i + 1] = loadA(i + 1)
            self.wait("tensor", "sq", ld_it[i])
            slot = abuf[i % 2]
            sbank = ("p0", "p1") if i % 2 == 0 else ("p2", "p3")
            stage = stg[i % 2]
            self.wait("vector", "gq")
            for t2i in range(8):
                self.use_bank(sbank[0]); self.use_bank(sbank[1])
                pr, pi = self.psb[sbank[0]], self.psb[sbank[1]]
                rhs = slot[:, 512 * t2i:512 * t2i + 512]
                self.opq("tensor", lambda E, rhs=rhs, pr=pr: E["tensor"].matmul(
                    pr[0:64, :], self.stA_sb[:, 0:64], rhs, start=True, stop=True))
                mm = self.op("tensor", "pe", lambda E, rhs=rhs, pi=pi: E["tensor"].matmul(
                    pi[0:64, :], self.stA_sb[:, 64:128], rhs, start=True, stop=True))
                self.wait("vector", "pe", mm)
                self.op("vector", "dv", lambda E, pr=pr, stage=stage, t2i=t2i: E["vector"].tensor_copy(
                    stage[0:64, 512 * t2i:512 * t2i + 512], pr[0:64, :]))
                self.op("vector", "dv", lambda E, pi=pi, stage=stage, t2i=t2i: E["vector"].tensor_copy(
                    stage[64:128, 512 * t2i:512 * t2i + 512], pi[0:64, :]))
                self.free_bank(sbank[0], "dv"); self.free_bank(sbank[1], "dv")
            pe_it[i] = self.c["pe"]
            self.wait("gpsimd", "dv")
            for plane in range(2):
                dst = y1[plane, :, 8 * t2g:8 * t2g + 8, 512 * dkc:512 * dkc + 512]
                self.dma("gq", dst, stage[64 * plane:64 * plane + 64, :])
        self.wait("sync", "gq")
        self.wait("vector", "gq")
        yt = [self.cC[:, 0:2048], self.cC[:, 2048:4096]]
        ldc = {}
        def loadC(k1):
            slot = yt[k1 % 2]
            self.dma("sq", slot[:, 0:1024], y1[0, k1, :, :])
            self.dma("sq", slot[:, 1024:2048], y1[1, k1, :, :])
            return self.c["sq"]
        ldc[0] = loadC(0)
        pe_k1 = {}
        for k1 in range(64):
            if k1 + 1 < 64:
                if k1 >= 1:
                    self.wait("sync", "pe", pe_k1.get(k1 - 1, 0))
                ldc[k1 + 1] = loadC(k1 + 1)
            self.wait("tensor", "sq", ldc[k1])
            slot = yt[k1 % 2]
            for dc in range(8):
                bank = "p0" if dc % 2 == 0 else "p1"
                self.use_bank(bank)
                ps = self.psb[bank]
                lc = self.stC_sb[:, 128 * k1:128 * k1 + 64]
                ls = self.stC_sb[:, 128 * k1 + 64:128 * k1 + 128]
                self.opq("tensor", lambda E, lc=lc, ps=ps, slot=slot, dc=dc: E["tensor"].matmul(
                    ps[0:64, 0:128], lc, slot[:, 128 * dc:128 * dc + 128],
                    start=True, stop=False))
                mm = self.op("tensor", "pe", lambda E, ls=ls, ps=ps, slot=slot, dc=dc: E["tensor"].matmul(
                    ps[0:64, 0:128], ls, slot[:, 1024 + 128 * dc:1024 + 128 * dc + 128],
                    start=False, stop=True))
                ev_slot = self.lnf[0:64, 512 * (dc % 2):512 * (dc % 2) + 128]
                self.wait("vector", "pe", mm)
                ev = self.op("vector", "dv", lambda E, ps=ps, ev_slot=ev_slot: E["vector"].tensor_copy(
                    ev_slot, ps[0:64, 0:128]))
                self.free_bank(bank, "dv")
                tbk = "p2" if dc % 2 == 0 else "p3"
                self.use_bank(tbk)
                pst = self.psb[tbk]
                self.wait("tensor", "dv", ev)
                td = self.op("tensor", "pe", lambda E, ev_slot=ev_slot, pst=pst: E["tensor"].transpose(
                    pst[:, 0:64], ev_slot, self.eyef_sb[0:64, 0:64]))
                self.wait("vector", "pe", td)
                arena = self.cA if dc < 4 else self.cB
                rt3 = arena[:, 4096 * (dc % 4):4096 * (dc % 4) + 4096].rearrange(
                    "p (l b k) -> p l b k", l=32, b=2, k=64)
                dst = rt3[:, :, :, k1]
                self.op("vector", "dv", lambda E, pst=pst, dst=dst: E["vector"].tensor_copy(dst, pst[:, 0:64]))
                self.free_bank(tbk, "dv")
            pe_k1[k1] = self.c["pe"]
        rt_done = self.c["dv"]
        self.wait("vector", "dv", rt_done)
        for dc in range(8):
            lx = self.dma("sq", self.xk[:, :], self.x0f[128 * dc:128 * dc + 128, :])
            self.wait("vector", "sq", lx)
            arena = self.cA if dc < 4 else self.cB
            rt = arena[:, 4096 * (dc % 4):4096 * (dc % 4) + 4096]
            self.wait("vector", "gq")
            wb = self.op("vector", "dv", lambda E, rt=rt: E["vector"].tensor_tensor(
                self.lnf[:, :], self.xk[:, :], rt, ALU.add))
            self.wait("gpsimd", "dv", wb)
            self.dma("gq", self.xres[128 * dc:128 * dc + 128, :], self.lnf[:, :])
            self.wait("sync", "dv", wb)

    def ph_ffn(self, dst_dram):
        self.wait("sync", "gq", self.cast_ffn)
        w1s = self.w1bf.rearrange("(c p) o -> p c o", p=128)
        w2s = self.w2bf.rearrange("(c p) o -> p c o", p=128)
        h1s = self.h1bf.rearrange("(c p) o -> p c o", p=128)
        h2s = self.h2bf.rearrange("(c p) o -> p c o", p=128)
        w3s = self.w3bf.rearrange("(c p) o -> p c o", p=128)
        wA = [self.cC[:, 0:4096], self.cC[:, 4096:8192]]
        wB = [self.cD[:, 0:4096], self.cD[:, 4096:8192]]
        for tg in range(NQG):
            y_done = self.ln_tg(self.xres, tg)
            self.wait("tensor", "dv", y_done)
            pe_u = {}
            self.wait("sync", "pe")
            self.wait("scalar", "pe")
            ldu = {0: (self.dma("sq", wA[0][:, 0:1024], w1s[:, :, 0:128]),
                       self.dma("aq", wB[0][:, 0:1024], w2s[:, :, 0:128]))}
            for i in range(32):
                if i + 1 < 32:
                    if i >= 1:
                        self.wait("sync", "pe", pe_u.get(i - 1, 0))
                        self.wait("scalar", "pe", pe_u.get(i - 1, 0))
                    a = self.dma("sq", wA[(i + 1) % 2][:, 0:1024],
                                 w1s[:, :, 128 * (i + 1):128 * (i + 2)])
                    b = self.dma("aq", wB[(i + 1) % 2][:, 0:1024],
                                 w2s[:, :, 128 * (i + 1):128 * (i + 2)])
                    ldu[i + 1] = (a, b)
                self.wait("tensor", "sq", ldu[i][0])
                self.wait("tensor", "aq", ldu[i][1])
                b1 = "p0" if i % 2 == 0 else "p2"
                b2 = "p1" if i % 2 == 0 else "p3"
                self.use_bank(b1); self.use_bank(b2)
                u1, u2 = self.psb[b1], self.psb[b2]
                for cc in range(8):
                    self.opq("tensor", lambda E, i=i, cc=cc, u1=u1: E["tensor"].matmul(
                        u1[:, :], wA[i % 2][:, 128 * cc:128 * cc + 128],
                        self.yb[:, 512 * cc:512 * cc + 512],
                        start=(cc == 0), stop=(cc == 7)))
                for cc in range(8):
                    ff = (cc == 0); lst = (cc == 7)
                    if lst:
                        pe_u[i] = self.op("tensor", "pe", lambda E, i=i, cc=cc, u2=u2: E["tensor"].matmul(
                            u2[:, :], wB[i % 2][:, 128 * cc:128 * cc + 128],
                            self.yb[:, 512 * cc:512 * cc + 512], start=False, stop=True))
                    else:
                        self.opq("tensor", lambda E, i=i, cc=cc, u2=u2, ff=ff: E["tensor"].matmul(
                            u2[:, :], wB[i % 2][:, 128 * cc:128 * cc + 128],
                            self.yb[:, 512 * cc:512 * cc + 512], start=ff, stop=False))
                self.wait("scalar", "pe", pe_u[i])
                self.wait("vector", "pe", pe_u[i])
                sa = self.pb[:, 512 * (i % 2):512 * (i % 2) + 512]
                sbf = self.pb[:, 1024 + 512 * (i % 2):1536 + 512 * (i % 2)]
                s1c = self.scal[:, self.S1 + i:self.S1 + i + 1]
                s2c = self.scal[:, self.S2 + i:self.S2 + i + 1]
                self.wait("scalar", "dv")
                ac1 = self.op("scalar", "ac", lambda E, u1=u1, sa=sa, s1c=s1c: E["scalar"].activation(
                    out=sa, in_=u1[:, :], func=AF.Silu, scale=s1c))
                self.free_bank(b1, "ac")
                self.op("vector", "dv", lambda E, u2=u2, sbf=sbf, s2c=s2c: E["vector"].tensor_scalar(
                    sbf, u2[:, :], s2c, None, ALU.mult))
                self.free_bank(b2, "dv")
                self.wait("vector", "ac", ac1)
                self.op("vector", "dv", lambda E, sa=sa, sbf=sbf, i=i: E["vector"].tensor_tensor(
                    self.cA[:, 512 * i:512 * i + 512], sa, sbf, ALU.mult))
            g_done = self.c["dv"]
            self.wait("tensor", "dv", g_done)
            pe_h = {}
            self.wait("sync", "pe")
            self.wait("scalar", "pe")
            ldh = {0: ("sq", self.dma("sq", wA[0], h1s[:, :, 0:128]))}
            for i in range(32):
                if i + 1 < 32:
                    q = "sq" if (i + 1) % 2 == 0 else "aq"
                    if i >= 1:
                        self.wait(self.QENG[q], "pe", pe_h.get(i - 1, 0))
                    ldh[i + 1] = (q, self.dma(q, wA[(i + 1) % 2],
                                              h1s[:, :, 128 * (i + 1):128 * (i + 2)]))
                self.wait("tensor", ldh[i][0], ldh[i][1])
                bank = ("p0", "p1", "p2", "p3")[i % 4]
                self.use_bank(bank)
                ps = self.psb[bank]
                for cc in range(32):
                    ff = (cc == 0); lst = (cc == 31)
                    if lst:
                        pe_h[i] = self.op("tensor", "pe", lambda E, i=i, cc=cc, ps=ps: E["tensor"].matmul(
                            ps[:, :], wA[i % 2][:, 128 * cc:128 * cc + 128],
                            self.cA[:, 512 * cc:512 * cc + 512], start=False, stop=True))
                    else:
                        self.opq("tensor", lambda E, i=i, cc=cc, ps=ps, ff=ff: E["tensor"].matmul(
                            ps[:, :], wA[i % 2][:, 128 * cc:128 * cc + 128],
                            self.cA[:, 512 * cc:512 * cc + 512], start=ff, stop=False))
                self.wait("scalar", "pe", pe_h[i])
                shc = self.scal[:, self.SH1 + i:self.SH1 + i + 1]
                self.op("scalar", "ac", lambda E, ps=ps, i=i, shc=shc: E["scalar"].activation(
                    out=self.cB[:, 512 * i:512 * i + 512], in_=ps[:, :],
                    func=AF.Gelu, scale=shc))
                self.free_bank(bank, "ac")
            g2_done = self.c["ac"]
            self.wait("tensor", "ac", g2_done)
            pe_h2 = {}
            self.wait("sync", "pe")
            self.wait("scalar", "pe")
            ldh2 = {0: ("sq", self.dma("sq", wA[0], h2s[:, :, 0:128]))}
            for i in range(32):
                if i + 1 < 32:
                    q = "sq" if (i + 1) % 2 == 0 else "aq"
                    if i >= 1:
                        self.wait(self.QENG[q], "pe", pe_h2.get(i - 1, 0))
                    ldh2[i + 1] = (q, self.dma(q, wA[(i + 1) % 2],
                                               h2s[:, :, 128 * (i + 1):128 * (i + 2)]))
                self.wait("tensor", ldh2[i][0], ldh2[i][1])
                bank = ("p0", "p1", "p2", "p3")[i % 4]
                self.use_bank(bank)
                ps = self.psb[bank]
                for cc in range(32):
                    ff = (cc == 0); lst = (cc == 31)
                    if lst:
                        pe_h2[i] = self.op("tensor", "pe", lambda E, i=i, cc=cc, ps=ps: E["tensor"].matmul(
                            ps[:, :], wA[i % 2][:, 128 * cc:128 * cc + 128],
                            self.cB[:, 512 * cc:512 * cc + 512], start=False, stop=True))
                    else:
                        self.opq("tensor", lambda E, i=i, cc=cc, ps=ps, ff=ff: E["tensor"].matmul(
                            ps[:, :], wA[i % 2][:, 128 * cc:128 * cc + 128],
                            self.cB[:, 512 * cc:512 * cc + 512], start=ff, stop=False))
                self.wait("scalar", "pe", pe_h2[i])
                shc = self.scal[:, self.SH2 + i:self.SH2 + i + 1]
                self.op("scalar", "ac", lambda E, ps=ps, i=i, shc=shc: E["scalar"].activation(
                    out=self.cA[:, 512 * i:512 * i + 512], in_=ps[:, :],
                    func=AF.Gelu, scale=shc))
                self.free_bank(bank, "ac")
            g3_done = self.c["ac"]
            self.wait("tensor", "ac", g3_done)
            pe_w3 = {}
            self.wait("sync", "pe")
            ldw3 = {0: self.dma("sq", wA[0], w3s[:, :, 0:128])}
            for i in range(8):
                if i + 1 < 8:
                    if i >= 1:
                        self.wait("sync", "pe", pe_w3.get(i - 1, 0))
                    ldw3[i + 1] = self.dma("sq", wA[(i + 1) % 2],
                                           w3s[:, :, 128 * (i + 1):128 * (i + 2)])
                self.wait("tensor", "sq", ldw3[i])
                bank = "p2" if i % 2 == 0 else "p3"
                self.use_bank(bank)
                ps = self.psb[bank]
                for cc in range(32):
                    ff = (cc == 0); lst = (cc == 31)
                    if lst:
                        pe_w3[i] = self.op("tensor", "pe", lambda E, i=i, cc=cc, ps=ps: E["tensor"].matmul(
                            ps[:, :], wA[i % 2][:, 128 * cc:128 * cc + 128],
                            self.cA[:, 512 * cc:512 * cc + 512], start=False, stop=True))
                    else:
                        self.opq("tensor", lambda E, i=i, cc=cc, ps=ps, ff=ff: E["tensor"].matmul(
                            ps[:, :], wA[i % 2][:, 128 * cc:128 * cc + 128],
                            self.cA[:, 512 * cc:512 * cc + 512], start=ff, stop=False))
                self.wait("vector", "pe", pe_w3[i])
                s3c = self.scal[:, self.S3 + i:self.S3 + i + 1]
                stgf = self.lnf[:, 512 * (i % 2):512 * (i % 2) + 512]
                self.wait("vector", "gq")
                wb = self.op("vector", "dv", lambda E, ps=ps, s3c=s3c, stgf=stgf, i=i: E["vector"].scalar_tensor_tensor(
                    stgf, ps[:, :], s3c, self.xk[:, 512 * i:512 * i + 512],
                    ALU.mult, ALU.add))
                self.free_bank(bank, "dv")
                self.wait("gpsimd", "dv", wb)
                self.dma("gq", dst_dram[128 * i:128 * i + 128, 512 * tg:512 * tg + 512], stgf)

    def ph_b_proj(self):
        self.wait("sync", "gq", self.cast_b)
        wsrc = self.wbf_b.rearrange("(c p) o -> p c o", p=128)
        wslot = [self.cC[:, 0:1024], self.cC[:, 1024:2048]]
        self.wait("vector", "sq", self.consts_ld)
        for sl in range(2):
            for ver in range(2):
                vs = 2048 + 2080 * sl + 520 * ver
                mv = self.m0() if ver == 0 else self.m1()
                for ts in range(4):
                    self.op("vector", "dv", lambda E, vs=vs, ts=ts, mv=mv: E["vector"].tensor_scalar(
                        self.cD[:, vs + 130 * ts + 64:vs + 130 * ts + 65],
                        self.ones_f[:, 0:1], mv, None, ALU.mult))
                    self.op("vector", "dv", lambda E, vs=vs, ts=ts, mv=mv: E["vector"].tensor_scalar(
                        self.cD[:, vs + 130 * ts + 129:vs + 130 * ts + 130],
                        self.ones_f[:, 0:1], mv, None, ALU.mult))
        for tg in range(NQG):
            y_done = self.ln_tg(self.xres, tg)
            self.wait("tensor", "dv", y_done)
            pe_i = {}
            self.wait("sync", "pe")
            ld = {0: self.dma("sq", wslot[0], wsrc[:, :, 0:128])}
            for i in range(24):
                if i + 1 < 24:
                    if i >= 1:
                        self.wait("sync", "pe", pe_i.get(i - 1, 0))
                    ld[i + 1] = self.dma("sq", wslot[(i + 1) % 2],
                                         wsrc[:, :, 128 * (i + 1):128 * (i + 2)])
                self.wait("tensor", "sq", ld[i])
                bank = "p0" if i % 2 == 0 else "p1"
                self.use_bank(bank)
                ps = self.psb[bank]
                w = wslot[i % 2]
                for cc in range(8):
                    ff = (cc == 0); lst = (cc == 7)
                    if lst:
                        pe_i[i] = self.op("tensor", "pe", lambda E, w=w, cc=cc, ps=ps: E["tensor"].matmul(
                            ps[:, :], w[:, 128 * cc:128 * cc + 128],
                            self.yb[:, 512 * cc:512 * cc + 512], start=False, stop=True))
                    else:
                        self.opq("tensor", lambda E, w=w, cc=cc, ps=ps, ff=ff: E["tensor"].matmul(
                            ps[:, :], w[:, 128 * cc:128 * cc + 128],
                            self.yb[:, 512 * cc:512 * cc + 512], start=ff, stop=False))
                self.wait("vector", "pe", pe_i[i])
                ob = i % 8
                if i < 8:
                    stg = self.pb[:, 512 * (i % 2):512 * (i % 2) + 512]
                    self.wait("vector", "gq")
                    ev = self.op("vector", "dv", lambda E, ps=ps, stg=stg: E["vector"].tensor_copy(stg, ps[:, :]))
                    self.free_bank(bank, "dv")
                    self.wait("gpsimd", "dv", ev)
                    self.dma("gq", self.qscr[128 * ob:128 * ob + 128, 512 * tg:512 * tg + 512], stg)
                elif i < 16:
                    self.wait("vector", "gq")
                    for ver, mv in ((0, self.m0()), (1, self.m1())):
                        stg = self.pb[:, 1024 + 1024 * ver + 512 * (i % 2):
                                      1024 + 1024 * ver + 512 * (i % 2) + 512]
                        dvs = self.op("vector", "dv", lambda E, ps=ps, stg=stg, mv=mv: E["vector"].tensor_scalar(
                            stg, ps[:, :], mv, None, ALU.mult))
                        self.wait("gpsimd", "dv", dvs)
                        self.dma("gq", self.ag3k_i[1024 * ver + 128 * ob:1024 * ver + 128 * ob + 128,
                                                   512 * tg:512 * tg + 512], stg)
                    self.free_bank(bank, "dv")
                else:
                    hp = ob
                    stg = self.pb[:, 3072 + 512 * (i % 2):3072 + 512 * (i % 2) + 512]
                    ev = self.op("vector", "dv", lambda E, ps=ps, stg=stg: E["vector"].tensor_copy(stg, ps[:, :]))
                    self.free_bank(bank, "dv")
                    sl = i % 2
                    self.wait("vector", "gq")
                    dvs = 0
                    for ts in range(4):
                        tbk = "ta" if ts % 2 == 0 else "tb"
                        self.use_bank(tbk)
                        pst = self.psb[tbk]
                        self.wait("tensor", "dv", ev)
                        td = self.op("tensor", "pe", lambda E, stg=stg, ts=ts, pst=pst: E["tensor"].transpose(
                            pst[:, 0:128], stg[:, 128 * ts:128 * ts + 128], self.eyeb_sb[:, :]))
                        self.wait("vector", "pe", td)
                        for ver, mv in ((0, self.m0()), (1, self.m1())):
                            vs = 2048 + 2080 * sl + 520 * ver
                            self.op("vector", "dv", lambda E, pst=pst, vs=vs, ts=ts, mv=mv: E["vector"].tensor_scalar(
                                self.cD[:, vs + 130 * ts + 0:vs + 130 * ts + 64],
                                pst[:, 0:64], mv, None, ALU.mult))
                            dvs = self.op("vector", "dv", lambda E, pst=pst, vs=vs, ts=ts, mv=mv: E["vector"].tensor_scalar(
                                self.cD[:, vs + 130 * ts + 65:vs + 130 * ts + 129],
                                pst[:, 64:128], mv, None, ALU.mult))
                        self.free_bank(tbk, "dv")
                    self.wait("gpsimd", "dv", dvs)
                    for ver in range(2):
                        vs = 2048 + 2080 * sl + 520 * ver
                        rbase = 32768 * ver + 4096 * hp + 512 * tg
                        dst = self.ag3v_i[rbase:rbase + 512, :].rearrange(
                            "(ts p) c -> p ts c", ts=4)
                        self.dma("gq", dst, self.cD[:, vs:vs + 520])
        self.wait("gpsimd", "gq")
        self.collective("cc4", self.ag3k_i, self.ag3k_o)
        self.collective("cc5", self.ag3v_i, self.ag3v_o)

    def ph_b_attn(self):
        self.wait("sync", "cc4", self.c["cc4"])
        self.wait("scalar", "cc5", self.c["cc5"])
        self.wait("sync", "gq", self.cast_b)
        wo_src = self.wbf_b.rearrange("(c p) o -> p c o", p=128)
        wslot = [self.cD[:, 0:1024], self.cD[:, 1024:2048]]
        for qg in range(NQG):
            for hp in range(8):
                self.wait("sync", "pe")
                lq = self.dma("sq", self.cB[:, 12288 + 512 * hp:12288 + 512 * hp + 512],
                              self.qscr[128 * hp:128 * hp + 128, 512 * qg:512 * qg + 512])
            self.wait("tensor", "sq", lq)
            aob_done = self.attn_qg(qg, "b")
            self.wait("tensor", "dv", aob_done)
            self.wait("sync", "pe")
            ld = {0: self.dma("sq", wslot[0], wo_src[:, :, 3072:3200])}
            pe_ob = {}
            for i in range(8):
                if i + 1 < 8:
                    if i >= 1:
                        self.wait("sync", "pe", pe_ob.get(i - 1, 0))
                    ld[i + 1] = self.dma("sq", wslot[(i + 1) % 2],
                                         wo_src[:, :, 3072 + 128 * (i + 1):3072 + 128 * (i + 2)])
                self.wait("tensor", "sq", ld[i])
                bank = "p1" if i % 2 == 0 else "p2"
                self.use_bank(bank)
                ps = self.psb[bank]
                w = wslot[i % 2]
                for cc in range(8):
                    ff = (cc == 0); lst = (cc == 7)
                    if lst:
                        pe_ob[i] = self.op("tensor", "pe", lambda E, w=w, cc=cc, ps=ps: E["tensor"].matmul(
                            ps[:, :], w[:, 128 * cc:128 * cc + 128],
                            self.cB[:, 8192 + 512 * cc:8192 + 512 * cc + 512],
                            start=False, stop=True))
                    else:
                        self.opq("tensor", lambda E, w=w, cc=cc, ps=ps, ff=ff: E["tensor"].matmul(
                            ps[:, :], w[:, 128 * cc:128 * cc + 128],
                            self.cB[:, 8192 + 512 * cc:8192 + 512 * cc + 512],
                            start=ff, stop=False))
                self.wait("scalar", "dv")
                lx = self.dma("aq", self.xk[:, 512 * i:512 * i + 512],
                              self.xres[128 * i:128 * i + 128, 512 * qg:512 * qg + 512])
                self.wait("vector", "pe", pe_ob[i])
                self.wait("vector", "aq", lx)
                stgf = self.lnf[:, 512 * (i % 2):512 * (i % 2) + 512]
                self.wait("vector", "gq")
                wb = self.op("vector", "dv", lambda E, ps=ps, stgf=stgf, i=i: E["vector"].tensor_tensor(
                    stgf, ps[:, :], self.xk[:, 512 * i:512 * i + 512], ALU.add))
                self.free_bank(bank, "dv")
                self.wait("gpsimd", "dv", wb)
                self.dma("gq", self.xres[128 * i:128 * i + 128, 512 * qg:512 * qg + 512], stgf)

    def ph_debug(self):
        name = self.debug_dump
        if not name or not hasattr(self, name):
            return
        src = getattr(self, name)
        rows, cols = src.tensor.shape
        rows = min(rows, 16384)
        cols = min(cols, 4096)
        self.barrier()
        for cc in ("cc1", "cc2", "cc3", "cc4", "cc5"):
            if self.c.get(cc, 0):
                self.wait("gpsimd", cc, self.c[cc])
        for base in range(0, rows, 128):
            self.dma("gq", self.dbg[base:base + 128, 0:cols], src[base:base + 128, 0:cols])
        self.barrier()

    def build(self):
        with contextlib.ExitStack() as stack:
            self._alloc(stack)
            pl = self.phase_limit
            self.src_ffn = self.xres
            self.ph_init()
            if pl in (104, 105):
                for dc in range(8):
                    self.dma("gq", self.xres[128 * dc:128 * dc + 128, :],
                             self.x0f[128 * dc:128 * dc + 128, :])
                self.barrier()
                if pl == 104:
                    self.ph_ffn(self.xres)
                else:
                    self.ph_b_proj()
                    self.ph_b_attn()
                self.ph_debug()
                self.barrier()
                self._emit_block()
                return self.nc
            if pl >= 1:
                self.ph_a1()
            if pl >= 2:
                self.ph_a2()
            if pl >= 3:
                self.ph_fft()
            if pl >= 4:
                self.barrier()
                self.ph_ffn(self.xres)
            if pl >= 5:
                self.barrier()
                self.ph_b_proj()
                self.ph_b_attn()
            if pl >= 6:
                self.barrier()
                self.ph_ffn(self.outT)
            self.ph_debug()
            self.barrier()
            self._emit_block()
        return self.nc

    def _emit_block(self):
        nc = self.nc
        E = {e: getattr(nc, e) for e in self.ENGS}
        with nc.Block() as block:
            @block.sync
            def _(sync):
                for f in self.prog["sync"]:
                    f(E)
            @block.gpsimd
            def _(gpsimd):
                for f in self.prog["gpsimd"]:
                    f(E)
            @block.scalar
            def _(scalar):
                for f in self.prog["scalar"]:
                    f(E)
            @block.vector
            def _(vector):
                for f in self.prog["vector"]:
                    f(E)
            @block.tensor
            def _(tensor):
                for f in self.prog["tensor"]:
                    f(E)



def _gidx(r):
    l = np.arange(ROWS) // 128
    off = np.arange(ROWS) % 128
    return 128 * (2 * l + r) + off


def _prep_shared(w):
    f = np.float32
    ct = np.ascontiguousarray
    d = {}
    d["wqkv"] = ct(np.asarray(w["L0_attn_a_Wqkv"], f))
    conv = np.asarray(w["L0_attn_a_conv_w"], f)
    cw = conv.reshape(16, 64, 64, 4)
    sbuf = np.zeros((128, 2048), f)
    for tau in range(4):
        for hp in range(8):
            blk = 64 * (4 * hp + tau)
            sbuf[0:64, blk:blk + 64] = cw[2 * hp, :, :, tau].T
            sbuf[64:128, blk:blk + 64] = cw[2 * hp + 1, :, :, tau].T
    d["convw_in"] = sbuf
    d["convb_in"] = ct(np.asarray(w["L0_attn_a_conv_b"], f).reshape(8, 128).T)
    d["nullk_in"] = ct(np.asarray(w["L0_attn_a_null_k"], f).reshape(8, 128).T)
    d["nullv_in"] = ct(np.asarray(w["L0_attn_a_null_v"], f).reshape(8, 128).T)
    d["wout"] = ct(np.asarray(w["L0_attn_a_W_out"], f))
    d["bout_in"] = ct(np.asarray(w["L0_attn_a_b_out"], f).reshape(8, 128).T)
    d["wqT"] = ct(np.asarray(w["L0_attn_b_Wq"], f).T)
    d["wkT"] = ct(np.asarray(w["L0_attn_b_Wk"], f).T)
    d["wvT"] = ct(np.asarray(w["L0_attn_b_Wv"], f).T)
    d["woT"] = ct(np.asarray(w["L0_attn_b_Wo"], f).T)
    d["w1t"] = ct(np.asarray(w["L0_ffn_w1_w"], f).T)
    d["w2t"] = ct(np.asarray(w["L0_ffn_w2_w"], f).T)
    d["h1t"] = ct(np.asarray(w["L0_ffn_h1_w"], f).T)
    d["h2t"] = ct(np.asarray(w["L0_ffn_h2_w"], f).T)
    d["w3t"] = ct(np.asarray(w["L0_ffn_w3_w"], f).T)
    d["s1_in"] = ct(np.asarray(w["L0_ffn_w1_s"], f).reshape(32, 128).T)
    d["s2_in"] = ct(np.asarray(w["L0_ffn_w2_s"], f).reshape(32, 128).T)
    d["sh1_in"] = ct(np.asarray(w["L0_ffn_h1_s"], f).reshape(32, 128).T)
    d["sh2_in"] = ct(np.asarray(w["L0_ffn_h2_s"], f).reshape(32, 128).T)
    d["s3_in"] = ct(np.asarray(w["L0_ffn_w3_s"], f).reshape(8, 128).T)
    n = np.arange(D_)
    CD = np.cos(2 * np.pi * np.outer(n, n) / D_)
    SD = np.sin(2 * np.pi * np.outer(n, n) / D_)
    d["cdsd"] = np.concatenate([CD, SD], 1).astype(BF)
    d["stA_in"] = _stepA_const()
    d["eyeb_in"] = np.eye(128, dtype=BF)
    d["eyef_in"] = np.eye(128, dtype=np.float32)
    d["ones_in"] = np.ones((128, 128), np.float32)
    return d


_CACHED = {}


def _get_nc():
    key = (PHASE_LIMIT, DEBUG_DUMP)
    if key not in _CACHED:
        _CACHED[key] = MK(phase_limit=PHASE_LIMIT, debug_dump=DEBUG_DUMP).build()
    return _CACHED[key]


def kernel(**inputs):
    shared = _prep_shared(inputs)
    x0 = np.asarray(inputs["L0_attn_a_x"], np.float32)
    percore = []
    for r in (0, 1):
        mvec = np.zeros((128, 8), np.float32)
        mvec[:, 0] = 1.0 if r == 0 else 0.0
        mvec[:, 1] = 0.0 if r == 0 else 1.0
        mvec[:, 2] = 1e-5
        mvec[:, 3] = 1.0 / 32.0
        mvec[:, 4] = 1.0 / 8.0
        mvec[:, 5] = -0.5
        percore.append({"stC_in": _stepC_const(r), "am_in": _maskA(r),
                        "bm_in": _maskB(r), "mvec_in": mvec})
    gidx = [_gidx(0), _gidx(1)]
    in_maps = []
    for c in range(8):
        b, r = c // 2, c % 2
        m = dict(shared)
        m.update(percore[r])
        m["x0f"] = np.ascontiguousarray(x0[b][gidx[r]].T)
        in_maps.append(m)
    nc = _get_nc()
    res = run_bass_kernel_spmd(nc, in_maps, core_ids=list(range(8)))
    out = np.zeros((B_, T_, D_), np.float32)
    for c in range(8):
        b, r = c // 2, c % 2
        out[b][gidx[r]] = res.results[c]["out"].T
    return out


# version 05
# speedup vs baseline: 1.0282x; 1.0181x over previous
import contextlib
import os
import numpy as np
import ml_dtypes

os.environ.setdefault("GAUGE_NTFF_IGNORE_DMA", "1")
import concourse.bass as bass
import concourse.mybir as mybir
from concourse.bass_utils import run_bass_kernel_spmd

F32 = mybir.dt.float32
BF16 = mybir.dt.bfloat16
AF = mybir.ActivationFunctionType
ALU = mybir.AluOpType
BF = ml_dtypes.bfloat16

B_, T_, D_, HEADS, CF, HID = 4, 8192, 1024, 16, 4, 4096
ROWS = T_ // 2
NL = ROWS // 128
NQG = ROWS // 512
N1 = T_ // 128
NEG = -1e30
GROUPS = [[0, 1], [2, 3], [4, 5], [6, 7]]

PHASE_LIMIT = 99
DEBUG_DUMP = None



def _perm():
    p = np.empty(N1, np.int64)
    p[:N1 // 2] = 2 * np.arange(N1 // 2)
    p[N1 // 2:] = 2 * np.arange(N1 // 2) + 1
    return p


def _stepA_const():
    perm = _perm()
    k1 = np.arange(N1)
    C = np.cos(2 * np.pi * perm[:, None] * k1[None, :] / N1)
    S = np.sin(2 * np.pi * perm[:, None] * k1[None, :] / N1)
    Ar = np.concatenate([C, -S], 0)
    Ai = np.concatenate([-S, -C], 0)
    return np.concatenate([Ar, Ai], 1).astype(BF)


def _stepC_const(r):
    t2 = np.arange(128)
    out = np.zeros((128, 64 * 128), np.float32)
    ll = np.arange(32)
    bb = np.arange(2)
    k2 = (4 * ll[:, None] + 2 * r + bb[None, :]).reshape(-1)
    for k1 in range(N1):
        th = 2 * np.pi * t2[:, None] * (k1 / T_ + k2[None, :] / 128.0)
        out[:, 128 * k1:128 * k1 + 64] = np.cos(th)
        out[:, 128 * k1 + 64:128 * k1 + 128] = np.sin(th)
    return out.astype(BF)


def _maskA(r):
    null = np.full((128, 512), NEG, np.float32)
    null[0, :] = 0.0
    bands = []
    for half in range(2):
        m = np.zeros((128, 512), np.float32)
        for a in range(4):
            ga = 2 * a + half
            for s in range(4):
                gq = 2 * s + r
                blk = m[32 * a:32 * a + 32, 128 * s:128 * s + 128]
                if ga < gq:
                    blk[:] = 0.0
                elif ga > gq:
                    blk[:] = NEG
                else:
                    c = np.arange(32)
                    i = np.arange(128)
                    blk[:] = np.where((4 * c[:, None] + 3) <= i[None, :], 0.0, NEG)
        bands.append(m)
    return np.concatenate([null] + bands, 1).astype(BF)


def _maskB(r):
    m = np.zeros((128, 8 * 512), np.float32)
    for half in range(2):
        for a in range(4):
            ga = 2 * a + half
            blk = m[:, (half * 4 + a) * 512:(half * 4 + a + 1) * 512]
            for s in range(4):
                gq = 2 * s + r
                sb = blk[:, 128 * s:128 * s + 128]
                if ga < gq:
                    sb[:] = 0.0
                elif ga > gq:
                    sb[:] = NEG
                else:
                    kk = np.arange(128)
                    i = np.arange(128)
                    sb[:] = np.where(kk[:, None] <= i[None, :], 0.0, NEG)
    return m.astype(BF)



class MK:
    ENGS = ("sync", "gpsimd", "scalar", "vector", "tensor")
    QENG = {"sq": "sync", "aq": "scalar", "gq": "gpsimd"}

    def __init__(self, phase_limit=99, debug_dump=None):
        self.phase_limit = phase_limit
        self.debug_dump = debug_dump
        self.nc = bass.Bass("TRN2", target_bir_lowering=False, debug=False)
        self.prog = {e: [] for e in self.ENGS}
        self.c = {}
        self.lastwait = {}
        self.bankfree = {}
        self._declare_dram()

    def emit(self, eng, fn):
        self.prog[eng].append(fn)

    def bump(self, sem, n):
        self.c[sem] = self.c.get(sem, 0) + n
        return self.c[sem]

    def dma(self, q, out, in_, **kw):
        eng = self.QENG[q]
        tgt = self.bump(q, 16)
        def f(E, out=out, in_=in_, kw=kw, eng=eng, q=q):
            E[eng].dma_start(out=out, in_=in_, **kw).then_inc(self.sem[q], 16)
        self.emit(eng, f)
        return tgt

    def wait(self, eng, semname, val=None):
        if val is None:
            val = self.c.get(semname, 0)
        if val <= 0:
            return
        key = (eng, semname)
        if self.lastwait.get(key, -1) >= val:
            return
        self.lastwait[key] = val
        def f(E, semname=semname, val=val, eng=eng):
            E[eng].wait_ge(self.sem[semname], val)
        self.emit(eng, f)

    def op(self, eng, semname, fn, n=1):
        tgt = self.bump(semname, n)
        def f(E, fn=fn, semname=semname, n=n):
            fn(E).then_inc(self.sem[semname], n)
        self.emit(eng, f)
        return tgt

    def opq(self, eng, fn):
        self.emit(eng, lambda E, fn=fn: fn(E))

    def barrier(self):
        for e in self.ENGS:
            for s in ("sq", "aq", "gq", "pe", "dv", "ac"):
                self.wait(e, s)

    def use_bank(self, bank):
        ent = self.bankfree.get(bank)
        if ent:
            self.wait("tensor", ent[0], ent[1])

    def free_bank(self, bank, sem):
        self.bankfree[bank] = (sem, self.c.get(sem, 0))

    def collective(self, sem, src, dst):
        self.bump(sem, 1)
        def f(E, sem=sem, src=src, dst=dst):
            E["gpsimd"].collective_compute(
                "AllReduce", ALU.add, replica_groups=GROUPS,
                ins=[src.opt()], outs=[dst.opt()]).then_inc(self.sem[sem])
        self.emit("gpsimd", f)

    def _declare_dram(self):
        nc = self.nc
        def di(name, shape, dt=F32):
            setattr(self, name, nc.dram_tensor(name, list(shape), dt,
                                               kind="ExternalInput").ap())
        def ds(name, shape, dt):
            setattr(self, name, nc.dram_tensor(name, list(shape), dt,
                                               kind="Internal").ap())
        di("x0f", (1024, 4096))
        di("wqkv", (1024, 3072))
        di("convw_in", (128, 2048))
        di("convb_in", (128, 8))
        di("nullk_in", (128, 8))
        di("nullv_in", (128, 8))
        di("wout", (1024, 1024))
        di("bout_in", (128, 8))
        di("wqT", (1024, 1024)); di("wkT", (1024, 1024))
        di("wvT", (1024, 1024)); di("woT", (1024, 1024))
        di("w1t", (1024, 4096)); di("w2t", (1024, 4096))
        di("h1t", (4096, 4096)); di("h2t", (4096, 4096))
        di("w3t", (4096, 1024))
        di("s1_in", (128, 32)); di("s2_in", (128, 32))
        di("sh1_in", (128, 32)); di("sh2_in", (128, 32))
        di("s3_in", (128, 8))
        di("mvec_in", (128, 8))
        di("cdsd", (1024, 2048), BF16)
        di("stA_in", (128, 128), BF16)
        di("stC_in", (128, 8192), BF16)
        di("am_in", (128, 1536), BF16)
        di("bm_in", (128, 4096), BF16)
        di("eyeb_in", (128, 128), BF16)
        di("eyef_in", (128, 128))
        di("ones_in", (128, 128))
        self.outT = nc.dram_tensor("out", [1024, 4096], F32,
                                   kind="ExternalOutput").ap()
        if self.debug_dump:
            self.dbg = nc.dram_tensor("dbg", [16384, 4096], F32,
                                      kind="ExternalOutput").ap()
        ds("xres", (1024, 4096), F32)
        ds("qscr", (1024, 4096), BF16)
        ds("wbf_qkv", (1024, 3072), BF16)
        ds("wbf_out", (1024, 1024), BF16)
        ds("wbf_b", (1024, 4096), BF16)
        ds("w1bf", (1024, 4096), BF16)
        ds("w2bf", (1024, 4096), BF16)
        ds("h1bf", (4096, 4096), BF16)
        ds("h2bf", (4096, 4096), BF16)
        ds("w3bf", (4096, 1024), BF16)
        ds("ag1k_i", (2048, 1024), BF16); ds("ag1k_o", (2048, 1024), BF16)
        ds("ag1v_i", (16384, 130), BF16); ds("ag1v_o", (16384, 130), BF16)
        ds("ag2_i", (16384, 1024), BF16); ds("ag2_o", (16384, 1024), BF16)
        ds("ag3k_i", (2048, 4096), BF16); ds("ag3k_o", (2048, 4096), BF16)
        ds("ag3v_i", (65536, 130), BF16); ds("ag3v_o", (65536, 130), BF16)
        ds("y1scr", (16384, 1024), BF16)

    def _alloc(self, stack):
        nc = self.nc
        def sb(name, shape, dt):
            setattr(self, name, stack.enter_context(
                nc.sbuf_tensor(name, list(shape), dt)))
        sb("cA", (128, 16384), BF16)
        sb("cB", (128, 16384), BF16)
        sb("cC", (128, 8192), BF16)
        sb("cD", (128, 8192), BF16)
        sb("xk", (128, 4096), F32)
        sb("yb", (128, 4096), BF16)
        sb("stC_sb", (128, 8192), BF16)
        sb("am_sb", (128, 1536), BF16)
        sb("bm_sb", (128, 4096), BF16)
        sb("nKT", (128, 1024), BF16)
        sb("stA_sb", (128, 128), BF16)
        sb("eyeb_sb", (128, 128), BF16)
        sb("eyef_sb", (128, 128), F32)
        sb("ones_f", (128, 128), F32)
        sb("scal", (128, 160), F32)
        sb("lnf", (128, 4096), F32)
        sb("pb", (128, 4096), BF16)
        self.psb = {}
        for nm in ("p0", "p1", "p2", "p3", "p4", "p5"):
            self.psb[nm] = stack.enter_context(
                nc.psum_tensor(nm, [128, 512], F32))
        self.psb["ta"] = stack.enter_context(
            nc.psum_tensor("ta", [128, 1024], BF16))
        self.psb["tb"] = stack.enter_context(
            nc.psum_tensor("tb", [128, 1024], BF16))
        self.sem = {}
        for n in ("sq", "aq", "gq", "pe", "dv", "ac",
                  "cc1", "cc2", "cc3", "cc4", "cc5"):
            self.sem[n] = stack.enter_context(nc.semaphore(n))

    S1, S2, SH1, SH2 = 0, 32, 64, 96
    S3, CONVB, BOUT, MV = 128, 136, 144, 152

    def m0(self):
        return self.scal[:, self.MV:self.MV + 1]

    def m1(self):
        return self.scal[:, self.MV + 1:self.MV + 2]

    def kconst(self, idx, p0=0, n=128):
        return self.scal[p0:p0 + n, self.MV + idx:self.MV + idx + 1]

    def ph_init(self):
        for dst, src in [(self.stC_sb, self.stC_in), (self.am_sb, self.am_in),
                         (self.bm_sb, self.bm_in), (self.stA_sb, self.stA_in),
                         (self.eyeb_sb, self.eyeb_in), (self.eyef_sb, self.eyef_in),
                         (self.ones_f, self.ones_in)]:
            self.dma("sq", dst[:, :], src[:, :])
        self.dma("sq", self.scal[:, 0:32], self.s1_in[:, :])
        self.dma("sq", self.scal[:, 32:64], self.s2_in[:, :])
        self.dma("sq", self.scal[:, 64:96], self.sh1_in[:, :])
        self.dma("sq", self.scal[:, 96:128], self.sh2_in[:, :])
        self.dma("sq", self.scal[:, 128:136], self.s3_in[:, :])
        self.dma("sq", self.scal[:, 136:144], self.convb_in[:, :])
        self.dma("sq", self.scal[:, 144:152], self.bout_in[:, :])
        self.dma("sq", self.scal[:, 152:160], self.mvec_in[:, :])
        self.consts_ld = self.c["sq"]

        self.dma("gq", self.wbf_qkv[:, :], self.wqkv[:, :])
        self.dma("gq", self.pb[:, 0:2048], self.convw_in[:, :])
        self.dma("gq", self.lnf[:, 0:8], self.nullk_in[:, :])
        self.dma("gq", self.lnf[:, 8:16], self.nullv_in[:, :])
        self.cast_a1 = self.c["gq"]
        self.dma("gq", self.wbf_out[:, :], self.wout[:, :])
        self.cast_wout = self.c["gq"]
        self.dma("gq", self.wbf_b[:, 0:1024], self.wqT[:, :])
        self.dma("gq", self.wbf_b[:, 1024:2048], self.wkT[:, :])
        self.dma("gq", self.wbf_b[:, 2048:3072], self.wvT[:, :])
        self.dma("gq", self.wbf_b[:, 3072:4096], self.woT[:, :])
        self.cast_b = self.c["gq"]
        self.dma("gq", self.w1bf[:, :], self.w1t[:, :])
        self.dma("gq", self.w2bf[:, :], self.w2t[:, :])
        for i in range(4):
            self.dma("gq", self.h1bf[1024 * i:1024 * (i + 1), :],
                     self.h1t[1024 * i:1024 * (i + 1), :])
        for i in range(4):
            self.dma("gq", self.h2bf[1024 * i:1024 * (i + 1), :],
                     self.h2t[1024 * i:1024 * (i + 1), :])
        self.dma("gq", self.w3bf[:, :], self.w3t[:, :])
        self.cast_ffn = self.c["gq"]

        self.wait("vector", "gq", self.cast_a1)
        self.wait("vector", "sq", self.consts_ld)
        self.op("vector", "dv", lambda E: E["vector"].memset(self.nKT[:, :], 0.0))
        for hp in range(8):
            self.op("vector", "dv", lambda E, hp=hp: E["vector"].tensor_copy(
                self.nKT[:, 128 * hp:128 * hp + 1], self.lnf[:, hp:hp + 1]))
        self.wait("tensor", "sq", self.consts_ld)
        self.wait("tensor", "gq", self.cast_a1)
        self.use_bank("p5")
        tnv = self.op("tensor", "pe", lambda E: E["tensor"].transpose(
            self.psb["p5"][0:8, 0:128], self.lnf[:, 8:16], self.eyef_sb[:, :]))
        self.wait("vector", "pe", tnv)
        self.op("vector", "dv", lambda E: E["vector"].tensor_copy(
            self.lnf[0:8, 16:144], self.psb["p5"][0:8, 0:128]))
        self.free_bank("p5", "dv")

    def build_nvt(self, dst, c0):
        self.op("vector", "dv", lambda E, dst=dst, c0=c0: E["vector"].memset(
            dst[:, c0:c0 + 1040], 0.0))
        for hp in range(8):
            b = c0 + 130 * hp
            self.op("vector", "dv", lambda E, dst=dst, b=b: E["vector"].memset(
                dst[:, b + 64:b + 65], 1.0))
            done = self.op("vector", "dv", lambda E, dst=dst, b=b: E["vector"].memset(
                dst[:, b + 129:b + 130], 1.0))
        self.wait("sync", "dv", done)

    def _nvt_rows(self, dst, c0):
        for hp in range(8):
            b = c0 + 130 * hp
            self.dma("gq", dst[0:1, b + 0:b + 64], self.lnf[hp:hp + 1, 16:80])
            self.dma("gq", dst[0:1, b + 65:b + 129], self.lnf[hp:hp + 1, 80:144])

    def ln_tg(self, src, tg):
        t0 = 512 * tg
        self.wait("sync", "dv")
        ld = 0
        for cc in range(8):
            ld = self.dma("sq", self.xk[:, 512 * cc:512 * (cc + 1)],
                          src[128 * cc:128 * (cc + 1), t0:t0 + 512])
        self.wait("tensor", "sq", ld)
        self.wait("scalar", "sq", ld)
        self.use_bank("p0"); self.use_bank("p1")
        for cc in range(8):
            xc = self.xk[:, 512 * cc:512 * (cc + 1)]
            self.opq("tensor", lambda E, xc=xc, cc=cc: E["tensor"].matmul(
                self.psb["p0"][0:1, :], self.ones_f[:, 0:1], xc,
                start=(cc == 0), stop=(cc == 7)))
            sq = self.lnf[:, 2048 + 512 * (cc % 2):2560 + 512 * (cc % 2)]
            if cc >= 2:
                self.wait("scalar", "pe")
            t = self.op("scalar", "ac", lambda E, xc=xc, sq=sq: E["scalar"].activation(
                out=sq, in_=xc, func=AF.Square))
            self.wait("tensor", "ac", t)
            self.op("tensor", "pe", lambda E, sq=sq, cc=cc: E["tensor"].matmul(
                self.psb["p1"][0:1, :], self.ones_f[:, 0:1], sq,
                start=(cc == 0), stop=(cc == 7)))
        stats_done = self.c["pe"]
        self.wait("vector", "pe", stats_done)
        mrow = self.lnf[0:1, 3072:3584]
        arow = self.lnf[0:1, 3584:4096]
        self.op("vector", "dv", lambda E, mrow=mrow: E["vector"].tensor_scalar(
            mrow, self.psb["p0"][0:1, :], 1.0 / 1024, None, ALU.mult))
        self.op("vector", "dv", lambda E, mrow=mrow, arow=arow: E["vector"].tensor_tensor(
            arow, mrow, mrow, ALU.mult))
        self.op("vector", "dv", lambda E, arow=arow: E["vector"].scalar_tensor_tensor(
            arow, self.psb["p1"][0:1, :], 1.0 / 1024, arow, ALU.mult, ALU.subtract))
        self.free_bank("p0", "dv"); self.free_bank("p1", "dv")
        self.wait("scalar", "dv")
        self.op("scalar", "ac", lambda E, arow=arow: E["scalar"].activation(
            out=arow, in_=arow, func=AF.Ln, bias=self.kconst(2, 0, 1)))
        self.op("scalar", "ac", lambda E, arow=arow: E["scalar"].activation(
            out=arow, in_=arow, func=AF.Exp, scale=self.kconst(5, 0, 1)))
        self.wait("vector", "ac")
        self.op("vector", "dv", lambda E, mrow=mrow, arow=arow: E["vector"].tensor_tensor(
            mrow, mrow, arow, ALU.mult))
        rows_done = self.c["dv"]
        self.wait("tensor", "dv", rows_done)
        self.use_bank("p2"); self.use_bank("p3")
        self.opq("tensor", lambda E, arow=arow: E["tensor"].matmul(
            self.psb["p2"][:, :], self.ones_f[0:1, :], arow, start=True, stop=True))
        self.op("tensor", "pe", lambda E, mrow=mrow: E["tensor"].matmul(
            self.psb["p3"][:, :], self.ones_f[0:1, :], mrow, start=True, stop=True))
        bc_done = self.c["pe"]
        self.wait("vector", "pe", bc_done)
        for cc in range(8):
            xc = self.xk[:, 512 * cc:512 * (cc + 1)]
            t1 = self.lnf[:, 2048 + 512 * (cc % 2):2560 + 512 * (cc % 2)]
            yc = self.yb[:, 512 * cc:512 * (cc + 1)]
            self.op("vector", "dv", lambda E, xc=xc, t1=t1: E["vector"].tensor_tensor(
                t1, xc, self.psb["p2"][:, :], ALU.mult))
            self.op("vector", "dv", lambda E, t1=t1, yc=yc: E["vector"].tensor_tensor(
                yc, t1, self.psb["p3"][:, :], ALU.subtract))
        self.free_bank("p2", "dv"); self.free_bank("p3", "dv")
        return self.c["dv"]

    def ph_a1(self):
        self.wait("sync", "gq", self.cast_a1)
        wslot = [self.cC[:, 0:1024], self.cC[:, 1024:2048]]
        wsrc = self.wbf_qkv.rearrange("(c p) o -> p c o", p=128)
        self.wait("vector", "gq", self.cast_a1)
        self.wait("vector", "sq", self.consts_ld)
        for sl in range(2):
            for ver in range(2):
                vs = 2048 + 520 * (2 * sl + ver)
                mv = self.m0() if ver == 0 else self.m1()
                self.op("vector", "dv", lambda E, vs=vs, mv=mv: E["vector"].tensor_scalar(
                    self.pb[:, vs + 64:vs + 65], self.ones_f[:, 0:1], mv, None, ALU.mult))
                self.op("vector", "dv", lambda E, vs=vs, mv=mv: E["vector"].tensor_scalar(
                    self.pb[:, vs + 129:vs + 130], self.ones_f[:, 0:1], mv, None, ALU.mult))
        for tg in range(NQG):
            y_done = self.ln_tg(self.x0f, tg)
            self.wait("tensor", "dv", y_done)
            pe_iter = {}
            self.wait("sync", "pe")
            ld = {0: self.dma("sq", wslot[0], wsrc[:, :, 0:128])}
            for i in range(24):
                if i + 1 < 24:
                    if i >= 1:
                        self.wait("sync", "pe", pe_iter.get(i - 1, 0))
                    ld[i + 1] = self.dma("sq", wslot[(i + 1) % 2],
                                         wsrc[:, :, 128 * (i + 1):128 * (i + 2)])
                self.wait("tensor", "sq", ld[i])
                bank = "p0" if i % 2 == 0 else "p1"
                self.use_bank(bank)
                ps = self.psb[bank]
                w = wslot[i % 2]
                for cc in range(8):
                    ff = (cc == 0); lst = (cc == 7)
                    if lst:
                        pe_iter[i] = self.op("tensor", "pe", lambda E, w=w, cc=cc, ps=ps: E["tensor"].matmul(
                            ps[:, :], w[:, 128 * cc:128 * cc + 128],
                            self.yb[:, 512 * cc:512 * cc + 512], start=False, stop=True))
                    else:
                        self.opq("tensor", lambda E, w=w, cc=cc, ps=ps, ff=ff: E["tensor"].matmul(
                            ps[:, :], w[:, 128 * cc:128 * cc + 128],
                            self.yb[:, 512 * cc:512 * cc + 512], start=ff, stop=False))
                self.wait("vector", "pe", pe_iter[i])
                if i < 8:
                    stg = self.cD[:, 512 * (i % 2):512 * (i % 2) + 512]
                    self.wait("vector", "gq")
                    ev = self.op("vector", "dv", lambda E, ps=ps, stg=stg: E["vector"].tensor_copy(stg, ps[:, :]))
                    self.free_bank(bank, "dv")
                    self.wait("gpsimd", "dv", ev)
                    self.dma("gq", self.qscr[128 * i:128 * i + 128, 512 * tg:512 * tg + 512], stg)
                elif i < 16:
                    kc = i - 8
                    self.op("vector", "dv", lambda E, ps=ps, kc=kc: E["vector"].tensor_copy(
                        self.cB[:, 512 * kc:512 * kc + 512], ps[:, :]))
                    self.free_bank(bank, "dv")
                else:
                    vc = i - 16
                    self.op("vector", "dv", lambda E, ps=ps, vc=vc: E["vector"].tensor_copy(
                        self.cA[:, 512 * vc:512 * vc + 512], ps[:, :]))
                    self.free_bank(bank, "dv")
            kv_done = self.c["dv"]
            self.wait("tensor", "dv", kv_done)
            for src_arena, is_v in ((self.cB, False), (self.cA, True)):
                for hp in range(8):
                    for par in range(2):
                        bank = "p2" if par == 0 else "p3"
                        self.use_bank(bank)
                        ps = self.psb[bank]
                        p0 = 64 * par
                        for tau in range(4):
                            wcol = 64 * (4 * hp + tau)
                            lhs = self.pb[p0:p0 + 64, wcol:wcol + 64]
                            rhs = src_arena[p0:p0 + 64, 512 * hp + tau:512 * hp + 512:4]
                            ff = (tau == 0); lst = (tau == 3)
                            if lst:
                                self.op("tensor", "pe", lambda E, lhs=lhs, rhs=rhs, ps=ps, p0=p0: E["tensor"].matmul(
                                    ps[0:64, 0:128], lhs, rhs, start=False, stop=True,
                                    tile_position=(p0, 0)))
                            else:
                                self.opq("tensor", lambda E, lhs=lhs, rhs=rhs, ps=ps, ff=ff, p0=p0: E["tensor"].matmul(
                                    ps[0:64, 0:128], lhs, rhs, start=ff, stop=False,
                                    tile_position=(p0, 0)))
                        mm = self.c["pe"]
                        self.wait("vector", "pe", mm)
                        bias = self.scal[p0:p0 + 64, self.CONVB + hp:self.CONVB + hp + 1]
                        if not is_v:
                            self.wait("vector", "gq")
                            for ver, mv, colb in ((0, self.m0(), 1024), (1, self.m1(), 3072)):
                                dst = self.cD[p0:p0 + 64, colb + 128 * hp:colb + 128 * hp + 128]
                                mvs = mv[p0:p0 + 64, :] if p0 else mv[0:64, :]
                                self.op("vector", "dv", lambda E, ps=ps, dst=dst, bias=bias, mvs=mvs: E["vector"].tensor_scalar(
                                    dst, ps[0:64, 0:128], bias, mvs, ALU.add, ALU.mult))
                        else:
                            dst = self.cD[p0:p0 + 64, 2048 + 128 * hp:2048 + 128 * hp + 128]
                            self.op("vector", "dv", lambda E, ps=ps, dst=dst, bias=bias: E["vector"].tensor_scalar(
                                dst, ps[0:64, 0:128], bias, None, ALU.add))
                        self.free_bank(bank, "dv")
            conv_done = self.c["dv"]
            self.wait("gpsimd", "dv", conv_done)
            for hp in range(8):
                self.dma("gq", self.ag1k_i[128 * hp:128 * hp + 128, 128 * tg:128 * tg + 128],
                         self.cD[:, 1024 + 128 * hp:1024 + 128 * hp + 128])
                self.dma("gq", self.ag1k_i[1024 + 128 * hp:1024 + 128 * hp + 128, 128 * tg:128 * tg + 128],
                         self.cD[:, 3072 + 128 * hp:3072 + 128 * hp + 128])
            for hp in range(8):
                tb = "ta" if hp % 2 == 0 else "tb"
                self.use_bank(tb)
                pst = self.psb[tb]
                self.wait("tensor", "dv", conv_done)
                td = self.op("tensor", "pe", lambda E, hp=hp, pst=pst: E["tensor"].transpose(
                    pst[:, 0:128], self.cD[:, 2048 + 128 * hp:2048 + 128 * hp + 128],
                    self.eyeb_sb[:, :]))
                self.wait("vector", "pe", td)
                sl = hp % 2
                self.wait("vector", "gq")
                dvs = 0
                for ver, mv in ((0, self.m0()), (1, self.m1())):
                    vs = 2048 + 520 * (2 * sl + ver)
                    self.op("vector", "dv", lambda E, pst=pst, vs=vs, mv=mv: E["vector"].tensor_scalar(
                        self.pb[:, vs + 0:vs + 64], pst[:, 0:64], mv, None, ALU.mult))
                    dvs = self.op("vector", "dv", lambda E, pst=pst, vs=vs, mv=mv: E["vector"].tensor_scalar(
                        self.pb[:, vs + 65:vs + 129], pst[:, 64:128], mv, None, ALU.mult))
                self.free_bank(tb, "dv")
                self.wait("gpsimd", "dv", dvs)
                for ver in range(2):
                    vs = 2048 + 520 * (2 * sl + ver)
                    rbase = 8192 * ver + 1024 * hp + 128 * tg
                    self.dma("gq", self.ag1v_i[rbase:rbase + 128, :],
                             self.pb[:, vs:vs + 130])
        self.wait("gpsimd", "gq")
        self.collective("cc1", self.ag1k_i, self.ag1k_o)
        self.collective("cc2", self.ag1v_i, self.ag1v_o)

    def attn_qg(self, qg, kind):
        j = qg
        scale = self.kconst(3) if kind == "a" else self.kconst(4)
        nch_half = (j + 1) if kind == "a" else 4 * (j + 1)
        ext = 128 * nch_half
        for hp in range(8):
            kE = self.cC[:, 0:ext]
            kO = self.cC[:, 4096:4096 + ext]
            self.wait("sync", "pe")
            ksrc = self.ag1k_o if kind == "a" else self.ag3k_o
            self.dma("sq", kE, ksrc[128 * hp:128 * hp + 128, 0:ext])
            l2 = self.dma("sq", kO, ksrc[1024 + 128 * hp:1024 + 128 * hp + 128, 0:ext])
            vsrc = self.ag1v_o if kind == "a" else self.ag3v_o
            vre = vsrc.rearrange("(x p) c -> p x c", p=128)
            blk = 8 if kind == "a" else 32
            base_e = blk * hp
            base_o = (64 if kind == "a" else 256) + blk * hp
            vE_c0, vO_c0 = 0, nch_half * 130
            self.wait("scalar", "pe")
            self.dma("aq", self.cA[:, vE_c0:vE_c0 + nch_half * 130],
                     vre[:, base_e:base_e + nch_half, :])
            l4 = self.dma("aq", self.cA[:, vO_c0:vO_c0 + nch_half * 130],
                          vre[:, base_o:base_o + nch_half, :])
            self.wait("tensor", "sq", l2)
            self.wait("tensor", "aq", l4)
            chunks = []
            if kind == "a":
                chunks.append(("null", 0, None, 0))
            for half in range(2):
                kb = kE if half == 0 else kO
                vc0 = vE_c0 if half == 0 else vO_c0
                for ci in range(nch_half):
                    if kind == "a":
                        mask = (1 + half) if ci == nch_half - 1 else None
                    else:
                        mask = (half * 4 + (ci - (nch_half - 4))) if ci >= nch_half - 4 else None
                    chunks.append((kb, 128 * ci, vc0 + 130 * ci, mask))
            qcolb = 12288 + 512 * hp
            self.use_bank("p4"); self.use_bank("p5")
            oh, ohp = self.psb["p4"], self.psb["p5"]
            nch = len(chunks)
            for ci, (kb, kcol, vcol, mask) in enumerate(chunks):
                sb_name = ("p0", "p1") if ci % 2 == 0 else ("p2", "p3")
                self.use_bank(sb_name[0]); self.use_bank(sb_name[1])
                s_h, s_hp = self.psb[sb_name[0]], self.psb[sb_name[1]]
                if kb == "null":
                    klhs0 = self.nKT[0:64, 128 * hp:128 * hp + 128]
                    klhs1 = self.nKT[64:128, 128 * hp:128 * hp + 128]
                else:
                    klhs0 = kb[0:64, kcol:kcol + 128]
                    klhs1 = kb[64:128, kcol:kcol + 128]
                self.opq("tensor", lambda E, klhs0=klhs0, s_h=s_h, qcolb=qcolb: E["tensor"].matmul(
                    s_h[:, :], klhs0, self.cB[0:64, qcolb:qcolb + 512],
                    start=True, stop=True, tile_position=(0, 0)))
                mm = self.op("tensor", "pe", lambda E, klhs1=klhs1, s_hp=s_hp, qcolb=qcolb: E["tensor"].matmul(
                    s_hp[:, :], klhs1, self.cB[64:128, qcolb:qcolb + 512],
                    start=True, stop=True, tile_position=(64, 0)))
                pcol = 1024 * (ci % 2)
                self.wait("scalar", "pe", mm)
                msk = self.am_sb if kind == "a" else self.bm_sb
                if kb == "null":
                    nb = 16384 - 1040 + 130 * hp
                    vl0 = self.cA[:, nb:nb + 65]
                    vl1 = self.cA[:, nb + 65:nb + 130]
                else:
                    vl0 = self.cA[:, vcol:vcol + 65]
                    vl1 = self.cA[:, vcol + 65:vcol + 130]
                first = ci == 0
                last = ci == nch - 1
                for which, sps, vl, ob in ((0, s_h, vl0, oh), (1, s_hp, vl1, ohp)):
                    pt = self.pb[:, pcol + 512 * which:pcol + 512 * which + 512]
                    if mask is not None:
                        sf = self.lnf[:, 512 * which:512 * which + 512]
                        self.wait("vector", "pe", mm)
                        self.wait("vector", "ac")
                        dv = self.op("vector", "dv", lambda E, sps=sps, sf=sf, mask=mask, msk=msk: E["vector"].tensor_tensor(
                            sf, sps[:, :], msk[:, 512 * mask:512 * mask + 512], ALU.add))
                        self.wait("scalar", "dv", dv)
                        expw = self.op("scalar", "ac", lambda E, sf=sf, pt=pt, scale=scale: E["scalar"].activation(
                            out=pt, in_=sf, func=AF.Exp, scale=scale))
                    else:
                        expw = self.op("scalar", "ac", lambda E, sps=sps, pt=pt, scale=scale: E["scalar"].activation(
                            out=pt, in_=sps[:, :], func=AF.Exp, scale=scale))
                    self.wait("tensor", "ac", expw)
                    if which == 0:
                        self.opq("tensor", lambda E, vl=vl, ob=ob, pt=pt, first=first, last=last: E["tensor"].matmul(
                            ob[0:65, :], vl, pt, start=first, stop=last))
                    else:
                        self.op("tensor", "pe", lambda E, vl=vl, ob=ob, pt=pt, first=first, last=last: E["tensor"].matmul(
                            ob[0:65, :], vl, pt, start=first, stop=last))
                self.free_bank(sb_name[0], "ac"); self.free_bank(sb_name[1], "ac")
            av_done = self.c["pe"]
            self.wait("vector", "pe", av_done)
            rlh = self.lnf[64:65, 2048:2560]
            rlhp = self.lnf[64:65, 2560:3072]
            self.op("vector", "dv", lambda E, oh=oh, rlh=rlh: E["vector"].reciprocal(rlh, oh[64:65, :]))
            rl_done = self.op("vector", "dv", lambda E, ohp=ohp, rlhp=rlhp: E["vector"].reciprocal(rlhp, ohp[64:65, :]))
            self.wait("tensor", "dv", rl_done)
            self.use_bank("p0")
            rb = self.psb["p0"]
            self.opq("tensor", lambda E, rb=rb, rlh=rlh: E["tensor"].matmul(
                rb[0:64, :], self.ones_f[64:65, 0:64], rlh,
                start=True, stop=True, tile_position=(64, 0)))
            bc = self.op("tensor", "pe", lambda E, rb=rb, rlhp=rlhp: E["tensor"].matmul(
                rb[64:128, :], self.ones_f[64:65, 0:64], rlhp,
                start=True, stop=True, tile_position=(64, 64)))
            self.wait("scalar", "pe", av_done)
            ob_h = self.lnf[0:64, 1024:1536]
            ob_t = self.lnf[0:64, 1536:2048]
            self.wait("scalar", "dv")
            self.wait("scalar", "sq")
            self.op("scalar", "ac", lambda E, oh=oh, ob_h=ob_h: E["scalar"].copy(ob_h, oh[0:64, :]))
            a2 = self.op("scalar", "ac", lambda E, ohp=ohp, ob_t=ob_t: E["scalar"].copy(ob_t, ohp[0:64, :]))
            self.free_bank("p4", "ac"); self.free_bank("p5", "ac")
            self.wait("sync", "ac", a2)
            mv = self.dma("sq", self.lnf[64:128, 1024:1536], ob_t)
            self.wait("vector", "sq", mv)
            self.wait("vector", "pe", bc)
            self.op("vector", "dv", lambda E, rb=rb, hp=hp: E["vector"].tensor_tensor(
                self.cB[:, 8192 + 512 * hp:8192 + 512 * hp + 512],
                self.lnf[:, 1024:1536], rb[:, :], ALU.mult))
            self.free_bank("p0", "dv")
        if self.debug_dump == "AOB_" + kind.upper():
            self.wait("gpsimd", "dv")
            self.dma("gq", self.dbg[128 * qg:128 * qg + 128, 0:4096],
                     self.cB[:, 8192:12288])
        return self.c["dv"]

    def ph_a2(self):
        self.wait("sync", "cc1", self.c["cc1"])
        self.wait("scalar", "cc2", self.c["cc2"])
        self.wait("sync", "gq", self.cast_wout)
        self.build_nvt(self.cA, 16384 - 1040)
        self.wait("gpsimd", "dv")
        self._nvt_rows(self.cA, 16384 - 1040)
        self.wait("tensor", "gq")
        self.wait("vector", "gq")
        wslot = [self.cC[:, 6144:7168], self.cC[:, 7168:8192]]
        wo_src = self.wbf_out.rearrange("(c p) o -> p c o", p=128)
        cd_src = self.cdsd.rearrange("(c p) o -> p c o", p=128)
        for qg in range(NQG):
            for hp in range(8):
                self.wait("sync", "pe")
                lq = self.dma("sq", self.cB[:, 12288 + 512 * hp:12288 + 512 * hp + 512],
                              self.qscr[128 * hp:128 * hp + 128, 512 * qg:512 * qg + 512])
            self.wait("tensor", "sq", lq)
            aob_done = self.attn_qg(qg, "a")
            self.wait("tensor", "dv", aob_done)
            self.wait("sync", "pe")
            ld = {0: self.dma("sq", wslot[0], wo_src[:, :, 0:128])}
            pe_ob = {}
            for i in range(8):
                if i + 1 < 8:
                    if i >= 1:
                        self.wait("sync", "pe", pe_ob.get(i - 1, 0))
                    ld[i + 1] = self.dma("sq", wslot[(i + 1) % 2],
                                         wo_src[:, :, 128 * (i + 1):128 * (i + 2)])
                self.wait("tensor", "sq", ld[i])
                bank = "p0" if i % 2 == 0 else "p1"
                self.use_bank(bank)
                ps = self.psb[bank]
                w = wslot[i % 2]
                for cc in range(8):
                    ff = (cc == 0); lst = (cc == 7)
                    if lst:
                        pe_ob[i] = self.op("tensor", "pe", lambda E, w=w, cc=cc, ps=ps: E["tensor"].matmul(
                            ps[:, :], w[:, 128 * cc:128 * cc + 128],
                            self.cB[:, 8192 + 512 * cc:8192 + 512 * cc + 512],
                            start=False, stop=True))
                    else:
                        self.opq("tensor", lambda E, w=w, cc=cc, ps=ps, ff=ff: E["tensor"].matmul(
                            ps[:, :], w[:, 128 * cc:128 * cc + 128],
                            self.cB[:, 8192 + 512 * cc:8192 + 512 * cc + 512],
                            start=ff, stop=False))
                self.wait("vector", "pe", pe_ob[i])
                bias = self.scal[:, self.BOUT + i:self.BOUT + i + 1]
                self.op("vector", "dv", lambda E, ps=ps, i=i, bias=bias: E["vector"].tensor_scalar(
                    self.yb[:, 512 * i:512 * i + 512], ps[:, :], bias, None, ALU.add))
                self.free_bank(bank, "dv")
            o_done = self.c["dv"]
            self.wait("tensor", "dv", o_done)
            for plane in range(2):
                self.wait("sync", "pe")
                ld2 = {0: self.dma("sq", wslot[0], cd_src[:, :, 1024 * plane:1024 * plane + 128])}
                pe_kb = {}
                for i in range(8):
                    if i + 1 < 8:
                        if i >= 1:
                            self.wait("sync", "pe", pe_kb.get(i - 1, 0))
                        ld2[i + 1] = self.dma("sq", wslot[(i + 1) % 2],
                                              cd_src[:, :, 1024 * plane + 128 * (i + 1):1024 * plane + 128 * (i + 2)])
                    self.wait("tensor", "sq", ld2[i])
                    bank = "p0" if i % 2 == 0 else "p1"
                    self.use_bank(bank)
                    ps = self.psb[bank]
                    w = wslot[i % 2]
                    for cc in range(8):
                        ff = (cc == 0); lst = (cc == 7)
                        if lst:
                            pe_kb[i] = self.op("tensor", "pe", lambda E, w=w, cc=cc, ps=ps: E["tensor"].matmul(
                                ps[:, :], w[:, 128 * cc:128 * cc + 128],
                                self.yb[:, 512 * cc:512 * cc + 512], start=False, stop=True))
                        else:
                            self.opq("tensor", lambda E, w=w, cc=cc, ps=ps, ff=ff: E["tensor"].matmul(
                                ps[:, :], w[:, 128 * cc:128 * cc + 128],
                                self.yb[:, 512 * cc:512 * cc + 512], start=ff, stop=False))
                    self.wait("vector", "pe", pe_kb[i])
                    zf = self.pb[:, 512 * (i % 2):512 * (i % 2) + 512]
                    self.wait("vector", "pe")
                    ev = self.op("vector", "dv", lambda E, ps=ps, zf=zf: E["vector"].tensor_copy(zf, ps[:, :]))
                    self.free_bank(bank, "dv")
                    for ts in range(4):
                        tbk = "ta" if (i * 4 + ts) % 2 == 0 else "tb"
                        self.use_bank(tbk)
                        pst = self.psb[tbk]
                        self.wait("tensor", "dv", ev)
                        td = self.op("tensor", "pe", lambda E, zf=zf, ts=ts, pst=pst: E["tensor"].transpose(
                            pst[:, 0:128], zf[:, 128 * ts:128 * ts + 128], self.eyeb_sb[:, :]))
                        self.wait("vector", "pe", td)
                        self.wait("vector", "gq")
                        for ver, mv in ((0, self.m0()), (1, self.m1())):
                            dst = self.cB[:, 4096 * ver + 1024 * ts + 128 * i:
                                          4096 * ver + 1024 * ts + 128 * i + 128]
                            self.op("vector", "dv", lambda E, pst=pst, dst=dst, mv=mv: E["vector"].tensor_scalar(
                                dst, pst[:, 0:128], mv, None, ALU.mult))
                        self.free_bank(tbk, "dv")
                z_done = self.c["dv"]
                self.wait("gpsimd", "dv", z_done)
                for ver in range(2):
                    for ts in range(4):
                        rbase = 8192 * ver + 4096 * plane + 512 * qg + 128 * ts
                        self.dma("gq", self.ag2_i[rbase:rbase + 128, :],
                                 self.cB[:, 4096 * ver + 1024 * ts:4096 * ver + 1024 * ts + 1024])
        self.wait("gpsimd", "gq")
        self.collective("cc3", self.ag2_i, self.ag2_o)

    def ph_fft(self):
        self.wait("sync", "cc3", self.c["cc3"])
        ag2 = self.ag2_o.rearrange("(r z l t) k -> r z l t k", r=2, z=2, l=32, t=128)
        y1 = self.y1scr.rearrange("(z k t) d -> z k t d", z=2, k=64, t=128)
        abuf = [self.cA[:, 0:4096], self.cA[:, 4096:8192]]
        stg = [self.cA[:, 8192:12288], self.cA[:, 12288:16384]]
        ld_it, pe_it = {}, {}
        def loadA(i):
            t2g, dkc = divmod(i, 2)
            slot = abuf[i % 2]
            for piece in range(4):
                r, z = piece % 2, piece // 2
                src = ag2[r, z, :, 8 * t2g:8 * t2g + 8, 512 * dkc:512 * dkc + 512]
                self.dma("sq", slot[32 * piece:32 * piece + 32, :], src)
            return self.c["sq"]
        ld_it[0] = loadA(0)
        NIT = 32
        for i in range(NIT):
            t2g, dkc = divmod(i, 2)
            if i + 1 < NIT:
                if i >= 1:
                    self.wait("sync", "pe", pe_it.get(i - 1, 0))
                ld_it[i + 1] = loadA(i + 1)
            self.wait("tensor", "sq", ld_it[i])
            slot = abuf[i % 2]
            sbank = ("p0", "p1") if i % 2 == 0 else ("p2", "p3")
            stage = stg[i % 2]
            self.wait("vector", "gq")
            for t2i in range(8):
                self.use_bank(sbank[0]); self.use_bank(sbank[1])
                pr, pi = self.psb[sbank[0]], self.psb[sbank[1]]
                rhs = slot[:, 512 * t2i:512 * t2i + 512]
                self.opq("tensor", lambda E, rhs=rhs, pr=pr: E["tensor"].matmul(
                    pr[0:64, :], self.stA_sb[:, 0:64], rhs, start=True, stop=True))
                mm = self.op("tensor", "pe", lambda E, rhs=rhs, pi=pi: E["tensor"].matmul(
                    pi[0:64, :], self.stA_sb[:, 64:128], rhs, start=True, stop=True))
                self.wait("vector", "pe", mm)
                self.op("vector", "dv", lambda E, pr=pr, stage=stage, t2i=t2i: E["vector"].tensor_copy(
                    stage[0:64, 512 * t2i:512 * t2i + 512], pr[0:64, :]))
                self.op("vector", "dv", lambda E, pi=pi, stage=stage, t2i=t2i: E["vector"].tensor_copy(
                    stage[64:128, 512 * t2i:512 * t2i + 512], pi[0:64, :]))
                self.free_bank(sbank[0], "dv"); self.free_bank(sbank[1], "dv")
            pe_it[i] = self.c["pe"]
            self.wait("gpsimd", "dv")
            for plane in range(2):
                dst = y1[plane, :, 8 * t2g:8 * t2g + 8, 512 * dkc:512 * dkc + 512]
                self.dma("gq", dst, stage[64 * plane:64 * plane + 64, :])
        self.wait("sync", "gq")
        self.wait("vector", "gq")
        yt = [self.cC[:, 0:2048], self.cC[:, 2048:4096]]
        ldc = {}
        def loadC(k1):
            slot = yt[k1 % 2]
            self.dma("sq", slot[:, 0:1024], y1[0, k1, :, :])
            self.dma("sq", slot[:, 1024:2048], y1[1, k1, :, :])
            return self.c["sq"]
        ldc[0] = loadC(0)
        pe_k1 = {}
        for k1 in range(64):
            if k1 + 1 < 64:
                if k1 >= 1:
                    self.wait("sync", "pe", pe_k1.get(k1 - 1, 0))
                ldc[k1 + 1] = loadC(k1 + 1)
            self.wait("tensor", "sq", ldc[k1])
            slot = yt[k1 % 2]
            for dc in range(8):
                bank = "p0" if dc % 2 == 0 else "p1"
                self.use_bank(bank)
                ps = self.psb[bank]
                lc = self.stC_sb[:, 128 * k1:128 * k1 + 64]
                ls = self.stC_sb[:, 128 * k1 + 64:128 * k1 + 128]
                self.opq("tensor", lambda E, lc=lc, ps=ps, slot=slot, dc=dc: E["tensor"].matmul(
                    ps[0:64, 0:128], lc, slot[:, 128 * dc:128 * dc + 128],
                    start=True, stop=False))
                mm = self.op("tensor", "pe", lambda E, ls=ls, ps=ps, slot=slot, dc=dc: E["tensor"].matmul(
                    ps[0:64, 0:128], ls, slot[:, 1024 + 128 * dc:1024 + 128 * dc + 128],
                    start=False, stop=True))
                ev_slot = self.lnf[0:64, 512 * (dc % 2):512 * (dc % 2) + 128]
                self.wait("vector", "pe", mm)
                ev = self.op("vector", "dv", lambda E, ps=ps, ev_slot=ev_slot: E["vector"].tensor_copy(
                    ev_slot, ps[0:64, 0:128]))
                self.free_bank(bank, "dv")
                tbk = "p2" if dc % 2 == 0 else "p3"
                self.use_bank(tbk)
                pst = self.psb[tbk]
                self.wait("tensor", "dv", ev)
                td = self.op("tensor", "pe", lambda E, ev_slot=ev_slot, pst=pst: E["tensor"].transpose(
                    pst[:, 0:64], ev_slot, self.eyef_sb[0:64, 0:64]))
                self.wait("vector", "pe", td)
                arena = self.cA if dc < 4 else self.cB
                rt3 = arena[:, 4096 * (dc % 4):4096 * (dc % 4) + 4096].rearrange(
                    "p (l b k) -> p l b k", l=32, b=2, k=64)
                dst = rt3[:, :, :, k1]
                self.op("vector", "dv", lambda E, pst=pst, dst=dst: E["vector"].tensor_copy(dst, pst[:, 0:64]))
                self.free_bank(tbk, "dv")
            pe_k1[k1] = self.c["pe"]
        rt_done = self.c["dv"]
        self.wait("vector", "dv", rt_done)
        for dc in range(8):
            lx = self.dma("sq", self.xk[:, :], self.x0f[128 * dc:128 * dc + 128, :])
            self.wait("vector", "sq", lx)
            arena = self.cA if dc < 4 else self.cB
            rt = arena[:, 4096 * (dc % 4):4096 * (dc % 4) + 4096]
            self.wait("vector", "gq")
            wb = self.op("vector", "dv", lambda E, rt=rt: E["vector"].tensor_tensor(
                self.lnf[:, :], self.xk[:, :], rt, ALU.add))
            self.wait("gpsimd", "dv", wb)
            self.dma("gq", self.xres[128 * dc:128 * dc + 128, :], self.lnf[:, :])
            self.wait("sync", "dv", wb)

    def ph_ffn(self, dst_dram):
        self.wait("sync", "gq", self.cast_ffn)
        w1s = self.w1bf.rearrange("(c p) o -> p c o", p=128)
        w2s = self.w2bf.rearrange("(c p) o -> p c o", p=128)
        h1s = self.h1bf.rearrange("(c p) o -> p c o", p=128)
        h2s = self.h2bf.rearrange("(c p) o -> p c o", p=128)
        w3s = self.w3bf.rearrange("(c p) o -> p c o", p=128)
        wA = [self.cC[:, 0:4096], self.cC[:, 4096:8192]]
        wB = [self.cD[:, 0:4096], self.cD[:, 4096:8192]]
        for tg in range(NQG):
            y_done = self.ln_tg(self.xres, tg)
            self.wait("tensor", "dv", y_done)
            pe_u = {}
            self.wait("sync", "pe")
            self.wait("scalar", "pe")
            ldu = {0: (self.dma("sq", wA[0][:, 0:1024], w1s[:, :, 0:128]),
                       self.dma("aq", wB[0][:, 0:1024], w2s[:, :, 0:128]))}
            for i in range(32):
                if i + 1 < 32:
                    if i >= 1:
                        self.wait("sync", "pe", pe_u.get(i - 1, 0))
                        self.wait("scalar", "pe", pe_u.get(i - 1, 0))
                    a = self.dma("sq", wA[(i + 1) % 2][:, 0:1024],
                                 w1s[:, :, 128 * (i + 1):128 * (i + 2)])
                    b = self.dma("aq", wB[(i + 1) % 2][:, 0:1024],
                                 w2s[:, :, 128 * (i + 1):128 * (i + 2)])
                    ldu[i + 1] = (a, b)
                self.wait("tensor", "sq", ldu[i][0])
                self.wait("tensor", "aq", ldu[i][1])
                b1 = "p0" if i % 2 == 0 else "p2"
                b2 = "p1" if i % 2 == 0 else "p3"
                self.use_bank(b1); self.use_bank(b2)
                u1, u2 = self.psb[b1], self.psb[b2]
                for cc in range(8):
                    self.opq("tensor", lambda E, i=i, cc=cc, u1=u1: E["tensor"].matmul(
                        u1[:, :], wA[i % 2][:, 128 * cc:128 * cc + 128],
                        self.yb[:, 512 * cc:512 * cc + 512],
                        start=(cc == 0), stop=(cc == 7)))
                for cc in range(8):
                    ff = (cc == 0); lst = (cc == 7)
                    if lst:
                        pe_u[i] = self.op("tensor", "pe", lambda E, i=i, cc=cc, u2=u2: E["tensor"].matmul(
                            u2[:, :], wB[i % 2][:, 128 * cc:128 * cc + 128],
                            self.yb[:, 512 * cc:512 * cc + 512], start=False, stop=True))
                    else:
                        self.opq("tensor", lambda E, i=i, cc=cc, u2=u2, ff=ff: E["tensor"].matmul(
                            u2[:, :], wB[i % 2][:, 128 * cc:128 * cc + 128],
                            self.yb[:, 512 * cc:512 * cc + 512], start=ff, stop=False))
                self.wait("scalar", "pe", pe_u[i])
                self.wait("vector", "pe", pe_u[i])
                sa = self.pb[:, 512 * (i % 2):512 * (i % 2) + 512]
                sbf = self.pb[:, 1024 + 512 * (i % 2):1536 + 512 * (i % 2)]
                s1c = self.scal[:, self.S1 + i:self.S1 + i + 1]
                s2c = self.scal[:, self.S2 + i:self.S2 + i + 1]
                self.wait("scalar", "dv")
                ac1 = self.op("scalar", "ac", lambda E, u1=u1, sa=sa, s1c=s1c: E["scalar"].activation(
                    out=sa, in_=u1[:, :], func=AF.Silu, scale=s1c))
                self.free_bank(b1, "ac")
                self.op("vector", "dv", lambda E, u2=u2, sbf=sbf, s2c=s2c: E["vector"].tensor_scalar(
                    sbf, u2[:, :], s2c, None, ALU.mult))
                self.free_bank(b2, "dv")
                self.wait("vector", "ac", ac1)
                self.op("vector", "dv", lambda E, sa=sa, sbf=sbf, i=i: E["vector"].tensor_tensor(
                    self.cA[:, 512 * i:512 * i + 512], sa, sbf, ALU.mult))
            g_done = self.c["dv"]
            self.wait("tensor", "dv", g_done)
            pe_h = {}
            self.wait("sync", "pe")
            ldh = {0: self.dma("sq", wA[0], h1s[:, :, 0:128])}
            for i in range(32):
                if i + 1 < 32:
                    if i >= 1:
                        self.wait("sync", "pe", pe_h.get(i - 1, 0))
                    ldh[i + 1] = self.dma("sq", wA[(i + 1) % 2],
                                          h1s[:, :, 128 * (i + 1):128 * (i + 2)])
                self.wait("tensor", "sq", ldh[i])
                bank = "p0" if i % 2 == 0 else "p1"
                self.use_bank(bank)
                ps = self.psb[bank]
                for cc in range(32):
                    ff = (cc == 0); lst = (cc == 31)
                    if lst:
                        pe_h[i] = self.op("tensor", "pe", lambda E, i=i, cc=cc, ps=ps: E["tensor"].matmul(
                            ps[:, :], wA[i % 2][:, 128 * cc:128 * cc + 128],
                            self.cA[:, 512 * cc:512 * cc + 512], start=False, stop=True))
                    else:
                        self.opq("tensor", lambda E, i=i, cc=cc, ps=ps, ff=ff: E["tensor"].matmul(
                            ps[:, :], wA[i % 2][:, 128 * cc:128 * cc + 128],
                            self.cA[:, 512 * cc:512 * cc + 512], start=ff, stop=False))
                self.wait("scalar", "pe", pe_h[i])
                shc = self.scal[:, self.SH1 + i:self.SH1 + i + 1]
                self.op("scalar", "ac", lambda E, ps=ps, i=i, shc=shc: E["scalar"].activation(
                    out=self.cB[:, 512 * i:512 * i + 512], in_=ps[:, :],
                    func=AF.Gelu, scale=shc))
                self.free_bank(bank, "ac")
            g2_done = self.c["ac"]
            self.wait("tensor", "ac", g2_done)
            pe_h2 = {}
            self.wait("sync", "pe")
            ldh2 = {0: self.dma("sq", wA[0], h2s[:, :, 0:128])}
            for i in range(32):
                if i + 1 < 32:
                    if i >= 1:
                        self.wait("sync", "pe", pe_h2.get(i - 1, 0))
                    ldh2[i + 1] = self.dma("sq", wA[(i + 1) % 2],
                                           h2s[:, :, 128 * (i + 1):128 * (i + 2)])
                self.wait("tensor", "sq", ldh2[i])
                bank = "p0" if i % 2 == 0 else "p1"
                self.use_bank(bank)
                ps = self.psb[bank]
                for cc in range(32):
                    ff = (cc == 0); lst = (cc == 31)
                    if lst:
                        pe_h2[i] = self.op("tensor", "pe", lambda E, i=i, cc=cc, ps=ps: E["tensor"].matmul(
                            ps[:, :], wA[i % 2][:, 128 * cc:128 * cc + 128],
                            self.cB[:, 512 * cc:512 * cc + 512], start=False, stop=True))
                    else:
                        self.opq("tensor", lambda E, i=i, cc=cc, ps=ps, ff=ff: E["tensor"].matmul(
                            ps[:, :], wA[i % 2][:, 128 * cc:128 * cc + 128],
                            self.cB[:, 512 * cc:512 * cc + 512], start=ff, stop=False))
                self.wait("scalar", "pe", pe_h2[i])
                shc = self.scal[:, self.SH2 + i:self.SH2 + i + 1]
                self.op("scalar", "ac", lambda E, ps=ps, i=i, shc=shc: E["scalar"].activation(
                    out=self.cA[:, 512 * i:512 * i + 512], in_=ps[:, :],
                    func=AF.Gelu, scale=shc))
                self.free_bank(bank, "ac")
            g3_done = self.c["ac"]
            self.wait("tensor", "ac", g3_done)
            pe_w3 = {}
            self.wait("sync", "pe")
            ldw3 = {0: self.dma("sq", wA[0], w3s[:, :, 0:128])}
            for i in range(8):
                if i + 1 < 8:
                    if i >= 1:
                        self.wait("sync", "pe", pe_w3.get(i - 1, 0))
                    ldw3[i + 1] = self.dma("sq", wA[(i + 1) % 2],
                                           w3s[:, :, 128 * (i + 1):128 * (i + 2)])
                self.wait("tensor", "sq", ldw3[i])
                bank = "p2" if i % 2 == 0 else "p3"
                self.use_bank(bank)
                ps = self.psb[bank]
                for cc in range(32):
                    ff = (cc == 0); lst = (cc == 31)
                    if lst:
                        pe_w3[i] = self.op("tensor", "pe", lambda E, i=i, cc=cc, ps=ps: E["tensor"].matmul(
                            ps[:, :], wA[i % 2][:, 128 * cc:128 * cc + 128],
                            self.cA[:, 512 * cc:512 * cc + 512], start=False, stop=True))
                    else:
                        self.opq("tensor", lambda E, i=i, cc=cc, ps=ps, ff=ff: E["tensor"].matmul(
                            ps[:, :], wA[i % 2][:, 128 * cc:128 * cc + 128],
                            self.cA[:, 512 * cc:512 * cc + 512], start=ff, stop=False))
                self.wait("vector", "pe", pe_w3[i])
                s3c = self.scal[:, self.S3 + i:self.S3 + i + 1]
                stgf = self.lnf[:, 512 * (i % 2):512 * (i % 2) + 512]
                self.wait("vector", "gq")
                wb = self.op("vector", "dv", lambda E, ps=ps, s3c=s3c, stgf=stgf, i=i: E["vector"].scalar_tensor_tensor(
                    stgf, ps[:, :], s3c, self.xk[:, 512 * i:512 * i + 512],
                    ALU.mult, ALU.add))
                self.free_bank(bank, "dv")
                self.wait("gpsimd", "dv", wb)
                self.dma("gq", dst_dram[128 * i:128 * i + 128, 512 * tg:512 * tg + 512], stgf)

    def ph_b_proj(self):
        self.wait("sync", "gq", self.cast_b)
        wsrc = self.wbf_b.rearrange("(c p) o -> p c o", p=128)
        wslot = [self.cC[:, 0:1024], self.cC[:, 1024:2048]]
        self.wait("vector", "sq", self.consts_ld)
        for sl in range(2):
            for ver in range(2):
                vs = 2048 + 2080 * sl + 520 * ver
                mv = self.m0() if ver == 0 else self.m1()
                for ts in range(4):
                    self.op("vector", "dv", lambda E, vs=vs, ts=ts, mv=mv: E["vector"].tensor_scalar(
                        self.cD[:, vs + 130 * ts + 64:vs + 130 * ts + 65],
                        self.ones_f[:, 0:1], mv, None, ALU.mult))
                    self.op("vector", "dv", lambda E, vs=vs, ts=ts, mv=mv: E["vector"].tensor_scalar(
                        self.cD[:, vs + 130 * ts + 129:vs + 130 * ts + 130],
                        self.ones_f[:, 0:1], mv, None, ALU.mult))
        for tg in range(NQG):
            y_done = self.ln_tg(self.xres, tg)
            self.wait("tensor", "dv", y_done)
            pe_i = {}
            self.wait("sync", "pe")
            ld = {0: self.dma("sq", wslot[0], wsrc[:, :, 0:128])}
            for i in range(24):
                if i + 1 < 24:
                    if i >= 1:
                        self.wait("sync", "pe", pe_i.get(i - 1, 0))
                    ld[i + 1] = self.dma("sq", wslot[(i + 1) % 2],
                                         wsrc[:, :, 128 * (i + 1):128 * (i + 2)])
                self.wait("tensor", "sq", ld[i])
                bank = "p0" if i % 2 == 0 else "p1"
                self.use_bank(bank)
                ps = self.psb[bank]
                w = wslot[i % 2]
                for cc in range(8):
                    ff = (cc == 0); lst = (cc == 7)
                    if lst:
                        pe_i[i] = self.op("tensor", "pe", lambda E, w=w, cc=cc, ps=ps: E["tensor"].matmul(
                            ps[:, :], w[:, 128 * cc:128 * cc + 128],
                            self.yb[:, 512 * cc:512 * cc + 512], start=False, stop=True))
                    else:
                        self.opq("tensor", lambda E, w=w, cc=cc, ps=ps, ff=ff: E["tensor"].matmul(
                            ps[:, :], w[:, 128 * cc:128 * cc + 128],
                            self.yb[:, 512 * cc:512 * cc + 512], start=ff, stop=False))
                self.wait("vector", "pe", pe_i[i])
                ob = i % 8
                if i < 8:
                    stg = self.pb[:, 512 * (i % 2):512 * (i % 2) + 512]
                    self.wait("vector", "gq")
                    ev = self.op("vector", "dv", lambda E, ps=ps, stg=stg: E["vector"].tensor_copy(stg, ps[:, :]))
                    self.free_bank(bank, "dv")
                    self.wait("gpsimd", "dv", ev)
                    self.dma("gq", self.qscr[128 * ob:128 * ob + 128, 512 * tg:512 * tg + 512], stg)
                elif i < 16:
                    self.wait("vector", "gq")
                    for ver, mv in ((0, self.m0()), (1, self.m1())):
                        stg = self.pb[:, 1024 + 1024 * ver + 512 * (i % 2):
                                      1024 + 1024 * ver + 512 * (i % 2) + 512]
                        dvs = self.op("vector", "dv", lambda E, ps=ps, stg=stg, mv=mv: E["vector"].tensor_scalar(
                            stg, ps[:, :], mv, None, ALU.mult))
                        self.wait("gpsimd", "dv", dvs)
                        self.dma("gq", self.ag3k_i[1024 * ver + 128 * ob:1024 * ver + 128 * ob + 128,
                                                   512 * tg:512 * tg + 512], stg)
                    self.free_bank(bank, "dv")
                else:
                    hp = ob
                    stg = self.pb[:, 3072 + 512 * (i % 2):3072 + 512 * (i % 2) + 512]
                    ev = self.op("vector", "dv", lambda E, ps=ps, stg=stg: E["vector"].tensor_copy(stg, ps[:, :]))
                    self.free_bank(bank, "dv")
                    sl = i % 2
                    self.wait("vector", "gq")
                    dvs = 0
                    for ts in range(4):
                        tbk = "ta" if ts % 2 == 0 else "tb"
                        self.use_bank(tbk)
                        pst = self.psb[tbk]
                        self.wait("tensor", "dv", ev)
                        td = self.op("tensor", "pe", lambda E, stg=stg, ts=ts, pst=pst: E["tensor"].transpose(
                            pst[:, 0:128], stg[:, 128 * ts:128 * ts + 128], self.eyeb_sb[:, :]))
                        self.wait("vector", "pe", td)
                        for ver, mv in ((0, self.m0()), (1, self.m1())):
                            vs = 2048 + 2080 * sl + 520 * ver
                            self.op("vector", "dv", lambda E, pst=pst, vs=vs, ts=ts, mv=mv: E["vector"].tensor_scalar(
                                self.cD[:, vs + 130 * ts + 0:vs + 130 * ts + 64],
                                pst[:, 0:64], mv, None, ALU.mult))
                            dvs = self.op("vector", "dv", lambda E, pst=pst, vs=vs, ts=ts, mv=mv: E["vector"].tensor_scalar(
                                self.cD[:, vs + 130 * ts + 65:vs + 130 * ts + 129],
                                pst[:, 64:128], mv, None, ALU.mult))
                        self.free_bank(tbk, "dv")
                    self.wait("gpsimd", "dv", dvs)
                    for ver in range(2):
                        vs = 2048 + 2080 * sl + 520 * ver
                        rbase = 32768 * ver + 4096 * hp + 512 * tg
                        dst = self.ag3v_i[rbase:rbase + 512, :].rearrange(
                            "(ts p) c -> p ts c", ts=4)
                        self.dma("gq", dst, self.cD[:, vs:vs + 520])
        self.wait("gpsimd", "gq")
        self.collective("cc4", self.ag3k_i, self.ag3k_o)
        self.collective("cc5", self.ag3v_i, self.ag3v_o)

    def ph_b_attn(self):
        self.wait("sync", "cc4", self.c["cc4"])
        self.wait("scalar", "cc5", self.c["cc5"])
        self.wait("sync", "gq", self.cast_b)
        wo_src = self.wbf_b.rearrange("(c p) o -> p c o", p=128)
        wslot = [self.cD[:, 0:1024], self.cD[:, 1024:2048]]
        for qg in range(NQG):
            for hp in range(8):
                self.wait("sync", "pe")
                lq = self.dma("sq", self.cB[:, 12288 + 512 * hp:12288 + 512 * hp + 512],
                              self.qscr[128 * hp:128 * hp + 128, 512 * qg:512 * qg + 512])
            self.wait("tensor", "sq", lq)
            aob_done = self.attn_qg(qg, "b")
            self.wait("tensor", "dv", aob_done)
            self.wait("sync", "pe")
            ld = {0: self.dma("sq", wslot[0], wo_src[:, :, 3072:3200])}
            pe_ob = {}
            for i in range(8):
                if i + 1 < 8:
                    if i >= 1:
                        self.wait("sync", "pe", pe_ob.get(i - 1, 0))
                    ld[i + 1] = self.dma("sq", wslot[(i + 1) % 2],
                                         wo_src[:, :, 3072 + 128 * (i + 1):3072 + 128 * (i + 2)])
                self.wait("tensor", "sq", ld[i])
                bank = "p1" if i % 2 == 0 else "p2"
                self.use_bank(bank)
                ps = self.psb[bank]
                w = wslot[i % 2]
                for cc in range(8):
                    ff = (cc == 0); lst = (cc == 7)
                    if lst:
                        pe_ob[i] = self.op("tensor", "pe", lambda E, w=w, cc=cc, ps=ps: E["tensor"].matmul(
                            ps[:, :], w[:, 128 * cc:128 * cc + 128],
                            self.cB[:, 8192 + 512 * cc:8192 + 512 * cc + 512],
                            start=False, stop=True))
                    else:
                        self.opq("tensor", lambda E, w=w, cc=cc, ps=ps, ff=ff: E["tensor"].matmul(
                            ps[:, :], w[:, 128 * cc:128 * cc + 128],
                            self.cB[:, 8192 + 512 * cc:8192 + 512 * cc + 512],
                            start=ff, stop=False))
                self.wait("scalar", "dv")
                lx = self.dma("aq", self.xk[:, 512 * i:512 * i + 512],
                              self.xres[128 * i:128 * i + 128, 512 * qg:512 * qg + 512])
                self.wait("vector", "pe", pe_ob[i])
                self.wait("vector", "aq", lx)
                stgf = self.lnf[:, 512 * (i % 2):512 * (i % 2) + 512]
                self.wait("vector", "gq")
                wb = self.op("vector", "dv", lambda E, ps=ps, stgf=stgf, i=i: E["vector"].tensor_tensor(
                    stgf, ps[:, :], self.xk[:, 512 * i:512 * i + 512], ALU.add))
                self.free_bank(bank, "dv")
                self.wait("gpsimd", "dv", wb)
                self.dma("gq", self.xres[128 * i:128 * i + 128, 512 * qg:512 * qg + 512], stgf)

    def ph_debug(self):
        name = self.debug_dump
        if not name or not hasattr(self, name):
            return
        src = getattr(self, name)
        rows, cols = src.tensor.shape
        rows = min(rows, 16384)
        cols = min(cols, 4096)
        self.barrier()
        for cc in ("cc1", "cc2", "cc3", "cc4", "cc5"):
            if self.c.get(cc, 0):
                self.wait("gpsimd", cc, self.c[cc])
        for base in range(0, rows, 128):
            self.dma("gq", self.dbg[base:base + 128, 0:cols], src[base:base + 128, 0:cols])
        self.barrier()

    def build(self):
        with contextlib.ExitStack() as stack:
            self._alloc(stack)
            pl = self.phase_limit
            self.src_ffn = self.xres
            self.ph_init()
            if pl in (104, 105):
                for dc in range(8):
                    self.dma("gq", self.xres[128 * dc:128 * dc + 128, :],
                             self.x0f[128 * dc:128 * dc + 128, :])
                self.barrier()
                if pl == 104:
                    self.ph_ffn(self.xres)
                else:
                    self.ph_b_proj()
                    self.ph_b_attn()
                self.ph_debug()
                self.barrier()
                self._emit_block()
                return self.nc
            if pl >= 1:
                self.ph_a1()
            if pl >= 2:
                self.ph_a2()
            if pl >= 3:
                self.ph_fft()
            if pl >= 4:
                self.barrier()
                self.ph_ffn(self.xres)
            if pl >= 5:
                self.barrier()
                self.ph_b_proj()
                self.ph_b_attn()
            if pl >= 6:
                self.barrier()
                self.ph_ffn(self.outT)
            self.ph_debug()
            self.barrier()
            self._emit_block()
        return self.nc

    def _emit_block(self):
        nc = self.nc
        E = {e: getattr(nc, e) for e in self.ENGS}
        with nc.Block() as block:
            @block.sync
            def _(sync):
                for f in self.prog["sync"]:
                    f(E)
            @block.gpsimd
            def _(gpsimd):
                for f in self.prog["gpsimd"]:
                    f(E)
            @block.scalar
            def _(scalar):
                for f in self.prog["scalar"]:
                    f(E)
            @block.vector
            def _(vector):
                for f in self.prog["vector"]:
                    f(E)
            @block.tensor
            def _(tensor):
                for f in self.prog["tensor"]:
                    f(E)



def _gidx(r):
    l = np.arange(ROWS) // 128
    off = np.arange(ROWS) % 128
    return 128 * (2 * l + r) + off


def _prep_shared(w):
    f = np.float32
    ct = np.ascontiguousarray
    d = {}
    d["wqkv"] = ct(np.asarray(w["L0_attn_a_Wqkv"], f))
    conv = np.asarray(w["L0_attn_a_conv_w"], f)
    cw = conv.reshape(16, 64, 64, 4)
    sbuf = np.zeros((128, 2048), f)
    for tau in range(4):
        for hp in range(8):
            blk = 64 * (4 * hp + tau)
            sbuf[0:64, blk:blk + 64] = cw[2 * hp, :, :, tau].T
            sbuf[64:128, blk:blk + 64] = cw[2 * hp + 1, :, :, tau].T
    d["convw_in"] = sbuf
    d["convb_in"] = ct(np.asarray(w["L0_attn_a_conv_b"], f).reshape(8, 128).T)
    d["nullk_in"] = ct(np.asarray(w["L0_attn_a_null_k"], f).reshape(8, 128).T)
    d["nullv_in"] = ct(np.asarray(w["L0_attn_a_null_v"], f).reshape(8, 128).T)
    d["wout"] = ct(np.asarray(w["L0_attn_a_W_out"], f))
    d["bout_in"] = ct(np.asarray(w["L0_attn_a_b_out"], f).reshape(8, 128).T)
    d["wqT"] = ct(np.asarray(w["L0_attn_b_Wq"], f).T)
    d["wkT"] = ct(np.asarray(w["L0_attn_b_Wk"], f).T)
    d["wvT"] = ct(np.asarray(w["L0_attn_b_Wv"], f).T)
    d["woT"] = ct(np.asarray(w["L0_attn_b_Wo"], f).T)
    d["w1t"] = ct(np.asarray(w["L0_ffn_w1_w"], f).T)
    d["w2t"] = ct(np.asarray(w["L0_ffn_w2_w"], f).T)
    d["h1t"] = ct(np.asarray(w["L0_ffn_h1_w"], f).T)
    d["h2t"] = ct(np.asarray(w["L0_ffn_h2_w"], f).T)
    d["w3t"] = ct(np.asarray(w["L0_ffn_w3_w"], f).T)
    d["s1_in"] = ct(np.asarray(w["L0_ffn_w1_s"], f).reshape(32, 128).T)
    d["s2_in"] = ct(np.asarray(w["L0_ffn_w2_s"], f).reshape(32, 128).T)
    d["sh1_in"] = ct(np.asarray(w["L0_ffn_h1_s"], f).reshape(32, 128).T)
    d["sh2_in"] = ct(np.asarray(w["L0_ffn_h2_s"], f).reshape(32, 128).T)
    d["s3_in"] = ct(np.asarray(w["L0_ffn_w3_s"], f).reshape(8, 128).T)
    n = np.arange(D_)
    CD = np.cos(2 * np.pi * np.outer(n, n) / D_)
    SD = np.sin(2 * np.pi * np.outer(n, n) / D_)
    d["cdsd"] = np.concatenate([CD, SD], 1).astype(BF)
    d["stA_in"] = _stepA_const()
    d["eyeb_in"] = np.eye(128, dtype=BF)
    d["eyef_in"] = np.eye(128, dtype=np.float32)
    d["ones_in"] = np.ones((128, 128), np.float32)
    return d


_CACHED = {}


def _get_nc():
    key = (PHASE_LIMIT, DEBUG_DUMP)
    if key not in _CACHED:
        _CACHED[key] = MK(phase_limit=PHASE_LIMIT, debug_dump=DEBUG_DUMP).build()
    return _CACHED[key]


def kernel(**inputs):
    shared = _prep_shared(inputs)
    x0 = np.asarray(inputs["L0_attn_a_x"], np.float32)
    percore = []
    for r in (0, 1):
        mvec = np.zeros((128, 8), np.float32)
        mvec[:, 0] = 1.0 if r == 0 else 0.0
        mvec[:, 1] = 0.0 if r == 0 else 1.0
        mvec[:, 2] = 1e-5
        mvec[:, 3] = 1.0 / 32.0
        mvec[:, 4] = 1.0 / 8.0
        mvec[:, 5] = -0.5
        percore.append({"stC_in": _stepC_const(r), "am_in": _maskA(r),
                        "bm_in": _maskB(r), "mvec_in": mvec})
    gidx = [_gidx(0), _gidx(1)]
    in_maps = []
    for c in range(8):
        b, r = c // 2, c % 2
        m = dict(shared)
        m.update(percore[r])
        m["x0f"] = np.ascontiguousarray(x0[b][gidx[r]].T)
        in_maps.append(m)
    nc = _get_nc()
    res = run_bass_kernel_spmd(nc, in_maps, core_ids=list(range(8)))
    out = np.zeros((B_, T_, D_), np.float32)
    for c in range(8):
        b, r = c // 2, c % 2
        out[b][gidx[r]] = res.results[c]["out"].T
    return out
